# version 04
# baseline (speedup 1.0000x reference)
import contextlib

import numpy as np

import concourse.bass as bass
import concourse.mybir as mybir
from concourse.bass_utils import run_bass_kernel_spmd

F32 = mybir.dt.float32
BF16 = mybir.dt.bfloat16
I32 = mybir.dt.int32
U8 = mybir.dt.uint8
AF = mybir.ActivationFunctionType
ALU = mybir.AluOpType
AX = mybir.AxisListType

S = 4096
D = 1024
H = 16
R = 8
QKD = 32
DR = 2048
DF = 3072
EPS = 1e-5
NT = S // 128
SC_Q = 0.125
SC_QV = float(1.0 / np.float32(np.sqrt(np.float32(QKD))))
SC_FFN = float(np.float32(1.0 / 127.0))

_MMC = [0]
D_SEQ = [False]
ENGS = ("sync", "tensor", "vector", "scalar", "gpsimd")


def _transp(t, out, in_, ident, start):
    return t.matmul(out, in_, ident, start=start, stop=True, is_transpose=True)


def _mm_noldw(t, out, lhsT, rhs, start, stop, tile_position=None):
    inst = mybir.InstMatmult(
        name=f"mmnl_{_MMC[0]}", engine=mybir.EngineType.PE,
        ins=(t.lower_ap(rhs), t.lower_ap(lhsT)), outs=(t.lower_ap(out),),
        replication_resolution=0, replication_shift_amnt=0, replication_num_rows=0,
        start_tensor_calc=start, stop_tensor_calc=stop, ldweights=False,
        tile_position=tile_position)
    _MMC[0] += 1
    return t.add_instruction(inst)


class Sem:
    def __init__(self, ctx, nc, name):
        self.h = ctx.enter_context(nc.semaphore(name))
        self.v = 0


class Prog:

    def __init__(self, nc):
        self.nc = nc
        self.q = {e: [] for e in ENGS}

    def wait(self, eng, sm, val=None):
        v = sm.v if val is None else val
        if v > 0:
            self.q[eng].append(lambda e, _h=sm.h, _v=v: e.wait_ge(_h, _v))

    def op(self, eng, fn, incs=()):
        assert len(incs) <= 1, "hardware: one semaphore update per instruction"
        cap = []
        for sm, n in incs:
            sm.v += n
            cap.append((sm.h, n))

        def run(e, _fn=fn, _cap=cap):
            inst = _fn(e)
            for h, n in _cap:
                inst.then_inc(h, n)
        self.q[eng].append(run)

    def run(self):
        with self.nc.Block() as blk:
            for name in ENGS:
                lst = self.q[name]
                if lst:
                    getattr(blk, name)(lambda e, _l=lst: [f(e) for f in _l])


class Banks:

    def __init__(self, ctx, nc, banks, tag):
        self.banks = banks
        self.sems = [Sem(ctx, nc, f"{tag}{i}") for i in range(len(banks))]
        self.used = [0] * len(banks)
        self.g = 0

    def acquire(self, P, eng="tensor"):
        i = self.g % len(self.banks)
        self.g += 1
        P.wait(eng, self.sems[i], self.used[i])
        return self.banks[i], self.sems[i]

    def release(self, sm):
        i = self.sems.index(sm)
        self.used[i] += 1
        return self.used[i]

    def wait_all(self, P, eng):
        for i in range(len(self.sems)):
            P.wait(eng, self.sems[i], self.used[i])


class Arena:
    def __init__(self, ctx, nc, nbytes):
        self.t = ctx.enter_context(nc.sbuf_tensor("arena", [128, nbytes], U8))
        self.nbytes = nbytes
        self.cur = 0
        self.peak = 0
        self.marks = []

    def alloc(self, cols, dtype, parts=128, align=64):
        isz = mybir.dt.size(dtype)
        off = (self.cur + align - 1) // align * align
        self.cur = off + cols * isz
        self.peak = max(self.peak, self.cur)
        assert self.cur <= self.nbytes, f"SBUF arena overflow {self.cur}"
        return self.t.ap().bitcast(dtype)[0:parts, off // isz: off // isz + cols]

    def mark(self):
        self.marks.append(self.cur)

    def release(self):
        self.cur = self.marks.pop()


def _ap(t, offset, dims):
    return bass.AP(t, offset, dims)


def build_program(taps=(), stop_after=None):
    tapset = set(taps)
    nc = bass.Bass("TRN2", target_bir_lowering=False)

    def dram(name, shape, dtype, kind=None):
        if kind is None:
            return nc.dram_tensor(name, shape, dtype)
        return nc.dram_tensor(name, shape, dtype, kind=kind)

    x_in = dram("x", [S, D], F32, "ExternalInput")
    wq = dram("Wq", [D, D], F32, "ExternalInput")
    bq = dram("bq", [D], F32, "ExternalInput")
    wk = dram("Wk", [D, D], F32, "ExternalInput")
    bk = dram("bk", [D], F32, "ExternalInput")
    wv = dram("Wv", [512, D], F32, "ExternalInput")
    bv = dram("bv", [512], F32, "ExternalInput")
    wqv = dram("Wqv", [512, D], F32, "ExternalInput")
    bqv = dram("bqv", [512], F32, "ExternalInput")
    ws = dram("Ws", [1, 96], F32, "ExternalInput")
    bs_in = dram("bs", [1], F32, "ExternalInput")
    wf = dram("Wf", [D, D], F32, "ExternalInput")
    bf_in = dram("bf", [D], F32, "ExternalInput")
    wr1 = dram("Wr1", [DR, D], F32, "ExternalInput")
    br1 = dram("br1", [DR], F32, "ExternalInput")
    wr2 = dram("Wr2", [D, DR], F32, "ExternalInput")
    br2 = dram("br2", [D], F32, "ExternalInput")
    g1 = dram("g1", [D], F32, "ExternalInput")
    b1 = dram("b1", [D], F32, "ExternalInput")
    g2 = dram("g2", [D], F32, "ExternalInput")
    b2 = dram("b2", [D], F32, "ExternalInput")
    wg = dram("wg", [DF, D], I32, "ExternalInput")
    wu = dram("wu", [DF, D], I32, "ExternalInput")
    wd = dram("wd", [D, DF], I32, "ExternalInput")
    out = dram("out", [S, D], F32, "ExternalOutput")

    def scr(name, shape, dtype, tag):
        return dram(name, shape, dtype, "ExternalOutput" if tag in tapset else None)

    y_sp = scr("y_sp", [S, D], F32, "A")
    qt_sp = scr("qt_sp", [8, 128, S], BF16, "B")
    mix_sp = scr("mix_sp", [S, D], BF16, "C")
    x1_sp = scr("x1_sp", [S, D], F32, "D")
    zt_sp = scr("zt_sp", [8, 128, S], BF16, "D")
    wq_bf = dram("wq_bf", [D, D], BF16)
    wk_bf = dram("wk_bf", [D, D], BF16)
    wv_bf = dram("wv_bf", [512, D], BF16)
    wqv_bf = dram("wqv_bf", [512, D], BF16)
    wf_bf = dram("wf_bf", [D, D], BF16)
    wr1_bf = dram("wr1_bf", [DR, D], BF16)
    wr2_bf = dram("wr2_bf", [D, DR], BF16)
    wg_bf = dram("wg_bf", [DF, D], BF16)
    wu_bf = dram("wu_bf", [DF, D], BF16)
    wd_bf = dram("wd_bf", [D, DF], BF16)
    kt_tap = scr("kt_tap", [128, 8 * S], BF16, "B") if "B" in tapset else None
    v_tap = scr("v_tap", [128, NT * 512], BF16, "B") if "B" in tapset else None
    cqv_tap = scr("cqv_tap", [128, NT * H], F32, "B") if "B" in tapset else None
    wqt_tap = scr("wqt_tap", [128, 8 * D], BF16, "B") if "B" in tapset else None
    ht_tap = scr("ht_tap", [128, 24 * 512], BF16, "E") if "E" in tapset else None
    zb_tap = scr("zb_tap", [128, 8 * 512], BF16, "E") if "E" in tapset else None
    wdt_tap = scr("wdt_tap", [128, 24 * D], BF16, "E") if "E" in tapset else None
    fb_tap = scr("fb_tap", [128, D], F32, "E") if "E" in tapset else None
    xnt_tap = scr("xnt_tap", [128, 8 * 512], BF16, "B") if "B" in tapset else None

    ctx = contextlib.ExitStack()
    with ctx:
        A = Arena(ctx, nc, 206208)

        id_bf = A.alloc(128, BF16)
        id_f32 = A.alloc(128, F32)
        ones_bf = A.alloc(8, BF16)
        wsdr = A.alloc(512, BF16)
        wsqr = A.alloc(512, BF16)
        bss = A.alloc(1, F32)
        bqs = A.alloc(8, F32)
        bks = A.alloc(8, F32)
        br1s = A.alloc(16, F32)
        eps_t = A.alloc(1, F32)
        A.mark()
        kt_sb = A.alloc(8 * S, BF16)
        v_sb = A.alloc(NT * 512, BF16)
        cqv_sb = A.alloc(NT * H, F32)
        A.mark()

        TAIL_OFF = A.nbytes - 32 * 1024
        assert TAIL_OFF % 64 == 0

        def tail_ap(off_bytes, cols, dtype):
            isz = mybir.dt.size(dtype)
            return A.t.ap().bitcast(dtype)[0:128, off_bytes // isz: off_bytes // isz + cols]

        wr1t_sb = tail_ap(TAIL_OFF, 8 * DR, BF16)

        pcl = ctx.enter_context(nc.psum_tensor("pcl", [128, 1024], F32))
        pclb = pcl[:].bitcast(BF16)
        pb = [ctx.enter_context(nc.psum_tensor(f"pb{i}", [128, 512], F32)) for i in range(6)]

        def sems(*names):
            return [Sem(ctx, nc, n) for n in names]

        A.mark()
        iota_j = A.alloc(128, F32)
        iota_p = A.alloc(128, F32)
        btmp = A.alloc(128, F32)
        P = Prog(nc)
        d0, g0, t0, v0 = sems("p0d", "p0g", "p0t", "p0v")
        P.op("gpsimd", lambda e: e.dma_start(out=wsdr[:, :], in_=_ap(ws, 32, [[0, 128], [0, 8], [1, 64]])), [(d0, 16)])
        P.op("gpsimd", lambda e: e.dma_start(out=wsqr[:, :], in_=_ap(ws, 0, [[0, 128], [0, 16], [1, 32]])), [(d0, 16)])
        P.op("sync", lambda e: e.dma_start(out=bss[:, :], in_=_ap(bs_in, 0, [[0, 128], [1, 1]])), [(d0, 16)])
        P.op("sync", lambda e: e.dma_start(out=btmp[0:8, :], in_=_ap(bq, 0, [[128, 8], [1, 128]])), [(d0, 16)])
        P.op("sync", lambda e: e.dma_start(out=btmp[8:16, :], in_=_ap(bk, 0, [[128, 8], [1, 128]])), [(d0, 16)])
        P.op("sync", lambda e: e.dma_start(out=btmp[16:32, :], in_=_ap(br1, 0, [[128, 16], [1, 128]])), [(d0, 16)])
        P.op("gpsimd", lambda e: e.iota(iota_j[:, :], [[1, 128]], channel_multiplier=0,
                                        allow_small_or_imprecise_dtypes=True))
        P.op("gpsimd", lambda e: e.iota(iota_p[:, :], [[0, 128]], channel_multiplier=1,
                                        allow_small_or_imprecise_dtypes=True), [(g0, 1)])
        P.op("gpsimd", lambda e: e.memset(ones_bf[:, :], 1.0), [(g0, 1)])
        P.op("gpsimd", lambda e: e.memset(eps_t[:, :], EPS), [(g0, 1)])
        P.wait("vector", g0)
        P.op("vector", lambda e: e.tensor_tensor(out=id_f32[:, :], in0=iota_j[:, :], in1=iota_p[:, :], op=ALU.is_equal))
        P.op("vector", lambda e: e.tensor_tensor(out=id_bf[:, :], in0=iota_j[:, :], in1=iota_p[:, :], op=ALU.is_equal), [(v0, 1)])
        P.wait("tensor", d0)
        P.wait("tensor", v0)
        P.op("tensor", lambda e: e.transpose(pcl[:, 0:128], btmp[:, :], id_f32[:, :]), [(t0, 1)])
        P.wait("vector", t0)
        P.op("vector", lambda e: e.tensor_scalar(out=bqs[:, :], in0=pcl[:, 0:8], scalar1=SC_Q, scalar2=None, op0=ALU.mult))
        P.op("vector", lambda e: e.tensor_copy(out=bks[:, :], in_=pcl[:, 8:16]))
        P.op("vector", lambda e: e.tensor_copy(out=br1s[:, :], in_=pcl[:, 16:32]))
        P.run()
        A.release()

        A.mark()
        wqt_sb = A.alloc(8 * D, BF16)
        wkt_sb = A.alloc(8 * D, BF16)
        wvt_sb = A.alloc(8 * 512, BF16)
        wqvt_sb = A.alloc(8 * 512, BF16)
        g1r = A.alloc(D, F32)
        b1r = A.alloc(D, F32)
        bvr = A.alloc(512, F32)
        bqvr_s = A.alloc(512, F32)
        xt_b = [A.alloc(D, F32) for _ in range(2)]
        xn_v = [b.bitcast(BF16) for b in xt_b]
        ln1_b = A.alloc(D, F32)
        yb_ab = [A.alloc(D, F32) for _ in range(2)]
        st_b = A.alloc(64, F32)
        xnt_b = A.alloc(8 * 512, BF16)
        qk_st = [A.alloc(512, BF16) for _ in range(2)]
        qv_b = A.alloc(512, BF16)
        cq_b = A.alloc(512, F32)
        cqt_b = A.alloc(16, F32)

        P = Prog(nc)
        (s_dx, s_dw, s_dy, s_dq, s_dwp, s_vv, s_sv, s_vs, s_vt, s_ts, s_st,
         s_xc, s_xcs, s_wpb) = sems(
            "abdx", "abdw", "abdy", "abdq", "abdwp", "abvv", "absv", "abvs",
            "abvt", "abts", "abst", "abxc", "abxcs", "abwpb")
        pclB = Banks(ctx, nc, [0, 1], "abpcl")
        mmB = Banks(ctx, nc, pb[0:4], "abmm")

        def ln_math(base):
            b = base
            P.op("vector", lambda e: e.tensor_scalar(out=st_b[:, b + 4:b + 6], in0=st_b[:, b + 0:b + 2],
                                                     scalar1=1.0 / D, scalar2=None, op0=ALU.mult), [(s_vv, 1)])
            P.op("vector", lambda e: e.tensor_scalar(out=st_b[:, b + 6:b + 8], in0=st_b[:, b + 2:b + 4],
                                                     scalar1=1.0 / D, scalar2=None, op0=ALU.mult), [(s_vv, 1)])
            P.wait("vector", s_vv)
            P.op("vector", lambda e: e.tensor_tensor(out=st_b[:, b + 8:b + 10], in0=st_b[:, b + 4:b + 6],
                                                     in1=st_b[:, b + 4:b + 6], op=ALU.mult), [(s_vv, 1)])
            P.wait("vector", s_vv)
            P.op("vector", lambda e: e.tensor_tensor(out=st_b[:, b + 10:b + 12], in0=st_b[:, b + 6:b + 8],
                                                     in1=st_b[:, b + 8:b + 10], op=ALU.subtract), [(s_vs, 1)])
            P.wait("scalar", s_vs)
            P.op("scalar", lambda e: e.activation(out=st_b[:, b + 12:b + 14], in_=st_b[:, b + 10:b + 12],
                                                  func=AF.Sqrt, bias=eps_t[:, :]), [(s_sv, 1)])
            P.wait("vector", s_sv)
            P.op("vector", lambda e: e.reciprocal(out=st_b[:, b + 14:b + 16], in_=st_b[:, b + 12:b + 14]), [(s_vv, 1)])
            P.wait("vector", s_vv)

        for dst, src in ((g1r, g1), (b1r, b1), (bvr, bv)):
            P.op("sync", lambda e, _d=dst, _s=src: e.dma_start(
                out=_d[:, :], in_=_s.ap().unsqueeze(0).partition_broadcast(128)), [(s_dw, 16)])
        P.op("sync", lambda e: e.dma_start(
            out=ln1_b[:, 0:512], in_=bqv.ap().unsqueeze(0).partition_broadcast(128)), [(s_dw, 16)])
        P.wait("vector", s_dw)
        P.op("vector", lambda e: e.tensor_scalar(out=bqvr_s[:, :], in0=ln1_b[:, 0:512],
                                                 scalar1=SC_QV, scalar2=None, op0=ALU.mult), [(s_vv, 1)])

        c_wc = Sem(ctx, nc, "abwc")
        for wsrc, wdst_bf in ((wq, wq_bf), (wk, wk_bf), (wv, wv_bf), (wqv, wqv_bf),
                              (wf, wf_bf), (wr1, wr1_bf), (wr2, wr2_bf),
                              (wg, wg_bf), (wu, wu_bf), (wd, wd_bf)):
            P.op("gpsimd", lambda e, _s=wsrc, _d=wdst_bf: e.dma_start(
                out=_d.ap()[:, :], in_=_s.ap()[:, :]), [(c_wc, 16)])
        s_dwT = Sem(ctx, nc, "abdwT")
        for wi, (wbf, wdst, rows, cols_t) in enumerate(
                ((wq_bf, wqt_sb, D, D), (wk_bf, wkt_sb, D, D),
                 (wv_bf, wvt_sb, 512, 512), (wqv_bf, wqvt_sb, 512, 512))):
            P.wait("sync", c_wc, (wi + 1) * 16)
            for s in range(8):
                P.op("sync", lambda e, _w=wbf, _d=wdst, _s=s, _r=rows, _ct=cols_t:
                     e.dma_start_transpose(_d[:, _s * _ct:(_s + 1) * _ct],
                                           _w.ap()[:, _s * 128:(_s + 1) * 128]), [(s_dwT, 16)])

        def emit_xload(gi):
            sl = gi % 2
            if gi >= 32:
                return
            if gi >= 2:
                P.wait("scalar", s_vt, x_marks[gi - 2][0])
                P.wait("scalar", s_ts, x_marks[gi - 2][2])
            P.op("scalar", lambda e, _t=gi, _sl=sl: e.dma_start(
                out=xt_b[_sl][:, :], in_=_ap(x_in, _t * 128 * D, [[D, 128], [1, D]])),
                [(s_dx, 16)])

        x_marks = {}
        emit_xload(0)
        emit_xload(1)
        for n in range(8):
            for grp in range(4):
                gi = n * 4 + grp
                sl = gi % 2
                t0g = 4 * n + grp
                P.wait("scalar", s_dx, 16 * (gi + 1))
                P.wait("scalar", s_vv)
                P.wait("scalar", s_vt)
                P.op("scalar", lambda e, _sl=sl: e.activation(
                    out=ln1_b[:, :], in_=xt_b[_sl][:, :],
                    func=AF.Square, accum_out=st_b[:, 2:3]), [(s_sv, 1)])
                xsq_mark = s_sv.v
                P.wait("vector", s_dx, 16 * (gi + 1))
                P.op("vector", lambda e, _sl=sl: e.reduce_sum(
                    out=st_b[:, 0:1], in_=xt_b[_sl][:, :], axis=AX.X), [(s_vv, 1)])
                P.wait("vector", s_sv)
                P.wait("vector", s_vv)
                ln_math(0)
                P.wait("vector", s_dy, 16 * max(0, gi - 1))
                P.op("vector", lambda e, _sl=sl: e.tensor_scalar(
                    out=yb_ab[_sl][:, :], in0=xt_b[_sl][:, :],
                    scalar1=st_b[:, 4:5], scalar2=st_b[:, 14:15],
                    op0=ALU.subtract, op1=ALU.mult), [(s_vv, 1)])
                yw_mark = s_vv.v
                P.wait("scalar", s_vv)
                P.wait("scalar", s_ts)
                P.op("scalar", lambda e, _sl=sl: e.activation(
                    out=xt_b[_sl][:, :], in_=yb_ab[_sl][:, :],
                    func=AF.Square, accum_out=st_b[:, 18:19]), [(s_sv, 1)])
                P.op("scalar", lambda e, _t=t0g, _sl=sl: e.dma_start(
                    out=_ap(y_sp, _t * 128 * D, [[D, 128], [1, D]]), in_=yb_ab[_sl][:, :]),
                    [(s_dy, 16)])
                P.op("vector", lambda e, _sl=sl: e.reduce_sum(
                    out=st_b[:, 16:17], in_=yb_ab[_sl][:, :], axis=AX.X), [(s_vv, 1)])
                P.wait("vector", s_sv)
                P.wait("vector", s_vv)
                ln_math(16)
                P.wait("vector", s_ts)
                P.op("vector", lambda e, _sl=sl: e.tensor_scalar(
                    out=xt_b[_sl][:, :], in0=yb_ab[_sl][:, :],
                    scalar1=st_b[:, 20:21], scalar2=st_b[:, 30:31],
                    op0=ALU.subtract, op1=ALU.mult), [(s_vv, 1)])
                P.wait("vector", s_vv)
                P.op("vector", lambda e, _sl=sl: e.tensor_tensor(
                    out=ln1_b[:, :], in0=xt_b[_sl][:, :],
                    in1=g1r[:, :], op=ALU.mult), [(s_vv, 1)])
                P.wait("vector", s_vv)
                P.op("vector", lambda e, _sl=sl: e.tensor_tensor(
                    out=xn_v[_sl][:, 0:D], in0=ln1_b[:, :],
                    in1=b1r[:, :], op=ALU.add), [(s_vt, 1)])
                P.wait("tensor", s_vt)
                _, hsm = pclB.acquire(P, "tensor")
                half = (pclB.g - 1) % 2
                for k in range(8):
                    P.op("tensor", lambda e, _k=k, _h=half, _sl=sl: _transp(e,
                        pclb[:, _h * 1024 + _k * 128: _h * 1024 + (_k + 1) * 128],
                        xn_v[_sl][:, _k * 128:(_k + 1) * 128], id_bf[:, :], _k == 0), [(s_ts, 1)])
                P.wait("vector", s_ts)
                P.op("vector", lambda e, _tc=grp, _h=half: e.tensor_copy(
                    out=xnt_b[:, :].rearrange("p (s c) -> p s c", c=512)[:, :, _tc * 128:(_tc + 1) * 128],
                    in_=pclb[:, _h * 1024:(_h + 1) * 1024].rearrange("p (s c) -> p s c", c=128)),
                    [(hsm, 1)])
                pclB.release(hsm)
                x_marks[gi] = (s_vt.v, xsq_mark, s_ts.v)
                emit_xload(gi + 2)
            pclB.wait_all(P, "tensor")
            for is_k, wt_sb in ((False, wqt_sb), (True, wkt_sb)):
                if n == 0:
                    P.wait("tensor", s_dwT, (16 if is_k else 8) * 16)
                for dt in range(8):
                    bank, bsm = mmB.acquire(P, "tensor")
                    for s in range(8):
                        P.op("tensor", lambda e, _w=wt_sb, _s=s, _dt=dt, _b=bank: e.matmul(
                            _b[:, :], _w[:, _s * D + _dt * 128: _s * D + (_dt + 1) * 128],
                            xnt_b[:, _s * 512:(_s + 1) * 512], start=(_s == 0), stop=(_s == 7)),
                            [(s_ts, 1)] if s == 7 else ())
                    P.wait("scalar", s_ts)
                    if is_k:
                        P.op("scalar", lambda e, _dt=dt, _n=n, _b=bank: e.activation(
                            out=kt_sb[:, _dt * S + _n * 512: _dt * S + _n * 512 + 512],
                            in_=_b[:, :], func=AF.Identity, bias=bks[:, _dt:_dt + 1]), [(bsm, 1)])
                    else:
                        qj = n * 8 + dt
                        qsl = qj % 2
                        P.wait("scalar", s_dq, max(0, (qj - 1) * 16))
                        P.op("scalar", lambda e, _dt=dt, _b=bank, _sl=qsl: e.activation(
                            out=qk_st[_sl][:, :], in_=_b[:, :], func=AF.Identity,
                            scale=SC_Q, bias=bqs[:, _dt:_dt + 1]), [(bsm, 1)])
                        P.op("scalar", lambda e, _dt=dt, _n=n, _sl=qsl: e.dma_start(
                            out=_ap(qt_sp, _dt * 128 * S + _n * 512, [[S, 128], [1, 512]]),
                            in_=qk_st[_sl][:, :]), [(s_dq, 16)])
                    mmB.release(bsm)
            for is_qv, wt_sb in ((False, wvt_sb), (True, wqvt_sb)):
                if n == 0:
                    P.wait("tensor", s_dwT, (32 if is_qv else 24) * 16)
                for tc in range(4):
                    t_abs = 4 * n + tc
                    bank, bsm = mmB.acquire(P, "tensor")
                    for s in range(8):
                        P.op("tensor", lambda e, _w=wt_sb, _s=s, _tc=tc, _b=bank: e.matmul(
                            _b[:, :], xnt_b[:, _s * 512 + _tc * 128: _s * 512 + (_tc + 1) * 128],
                            _w[:, _s * 512:(_s + 1) * 512], start=(_s == 0), stop=(_s == 7)),
                            [(s_ts, 1)] if s == 7 else ())
                    P.wait("vector", s_ts)
                    if not is_qv:
                        P.op("vector", lambda e, _t=t_abs, _b=bank: e.tensor_tensor(
                            out=v_sb[:, _t * 512:(_t + 1) * 512], in0=_b[:, :], in1=bvr[:, :],
                            op=ALU.add), [(bsm, 1)])
                        mmB.release(bsm)
                    else:
                        P.op("vector", lambda e, _b=bank: e.scalar_tensor_tensor(
                            out=qv_b[:, :], in0=_b[:, :], scalar=SC_QV, in1=bqvr_s[:, :],
                            op0=ALU.mult, op1=ALU.add), [(bsm, 1)])
                        bcnt = mmB.release(bsm)
                        P.wait("vector", bsm, bcnt)
                        P.op("vector", lambda e: e.tensor_tensor(
                            out=cq_b[:, :], in0=qv_b[:, :], in1=wsqr[:, :], op=ALU.mult), [(s_vv, 1)])
                        P.wait("vector", s_vv)
                        P.op("vector", lambda e: e.reduce_sum(
                            out=cqt_b[:, :], in_=cq_b[:, :].rearrange("p (h q) -> p h q", q=32),
                            axis=AX.X), [(s_vv, 1)])
                        P.wait("vector", s_vv)
                        P.op("vector", lambda e, _t=t_abs: e.tensor_scalar(
                            out=cqv_sb[:, _t * H:(_t + 1) * H], in0=cqt_b[:, :],
                            scalar1=bss[:, 0:1], scalar2=None, op0=ALU.add), [(s_vv, 1)])

        P.wait("gpsimd", s_dy)
        P.wait("gpsimd", s_dq)
        P.wait("gpsimd", c_wc)
        if kt_tap is not None:
            s_tp, = sems("abtap")
            P.op("gpsimd", lambda e: e.dma_start(out=kt_tap.ap()[:, :], in_=kt_sb[:, :]), [(s_tp, 16)])
            P.op("gpsimd", lambda e: e.dma_start(out=v_tap.ap()[:, :], in_=v_sb[:, :]), [(s_tp, 16)])
            P.op("gpsimd", lambda e: e.dma_start(out=cqv_tap.ap()[:, :], in_=cqv_sb[:, :]), [(s_tp, 16)])
            P.op("gpsimd", lambda e: e.dma_start(out=wqt_tap.ap()[:, :], in_=wqt_sb[:, :]), [(s_tp, 16)])
            P.op("gpsimd", lambda e: e.dma_start(out=xnt_tap.ap()[:, :], in_=xnt_b[:, :]), [(s_tp, 16)])
            P.wait("gpsimd", s_tp)
        P.run()
        A.release()
        if stop_after == "AB":
            return nc

        A.mark()
        qbuf = [A.alloc(8 * 256, BF16) for _ in range(2)]
        ptb = A.alloc(16 * 1024, BF16)
        stg = [A.alloc(1024, BF16) for _ in range(2)]
        prod = A.alloc(1024, F32)
        mixst = A.alloc(2 * 2048, BF16)
        csc = A.alloc(64, F32)
        macc = [A.alloc(64, F32) for _ in range(2)]
        assert A.cur <= TAIL_OFF, f"C frame {A.cur} overlaps tail {TAIL_OFF}"

        P = Prog(nc)
        (c_st, c_exp, c_pv1, c_pv2, c_oev1, c_oev2, c_q, c_mix, c_mixd,
         c_vv, c_vs, c_sv) = sems("cst", "cexp", "cpv1", "cpv2", "coev1",
                                  "coev2", "cq", "cmix", "cmixd", "cvv", "cvs", "csv")

        NB = 2048
        LAG1, LAG2 = 2, 8

        for ic in range(16):
            if ic >= 2:
                P.wait("sync", c_st, (ic - 1) * 128)
            P.op("sync", lambda e, _ic=ic: e.dma_start(
                out=qbuf[_ic % 2][:, :],
                in_=_ap(qt_sp, _ic * 256, [[S, 128], [128 * S, 8], [1, 256]])), [(c_q, 16)])
        c_pw = Sem(ctx, nc, "cpw")
        P.wait("sync", c_wc, 6 * 16)
        for s in range(8):
            P.op("sync", lambda e, _s=s: e.dma_start_transpose(
                wr1t_sb[:, _s * DR:(_s + 1) * DR], wr1_bf.ap()[:, _s * 128:(_s + 1) * 128]), [(c_pw, 16)])

        def pv_batch(kb, head, flush=False):
            if kb < 0 or kb >= NB:
                return
            ub, bb = divmod(kb, 16)
            icb, pairb = divmod(ub, 8)
            oev = c_oev1 if head == 0 else c_oev2
            pvs = c_pv1 if head == 0 else c_pv2
            obk = (pb[0], pb[1]) if head == 0 else (pb[2], pb[3])
            dnb = pb[4] if head == 0 else pb[5]
            if bb == 0:
                P.wait("tensor", oev, ub)
            slot = kb % 16
            last_inst = [None]
            for jl in range(2):
                jt = 2 * bb + jl
                for isub in range(2):
                    lbase = slot * 1024 + head * 512 + jl * 256 + isub * 128
                    P.op("tensor", lambda e, _lb=lbase, _jt=jt, _o=obk[isub],
                         _st=(bb == 0 and jl == 0), _sp=(bb == 15 and jl == 1):
                         e.matmul(_o[:, :], ptb[:, _lb:_lb + 128], v_sb[:, _jt * 512:(_jt + 1) * 512],
                                  start=_st, stop=_sp))
                    incs = [(pvs, 1)] if (jl == 1 and isub == 1) else ()
                    P.op("tensor", lambda e, _lb=lbase, _d=dnb, _is=isub,
                         _st=(bb == 0 and jl == 0 and isub == 0),
                         _sp=(bb == 15 and jl == 1 and isub == 1):
                         _mm_noldw(e, _d[:, _is:_is + 1], ptb[:, _lb:_lb + 128], ones_bf[:, 0:1],
                                   _st, _sp), incs)

        pend_cexp = {}

        def emit_pending_cexp(kk):
            for fn in pend_cexp.pop(kk, []):
                fn()

        for k in range(NB + LAG2 + 4):
            emit_pending_cexp(k)
            if k == NB:
                P.wait("tensor", c_exp, NB)
            if k >= NB + LAG2:
                continue
            if k < NB:
                u, b = divmod(k, 16)
                ic, pair = divmod(u, 8)
                if b == 0 and pair == 0:
                    P.wait("tensor", c_q, 16 * (ic + 1))
                P.wait("tensor", c_exp, k)
                for jl in range(2):
                    jt = 2 * b + jl
                    for hh in range(2):
                        ccol = jl * 256 + hh * 512
                        incs = [(c_st, 1)] if (jl == 1 and hh == 1) else ()
                        P.op("tensor", lambda e, _p=pair, _jt=jt, _ic=ic, _hh=hh, _cc=ccol, _st=(jl == 0):
                             e.matmul(pcl[:, _cc:_cc + 256],
                                      kt_sb[_hh * 64:(_hh + 1) * 64, _p * S + _jt * 128: _p * S + (_jt + 1) * 128],
                                      qbuf[_ic % 2][_hh * 64:(_hh + 1) * 64, _p * 256:(_p + 1) * 256],
                                      start=_st, stop=True,
                                      tile_position=(_hh * 64, 0)), incs)
                P.wait("scalar", c_st, k + 1)
                if k >= 16:
                    P.wait("scalar", c_pv2, k - 15)
                P.op("scalar", lambda e, _sl=(k % 16): e.activation(
                    out=ptb[:, _sl * 1024:(_sl + 1) * 1024], in_=pcl[:, 0:1024], func=AF.Exp),
                    [(c_exp, 1)])
            pv_batch(k - LAG2, 1)
            pv_batch(k - LAG1, 0)
            for head, lag in ((0, LAG1), (1, LAG2)):
                kb = k - lag
                if kb < 0 or kb >= NB or kb % 16 != 15:
                    continue
                ub = kb // 16
                icb, pairb = divmod(ub, 8)
                pvs = c_pv1 if head == 0 else c_pv2
                oev = c_oev1 if head == 0 else c_oev2
                obk = (pb[0], pb[1]) if head == 0 else (pb[2], pb[3])
                dnb = pb[4] if head == 0 else pb[5]
                hg = pairb * 2 + head
                P.wait("vector", pvs, 16 * (ub + 1))
                if icb > 0 or pairb > 0 or head > 0:
                    if pairb == 0 and head == 0:
                        P.wait("vector", c_mixd, max(0, (icb - 1) * 16))
                for isub in range(2):
                    P.op("vector", lambda e, _h=head, _is=isub, _o=obk[isub]: e.tensor_copy(
                        out=stg[_h][:, _is * 512:(_is + 1) * 512], in_=_o[:, :]),
                        [] if isub == 0 else [(c_vv, 1)])
                P.op("vector", lambda e, _d=dnb: e.reciprocal(out=csc[:, 0:2], in_=_d[:, 0:2]), [(oev, 1)])
                P.wait("vector", c_vv)
                P.wait("vector", oev)
                P.op("vector", lambda e, _h=head: e.tensor_tensor(
                    out=prod[:, :].rearrange("p (s c) -> p s c", c=512),
                    in0=stg[_h][:, :].rearrange("p (s c) -> p s c", c=512),
                    in1=wsdr[:, :].unsqueeze(1).broadcast_to([128, 2, 512]),
                    op=ALU.mult), [(c_vv, 1)])
                P.wait("vector", c_vv)
                P.op("vector", lambda e: e.reduce_sum(
                    out=csc[:, 16:32].rearrange("p (s r) -> p s r", r=8),
                    in_=prod[:, :].rearrange("p (s r d) -> p s r d", r=8, d=64),
                    axis=AX.X), [(c_vv, 1)])
                P.wait("vector", c_vv)
                P.op("vector", lambda e: e.tensor_tensor(
                    out=csc[:, 16:32].rearrange("p (s r) -> p s r", r=8),
                    in0=csc[:, 16:32].rearrange("p (s r) -> p s r", r=8),
                    in1=csc[:, 0:2].unsqueeze(2).broadcast_to([128, 2, 8]),
                    op=ALU.mult), [(c_vv, 1)])
                P.wait("vector", c_vv)
                P.op("vector", lambda e, _ic=icb, _hg=hg: e.tensor_tensor(
                    out=csc[:, 16:32].rearrange("p (s r) -> p s r", r=8),
                    in0=csc[:, 16:32].rearrange("p (s r) -> p s r", r=8),
                    in1=cqv_sb[:, :].rearrange("p (t h) -> p t h", h=H)
                        [:, 2 * _ic:2 * _ic + 2, _hg:_hg + 1].broadcast_to([128, 2, 8]),
                    op=ALU.add), [(c_vs, 1)])
                vs_val = c_vs.v

                def defer(_vs=None):
                    _vsv = vs_val

                    def go():
                        P.wait("scalar", c_vs, _vsv)
                        P.op("scalar", lambda e: e.activation(out=csc[:, 32:48], in_=csc[:, 16:32],
                                                              func=AF.Exp), [(c_sv, 1)])
                    return go
                pend_cexp.setdefault(k + 2, []).append(defer())
                P.wait("vector", c_sv, c_sv.v + 1)
                c_sv.v += 0

                P.op("vector", lambda e: e.reduce_sum(
                    out=csc[:, 2:4], in_=csc[:, 32:48].rearrange("p (s r) -> p s r", r=8),
                    axis=AX.X), [(c_vv, 1)])
                P.wait("vector", c_vv)
                P.op("vector", lambda e: e.reciprocal(out=csc[:, 4:6], in_=csc[:, 2:4]), [(c_vv, 1)])
                P.wait("vector", c_vv)
                P.op("vector", lambda e: e.tensor_tensor(
                    out=csc[:, 48:64].rearrange("p (s r) -> p s r", r=8),
                    in0=csc[:, 32:48].rearrange("p (s r) -> p s r", r=8),
                    in1=csc[:, 4:6].unsqueeze(2).broadcast_to([128, 2, 8]),
                    op=ALU.mult), [(c_vv, 1)])
                P.wait("vector", c_vv)
                P.op("vector", lambda e: e.tensor_tensor(
                    out=csc[:, 48:64].rearrange("p (s r) -> p s r", r=8),
                    in0=csc[:, 48:64].rearrange("p (s r) -> p s r", r=8),
                    in1=csc[:, 0:2].unsqueeze(2).broadcast_to([128, 2, 8]),
                    op=ALU.mult), [(c_vv, 1)])
                P.wait("vector", c_vv)
                for isub in range(2):
                    for r in range(R):
                        src = lambda _h=head, _is=isub, _r=r: stg[_h][:, _is * 512 + _r * 64: _is * 512 + _r * 64 + 64]
                        wcol = lambda _is=isub, _r=r: csc[:, 48 + _is * 8 + _r: 49 + _is * 8 + _r]
                        if r == 0:
                            P.op("vector", lambda e, _s=src, _w=wcol, _is=isub: e.tensor_scalar(
                                out=macc[_is][:, :], in0=_s(), scalar1=_w(), scalar2=None,
                                op0=ALU.mult), [(c_vv, 1)])
                        elif r < R - 1:
                            P.wait("vector", c_vv)
                            P.op("vector", lambda e, _s=src, _w=wcol, _is=isub, _r=r: e.scalar_tensor_tensor(
                                out=macc[_is][:, :] if _r % 2 == 0 else macc[_is][:, :],
                                in0=_s(), scalar=_w(), in1=macc[_is][:, :],
                                op0=ALU.mult, op1=ALU.add), [(c_vv, 1)])
                        else:
                            P.wait("vector", c_vv)
                            dst_col = (icb % 2) * 2048 + isub * 1024 + hg * 64
                            if pairb == 7 and head == 1 and isub == 1:
                                incs = [(c_mix, 1)]
                            else:
                                incs = [(c_vv, 1)]
                            P.op("vector", lambda e, _s=src, _w=wcol, _is=isub, _dc=dst_col: e.scalar_tensor_tensor(
                                out=mixst[:, _dc:_dc + 64], in0=_s(), scalar=_w(), in1=macc[_is][:, :],
                                op0=ALU.mult, op1=ALU.add), incs)
        for ic in range(16):
            P.wait("gpsimd", c_mix, ic + 1)
            P.op("gpsimd", lambda e, _ic=ic: e.dma_start(
                out=_ap(mix_sp, _ic * 256 * D, [[D, 128], [128 * D, 2], [1, D]]),
                in_=mixst[:, (_ic % 2) * 2048:(_ic % 2) * 2048 + 2048].bitcast(BF16)), [(c_mixd, 16)])
        P.wait("gpsimd", c_mixd)
        P.run()
        A.release()
        if stop_after == "C":
            return nc

        A.release()
        A.release()
        A.mark()

        wft_sb = A.alloc(8 * D, BF16)
        wr2t_sb = A.alloc(16 * D, BF16)
        g2r = A.alloc(D, F32)
        b2r = A.alloc(D, F32)
        bfr = A.alloc(D, F32)
        br2r = A.alloc(D, F32)
        mixin = [A.alloc(2 * D, BF16) for _ in range(2)]
        yb = [A.alloc(2 * D, F32) for _ in range(2)]
        x0b = [A.alloc(2 * D, F32) for _ in range(2)]
        mixT = A.alloc(8 * 256, BF16)
        onT = A.alloc(8 * 256, BF16)
        r1T = A.alloc(16 * 256, BF16)
        sq_scr = r1T.bitcast(F32)
        o1b = [A.alloc(2 * D, F32) for _ in range(2)]
        ab = A.alloc(2 * D, F32)
        onb = A.alloc(2 * D, BF16)
        zb = [A.alloc(2 * D, BF16) for _ in range(2)]
        tmpd = A.alloc(2 * D, F32)
        ztst = ab.bitcast(BF16)[:, 0:2 * D]
        std = A.alloc(64, F32)
        assert A.cur <= TAIL_OFF, f"D frame {A.cur} overlaps tail {TAIL_OFF}"

        P = Prog(nc)
        (d_dm, d_dy, d_dx0, d_dx1, d_dzt, d_vv, d_sv, d_vs, d_vt, d_ts, d_dw2) = sems(
            "ddm", "ddy", "ddx0", "ddx1", "ddzt", "dvv", "dsv", "dvs", "dvt", "dts", "ddw2")
        pclB2 = Banks(ctx, nc, [0, 1], "dpcl")
        mmB2 = Banks(ctx, nc, pb[0:4], "dmm")

        d_dwf = Sem(ctx, nc, "ddwf")
        for s in range(8):
            P.op("sync", lambda e, _s=s: e.dma_start_transpose(
                wft_sb[:, _s * D:(_s + 1) * D], wf_bf.ap()[:, _s * 128:(_s + 1) * 128]), [(d_dwf, 16)])
        for dst, srcb in ((g2r, g2), (b2r, b2), (bfr, bf_in), (br2r, br2)):
            P.op("sync", lambda e, _d=dst, _s=srcb: e.dma_start(
                out=_d[:, :], in_=_s.ap().unsqueeze(0).partition_broadcast(128)), [(d_dw2, 16)])
        for s in range(16):
            P.op("sync", lambda e, _s=s: e.dma_start_transpose(
                wr2t_sb[:, _s * D:(_s + 1) * D], wr2_bf.ap()[:, _s * 128:(_s + 1) * 128]), [(d_dw2, 16)])

        def ln_math_d(base):
            b = base
            P.op("vector", lambda e: e.tensor_scalar(out=std[:, b + 4:b + 6], in0=std[:, b + 0:b + 2],
                                                     scalar1=1.0 / D, scalar2=None, op0=ALU.mult), [(d_vv, 1)])
            P.op("vector", lambda e: e.tensor_scalar(out=std[:, b + 6:b + 8], in0=std[:, b + 2:b + 4],
                                                     scalar1=1.0 / D, scalar2=None, op0=ALU.mult), [(d_vv, 1)])
            P.wait("vector", d_vv)
            P.op("vector", lambda e: e.tensor_tensor(out=std[:, b + 8:b + 10], in0=std[:, b + 4:b + 6],
                                                     in1=std[:, b + 4:b + 6], op=ALU.mult), [(d_vv, 1)])
            P.wait("vector", d_vv)
            P.op("vector", lambda e: e.tensor_tensor(out=std[:, b + 10:b + 12], in0=std[:, b + 6:b + 8],
                                                     in1=std[:, b + 8:b + 10], op=ALU.subtract), [(d_vs, 1)])
            P.wait("scalar", d_vs)
            P.op("scalar", lambda e: e.activation(out=std[:, b + 12:b + 14], in_=std[:, b + 10:b + 12],
                                                  func=AF.Sqrt, bias=eps_t[:, :]), [(d_sv, 1)])
            P.wait("vector", d_sv)
            P.op("vector", lambda e: e.reciprocal(out=std[:, b + 14:b + 16], in_=std[:, b + 12:b + 14]), [(d_vv, 1)])
            P.wait("vector", d_vv)

        def stats(src_fn, base, pe_mark=None):
            P.wait("scalar", d_ts, pe_mark)
            P.wait("scalar", d_vv)
            for i2 in range(2):
                P.op("scalar", lambda e, _i=i2: e.activation(
                    out=sq_scr[:, _i * D:(_i + 1) * D], in_=src_fn(_i), func=AF.Square,
                    accum_out=std[:, base + 2 + _i:base + 3 + _i]), [(d_sv, 1)])
                P.op("vector", lambda e, _i=i2: e.reduce_sum(
                    out=std[:, base + _i:base + 1 + _i], in_=src_fn(_i), axis=AX.X), [(d_vv, 1)])
            P.wait("vector", d_sv)
            P.wait("vector", d_vv)
            ln_math_d(base)

        pclviews = [pclb[:, 0:1024], pclb[:, 1024:2048]]
        zviews = [pb[4][:].bitcast(BF16), pb[5][:].bitcast(BF16)]
        zB = Banks(ctx, nc, [4, 5], "dzb")

        def transpose_in(src_bf16, dstT, pool=None, views=None):
            pool = pool or pclB2
            views = views or pclviews
            for i2 in range(2):
                _, hsm = pool.acquire(P, "tensor")
                vw = views[(pool.g - 1) % 2]
                for kk in range(8):
                    P.op("tensor", lambda e, _i=i2, _k=kk, _v=vw: _transp(e,
                        _v[:, _k * 128:(_k + 1) * 128],
                        src_bf16[:, _i * D + _k * 128: _i * D + (_k + 1) * 128], id_bf[:, :], _k == 0), [(d_ts, 1)])
                P.wait("vector", d_ts)
                P.op("vector", lambda e, _i=i2, _v=vw, _d=dstT: e.tensor_copy(
                    out=_d[:, :].rearrange("p (s c) -> p s c", c=256)[:, :, _i * 128:(_i + 1) * 128],
                    in_=_v[:, :].rearrange("p (s c) -> p s c", c=128)),
                    [(hsm, 1)])
                pool.release(hsm)

        d_marks = {}

        p1_mark = {}
        p2_mark = {}

        def emit_loads(c):
            sl = c % 2
            if c >= 2:
                P.wait("sync", d_ts, p1_mark[c - 2])
                P.wait("sync", d_vv, d_marks[c - 2][0])
            P.op("sync", lambda e, _t=2 * c, _sl=sl: e.dma_start(
                out=mixin[_sl][:, :], in_=_ap(mix_sp, _t * 128 * D, [[D, 128], [128 * D, 2], [1, D]])), [(d_dm, 16)])
            P.op("sync", lambda e, _t=2 * c, _sl=sl: e.dma_start(
                out=yb[_sl][:, :], in_=_ap(y_sp, _t * 128 * D, [[D, 128], [128 * D, 2], [1, D]])), [(d_dy, 16)])
            P.op("sync", lambda e, _t=2 * c, _sl=sl: e.dma_start(
                out=x0b[_sl][:, :], in_=_ap(x_in, _t * 128 * D, [[D, 128], [128 * D, 2], [1, D]])), [(d_dx0, 16)])

        def emit_P1_pe(c):
            P.wait("tensor", d_dm, (c + 1) * 16)
            transpose_in(mixin[c % 2][:, :], mixT)
            pclB2.wait_all(P, "tensor")
            P.wait("tensor", d_dwf, 8 * 16)
            P.wait("tensor", d_dw2, 4 * 16)
            P.wait("tensor", d_dy, (c + 1) * 16)
            for tc in range(2):
                for nc2 in range(2):
                    bank, bsm = mmB2.acquire(P, "tensor")
                    for s in range(8):
                        P.op("tensor", lambda e, _s=s, _tc=tc, _n2=nc2, _b=bank: e.matmul(
                            _b[:, :], mixT[:, _s * 256 + _tc * 128: _s * 256 + (_tc + 1) * 128],
                            wft_sb[:, _s * D + _n2 * 512: _s * D + (_n2 + 1) * 512],
                            start=(_s == 0), stop=(_s == 7)), [(d_ts, 1)] if s == 7 else ())
                    emit_P1_pe.banks.append((tc, nc2, bank, bsm, d_ts.v))
            p1_mark[c] = d_ts.v

        def emit_o1_v(c):
            sl = c % 2
            if c >= 2:
                P.wait("vector", d_dx1, (c - 1) * 16)
            for (tc, nc2, bank, bsm, mmk) in emit_P1_pe.banks:
                P.wait("vector", d_ts, mmk)
                P.op("vector", lambda e, _b=bank, _n2=nc2: e.scalar_tensor_tensor(
                    out=tmpd[:, _n2 * 512:(_n2 + 1) * 512], in0=_b[:, :], scalar=1.0,
                    in1=bfr[:, _n2 * 512:(_n2 + 1) * 512], op0=ALU.mult, op1=ALU.add),
                    [(bsm, 1)])
                bcnt = mmB2.release(bsm)
                P.wait("vector", bsm, bcnt)
                P.op("vector", lambda e, _sl=sl, _tc=tc, _n2=nc2: e.tensor_tensor(
                    out=o1b[_sl][:, _tc * D + _n2 * 512: _tc * D + (_n2 + 1) * 512],
                    in0=tmpd[:, _n2 * 512:(_n2 + 1) * 512],
                    in1=yb[_sl][:, _tc * D + _n2 * 512: _tc * D + (_n2 + 1) * 512],
                    op=ALU.add), [(d_vv, 1)])
            emit_P1_pe.banks = []

        def emit_LN_v(c):
            sl = c % 2
            stats(lambda i, _sl=sl: o1b[_sl][:, i * D:(i + 1) * D], 0,
                  pe_mark=p2_mark.get(c - 1))
            if c >= 1:
                P.wait("vector", d_ts, p2_mark[c - 1])
            for i2 in range(2):
                P.op("vector", lambda e, _i=i2: e.scalar_tensor_tensor(
                    out=std[:, 48 + _i:49 + _i], in0=std[:, 4 + _i:5 + _i], scalar=-1.0,
                    in1=std[:, 14 + _i:15 + _i], op0=ALU.mult, op1=ALU.mult), [(d_vs, 1)])
            P.wait("scalar", d_vv)
            P.wait("scalar", d_vt)
            P.wait("scalar", d_vs)
            for i2 in range(2):
                P.op("scalar", lambda e, _i=i2, _sl=sl: e.activation(
                    out=tmpd[:, _i * D:(_i + 1) * D], in_=o1b[_sl][:, _i * D:(_i + 1) * D],
                    func=AF.Identity, scale=std[:, 14 + _i:15 + _i],
                    bias=std[:, 48 + _i:49 + _i]), [(d_sv, 1)])
            P.wait("vector", d_sv)
            for i2 in range(2):
                P.op("vector", lambda e, _i=i2: e.tensor_tensor(
                    out=tmpd[:, _i * D:(_i + 1) * D], in0=tmpd[:, _i * D:(_i + 1) * D],
                    in1=g2r[:, :], op=ALU.mult), [(d_vv, 1)])
                P.wait("vector", d_vv)
                P.op("vector", lambda e, _i=i2: e.tensor_tensor(
                    out=onb[:, _i * D:(_i + 1) * D], in0=tmpd[:, _i * D:(_i + 1) * D],
                    in1=b2r[:, :], op=ALU.add), [(d_vt, 1)])

        def emit_P2_pe(c):
            P.wait("tensor", d_vt)
            transpose_in(onb[:, :], onT)
            pclB2.wait_all(P, "tensor")
            P.wait("tensor", c_pw, 8 * 16)
            for dt in range(16):
                bank, bsm = mmB2.acquire(P, "tensor")
                for s in range(8):
                    P.op("tensor", lambda e, _s=s, _dt=dt, _b=bank: e.matmul(
                        _b[:, 0:256], wr1t_sb[:, _s * DR + _dt * 128: _s * DR + (_dt + 1) * 128],
                        onT[:, _s * 256:(_s + 1) * 256], start=(_s == 0), stop=(_s == 7)),
                        [(d_ts, 1)] if s == 7 else ())
                P.wait("scalar", d_ts)
                P.op("scalar", lambda e, _dt=dt, _b=bank: e.activation(
                    out=r1T[:, _dt * 256:(_dt + 1) * 256], in_=_b[:, 0:256], func=AF.Relu,
                    bias=br1s[:, _dt:_dt + 1]), [(bsm, 1)])
                mmB2.release(bsm)
            mmB2.wait_all(P, "tensor")
            P.wait("tensor", d_dw2, (4 + 16) * 16)
            for tc in range(2):
                for nc2 in range(2):
                    bank, bsm = mmB2.acquire(P, "tensor")
                    for s in range(16):
                        P.op("tensor", lambda e, _s=s, _tc=tc, _n2=nc2, _b=bank: e.matmul(
                            _b[:, :], r1T[:, _s * 256 + _tc * 128: _s * 256 + (_tc + 1) * 128],
                            wr2t_sb[:, _s * D + _n2 * 512: _s * D + (_n2 + 1) * 512],
                            start=(_s == 0), stop=(_s == 15)), [(d_ts, 1)] if s == 15 else ())
                    emit_P2_pe.banks.append((tc, nc2, bank, bsm, d_ts.v))
            p2_mark[c] = d_ts.v

        def emit_a_v(c):
            sl = c % 2
            P.wait("vector", d_dzt, max(0, c * 16))
            for (tc, nc2, bank, bsm, mmk) in emit_P2_pe.banks:
                P.wait("vector", d_ts, mmk)
                P.op("vector", lambda e, _b=bank, _n2=nc2: e.scalar_tensor_tensor(
                    out=tmpd[:, _n2 * 512:(_n2 + 1) * 512], in0=_b[:, :], scalar=1.0,
                    in1=br2r[:, _n2 * 512:(_n2 + 1) * 512], op0=ALU.mult, op1=ALU.add),
                    [(bsm, 1)])
                bcnt = mmB2.release(bsm)
                P.wait("vector", bsm, bcnt)
                P.op("vector", lambda e, _sl=sl, _tc=tc, _n2=nc2: e.tensor_tensor(
                    out=ab[:, _tc * D + _n2 * 512: _tc * D + (_n2 + 1) * 512],
                    in0=tmpd[:, _n2 * 512:(_n2 + 1) * 512],
                    in1=o1b[_sl][:, _tc * D + _n2 * 512: _tc * D + (_n2 + 1) * 512],
                    op=ALU.add), [(d_vv, 1)])
            emit_P2_pe.banks = []

        def emit_chains_v(c):
            sl = c % 2
            stats(lambda i: ab[:, i * D:(i + 1) * D], 16)
            for i2 in range(2):
                P.op("vector", lambda e, _i=i2: e.scalar_tensor_tensor(
                    out=std[:, 50 + _i:51 + _i], in0=std[:, 20 + _i:21 + _i], scalar=-1.0,
                    in1=std[:, 30 + _i:31 + _i], op0=ALU.mult, op1=ALU.mult), [(d_vs, 1)])
            P.wait("scalar", d_vv)
            P.wait("scalar", d_vt)
            P.wait("scalar", d_vs)
            for i2 in range(2):
                P.op("scalar", lambda e, _i=i2: e.activation(
                    out=tmpd[:, _i * D:(_i + 1) * D], in_=ab[:, _i * D:(_i + 1) * D],
                    func=AF.Identity, scale=std[:, 30 + _i:31 + _i],
                    bias=std[:, 50 + _i:51 + _i]), [(d_sv, 1)])
            P.wait("vector", d_dx0, (c + 1) * 16)
            P.wait("vector", d_sv)
            for i2 in range(2):
                P.op("vector", lambda e, _i=i2, _sl=sl: e.tensor_tensor(
                    out=o1b[_sl][:, _i * D:(_i + 1) * D], in0=tmpd[:, _i * D:(_i + 1) * D],
                    in1=x0b[_sl][:, _i * D:(_i + 1) * D], op=ALU.add), [(d_vv, 1)])
            P.wait("gpsimd", d_vv)
            P.op("gpsimd", lambda e, _t=2 * c, _sl=sl: e.dma_start(
                out=_ap(x1_sp, _t * 128 * D, [[D, 128], [128 * D, 2], [1, D]]), in_=o1b[_sl][:, :]), [(d_dx1, 16)])
            stats(lambda i, _sl=sl: o1b[_sl][:, i * D:(i + 1) * D], 32)
            for i2 in range(2):
                P.op("vector", lambda e, _i=i2: e.scalar_tensor_tensor(
                    out=std[:, 52 + _i:53 + _i], in0=std[:, 36 + _i:37 + _i], scalar=-1.0,
                    in1=std[:, 46 + _i:47 + _i], op0=ALU.mult, op1=ALU.mult), [(d_vs, 1)])
            P.wait("scalar", d_vv)
            P.wait("scalar", d_vs)
            P.wait("scalar", d_ts)
            for i2 in range(2):
                P.op("scalar", lambda e, _i=i2, _sl=sl: e.activation(
                    out=zb[_sl][:, _i * D:(_i + 1) * D], in_=o1b[_sl][:, _i * D:(_i + 1) * D],
                    func=AF.Identity, scale=std[:, 46 + _i:47 + _i],
                    bias=std[:, 52 + _i:53 + _i]), [(d_sv, 1)])
            d_marks[c] = (d_vv.v, d_ts.v)

        def emit_P3_pe(c):
            P.wait("tensor", d_sv)
            P.wait("tensor", d_dzt)
            transpose_in(zb[c % 2][:, :], ztst)
            pclB2.wait_all(P, "gpsimd")
            P.op("gpsimd", lambda e, _c=c: e.dma_start(
                out=_ap(zt_sp, _c * 256, [[S, 128], [128 * S, 8], [1, 256]]), in_=ztst[:, :]), [(d_dzt, 16)])

        emit_P1_pe.banks = []
        emit_P2_pe.banks = []
        emit_loads(0)
        emit_loads(1)
        if D_SEQ[0]:
            for c in range(16):
                emit_P1_pe(c)
                emit_o1_v(c)
                emit_LN_v(c)
                emit_P2_pe(c)
                emit_a_v(c)
                emit_chains_v(c)
                if 2 <= c + 2 < 16:
                    emit_loads(c + 2)
                emit_P3_pe(c)
        else:
            for c in range(18):
                if c < 16:
                    emit_P1_pe(c)
                if 1 <= c <= 16:
                    emit_LN_v(c - 1)
                if c >= 2:
                    emit_P3_pe(c - 2)
                if c < 16:
                    emit_o1_v(c)
                if 1 <= c <= 16:
                    emit_P2_pe(c - 1)
                    emit_a_v(c - 1)
                    emit_chains_v(c - 1)
                if 2 <= c + 1 < 16:
                    emit_loads(c + 1)
        P.wait("gpsimd", d_dx1)
        P.wait("gpsimd", d_dzt)
        P.run()
        A.release()
        if stop_after == "D":
            return nc

        A.mark()
        wgt_sb = A.alloc(8 * DF, BF16)
        wut_sb = A.alloc(8 * DF, BF16)
        wdt_sb = A.alloc(24 * D, BF16)
        ztb = A.alloc(8 * 512, BF16)
        htb = A.alloc(24 * 512, BF16)
        silb = A.alloc(512, F32)
        x1b = A.alloc(D, F32)
        fb = A.alloc(D, F32)
        lnfb = A.alloc(D, F32)
        outb = A.alloc(D, F32)
        ste = A.alloc(32, F32)
        tmpe = A.alloc(D, F32)

        P = Prog(nc)
        (e_dz, e_dx1, e_do, e_vv, e_sv, e_vs, e_ts, e_dwg, e_dwu, e_dwd, e_hc) = sems(
            "edz", "edx1", "edo", "evv", "esv", "evs", "ets", "edwg", "edwu", "edwd", "ehc")
        mmB3 = Banks(ctx, nc, pb[0:4], "emm")

        P.op("sync", lambda e: e.dma_start(
            out=ztb[:, :], in_=_ap(zt_sp, 0, [[S, 128], [128 * S, 8], [1, 512]])), [(e_dz, 16)])
        for sb, wbf, wsem in ((wgt_sb, wg_bf, e_dwg), (wut_sb, wu_bf, e_dwu)):
            for s in range(8):
                P.op("sync", lambda e, _w=wbf, _sb=sb, _s=s: e.dma_start_transpose(
                    _sb[:, _s * DF:(_s + 1) * DF], _w.ap()[:, _s * 128:(_s + 1) * 128]), [(wsem, 16)])
        for cs in range(24):
            P.op("sync", lambda e, _cs=cs: e.dma_start_transpose(
                wdt_sb[:, _cs * D:(_cs + 1) * D], wd_bf.ap()[:, _cs * 128:(_cs + 1) * 128]), [(e_dwd, 16)])

        def ln_math_e():
            P.op("vector", lambda e: e.tensor_scalar(out=ste[:, 4:5], in0=ste[:, 0:1],
                                                     scalar1=1.0 / D, scalar2=None, op0=ALU.mult), [(e_vv, 1)])
            P.op("vector", lambda e: e.tensor_scalar(out=ste[:, 6:7], in0=ste[:, 2:3],
                                                     scalar1=1.0 / D, scalar2=None, op0=ALU.mult), [(e_vv, 1)])
            P.wait("vector", e_vv)
            P.op("vector", lambda e: e.tensor_tensor(out=ste[:, 8:9], in0=ste[:, 4:5],
                                                     in1=ste[:, 4:5], op=ALU.mult), [(e_vv, 1)])
            P.wait("vector", e_vv)
            P.op("vector", lambda e: e.tensor_tensor(out=ste[:, 10:11], in0=ste[:, 6:7],
                                                     in1=ste[:, 8:9], op=ALU.subtract), [(e_vs, 1)])
            P.wait("scalar", e_vs)
            P.op("scalar", lambda e: e.activation(out=ste[:, 12:13], in_=ste[:, 10:11],
                                                  func=AF.Sqrt, bias=eps_t[:, :]), [(e_sv, 1)])
            P.wait("vector", e_sv)
            P.op("vector", lambda e: e.reciprocal(out=ste[:, 14:15], in_=ste[:, 12:13]), [(e_vv, 1)])
            P.wait("vector", e_vv)

        e_marks = {}
        for n in range(8):
            if n >= 1:
                P.wait("sync", e_ts, e_marks[n - 1])
                P.op("sync", lambda e, _n=n: e.dma_start(
                    out=ztb[:, :], in_=_ap(zt_sp, _n * 512, [[S, 128], [128 * S, 8], [1, 512]])), [(e_dz, 16)])
            P.wait("tensor", e_dz, 16 * (n + 1))
            if n == 0:
                P.wait("tensor", e_dwg, 8 * 16)
            for dt in range(24):
                bank_g, bsm_g = mmB3.acquire(P, "tensor")
                for s in range(8):
                    P.op("tensor", lambda e, _s=s, _dt=dt, _b=bank_g: e.matmul(
                        _b[:, :], wgt_sb[:, _s * DF + _dt * 128: _s * DF + (_dt + 1) * 128],
                        ztb[:, _s * 512:(_s + 1) * 512], start=(_s == 0), stop=(_s == 7)),
                        [(e_ts, 1)] if s == 7 else ())
                if n == 0 and dt == 0:
                    P.wait("tensor", e_dwu, 8 * 16)
                bank_u, bsm_u = mmB3.acquire(P, "tensor")
                for s in range(8):
                    P.op("tensor", lambda e, _s=s, _dt=dt, _b=bank_u: e.matmul(
                        _b[:, :], wut_sb[:, _s * DF + _dt * 128: _s * DF + (_dt + 1) * 128],
                        ztb[:, _s * 512:(_s + 1) * 512], start=(_s == 0), stop=(_s == 7)),
                        [(e_ts, 1)] if s == 7 else ())
                P.wait("scalar", e_ts)
                P.op("scalar", lambda e, _b=bank_g: e.activation(
                    out=silb[:, :], in_=_b[:, :], func=AF.Silu, scale=SC_FFN), [(bsm_g, 1)])
                gcnt = mmB3.release(bsm_g)
                P.wait("vector", bsm_g, gcnt)
                P.wait("vector", e_hc, n)
                P.op("vector", lambda e, _b=bank_u, _dt=dt: e.scalar_tensor_tensor(
                    out=htb[:, _dt * 512:(_dt + 1) * 512], in0=_b[:, :], scalar=SC_FFN,
                    in1=silb[:, :], op0=ALU.mult, op1=ALU.mult), [(bsm_u, 1)])
                mmB3.release(bsm_u)
            e_marks[n] = e_ts.v
            mmB3.wait_all(P, "tensor")
            if n == 0:
                P.wait("tensor", e_dwd, 24 * 16)
            for tc in range(4):
                t_abs = 4 * n + tc
                P.wait("sync", e_do, max(0, t_abs * 16))
                P.op("sync", lambda e, _t=t_abs: e.dma_start(
                    out=x1b[:, :], in_=_ap(x1_sp, _t * 128 * D, [[D, 128], [1, D]])), [(e_dx1, 16)])
                for nc2 in range(2):
                    bank, bsm = mmB3.acquire(P, "tensor")
                    last_of_chunk = (tc == 3 and nc2 == 1)
                    for s in range(24):
                        if s < 23:
                            incs = []
                        else:
                            incs = [(e_hc, 1)] if last_of_chunk else [(e_ts, 1)]
                        P.op("tensor", lambda e, _s=s, _tc=tc, _n2=nc2, _b=bank: e.matmul(
                            _b[:, :], htb[:, _s * 512 + _tc * 128: _s * 512 + (_tc + 1) * 128],
                            wdt_sb[:, _s * D + _n2 * 512: _s * D + (_n2 + 1) * 512],
                            start=(_s == 0), stop=(_s == 23)), incs)
                    if last_of_chunk:
                        P.wait("vector", e_hc, n + 1)
                    else:
                        P.wait("vector", e_ts)
                    P.op("vector", lambda e, _b=bank, _n2=nc2: e.tensor_scalar(
                        out=fb[:, _n2 * 512:(_n2 + 1) * 512], in0=_b[:, :],
                        scalar1=SC_FFN, scalar2=None, op0=ALU.mult), [(bsm, 1)])
                    mmB3.release(bsm)
                mmB3.wait_all(P, "scalar")
                P.op("scalar", lambda e: e.activation(
                    out=tmpe[:, :], in_=fb[:, :], func=AF.Square, accum_out=ste[:, 2:3]), [(e_sv, 1)])
                P.op("vector", lambda e: e.reduce_sum(out=ste[:, 0:1], in_=fb[:, :], axis=AX.X), [(e_vv, 1)])
                P.wait("vector", e_sv)
                P.wait("vector", e_vv)
                ln_math_e()
                P.wait("vector", e_dx1)
                P.op("vector", lambda e: e.tensor_scalar(
                    out=tmpe[:, :], in0=fb[:, :], scalar1=ste[:, 4:5], scalar2=ste[:, 14:15],
                    op0=ALU.subtract, op1=ALU.mult), [(e_vv, 1)])
                P.wait("vector", e_vv)
                P.op("vector", lambda e: e.tensor_tensor(
                    out=outb[:, :], in0=tmpe[:, :], in1=x1b[:, :], op=ALU.add), [(e_vv, 1)])
                P.wait("gpsimd", e_vv)
                P.op("gpsimd", lambda e, _t=t_abs: e.dma_start(
                    out=_ap(out, _t * 128 * D, [[D, 128], [1, D]]), in_=outb[:, :]), [(e_do, 16)])
        if ht_tap is not None:
            s_tp2, = sems("etap")
            P.wait("gpsimd", e_vv)
            P.op("gpsimd", lambda e: e.dma_start(out=ht_tap.ap()[:, :], in_=htb[:, :]), [(s_tp2, 16)])
            P.op("gpsimd", lambda e: e.dma_start(out=zb_tap.ap()[:, :], in_=ztb[:, :]), [(s_tp2, 16)])
            P.op("gpsimd", lambda e: e.dma_start(out=wdt_tap.ap()[:, :], in_=wdt_sb[:, :]), [(s_tp2, 16)])
            P.op("gpsimd", lambda e: e.dma_start(out=fb_tap.ap()[:, :], in_=fb[:, :]), [(s_tp2, 16)])
            P.wait("gpsimd", s_tp2)
        P.wait("gpsimd", e_do)
        P.run()
        A.release()

        assert A.peak <= A.nbytes

    return nc


_NC_CACHE = {}


def _get_program(taps=()):
    key = tuple(sorted(taps))
    if key not in _NC_CACHE:
        _NC_CACHE[key] = build_program(taps)
    return _NC_CACHE[key]


def kernel(L0_attn_x, L0_attn_Wq, L0_attn_bq, L0_attn_Wk, L0_attn_bk,
           L0_attn_Wv, L0_attn_bv, L0_attn_Wqv, L0_attn_bqv, L0_attn_Ws,
           L0_attn_bs, L0_attn_Wf, L0_attn_bf, L0_attn_Wr1, L0_attn_br1,
           L0_attn_Wr2, L0_attn_br2, L0_attn_g1, L0_attn_b1, L0_attn_g2,
           L0_attn_b2, L0_ffn_w_gate, L0_ffn_w_up, L0_ffn_w_down):
    x = np.ascontiguousarray(np.asarray(L0_attn_x, dtype=np.float32))
    weights = {
        "Wq": L0_attn_Wq, "bq": L0_attn_bq, "Wk": L0_attn_Wk, "bk": L0_attn_bk,
        "Wv": L0_attn_Wv, "bv": L0_attn_bv, "Wqv": L0_attn_Wqv, "bqv": L0_attn_bqv,
        "Ws": L0_attn_Ws, "bs": L0_attn_bs, "Wf": L0_attn_Wf, "bf": L0_attn_bf,
        "Wr1": L0_attn_Wr1, "br1": L0_attn_br1, "Wr2": L0_attn_Wr2, "br2": L0_attn_br2,
        "g1": L0_attn_g1, "b1": L0_attn_b1, "g2": L0_attn_g2, "b2": L0_attn_b2,
        "wg": L0_ffn_w_gate, "wu": L0_ffn_w_up, "wd": L0_ffn_w_down,
    }
    weights = {k: np.ascontiguousarray(np.asarray(v)) for k, v in weights.items()}
    nc = _get_program()
    in_maps = [dict(weights, x=np.ascontiguousarray(x[c])) for c in range(8)]
    res = run_bass_kernel_spmd(nc, in_maps, core_ids=list(range(8)))
    return np.stack([np.asarray(r["out"], dtype=np.float32) for r in res.results], axis=0)


# version 06
# speedup vs baseline: 1.0094x; 1.0094x over previous
import contextlib

import numpy as np

import concourse.bass as bass
import concourse.mybir as mybir
from concourse.bass_utils import run_bass_kernel_spmd

F32 = mybir.dt.float32
BF16 = mybir.dt.bfloat16
I32 = mybir.dt.int32
U8 = mybir.dt.uint8
AF = mybir.ActivationFunctionType
ALU = mybir.AluOpType
AX = mybir.AxisListType

S = 4096
D = 1024
H = 16
R = 8
QKD = 32
DR = 2048
DF = 3072
EPS = 1e-5
NT = S // 128
SC_Q = 0.125
SC_QV = float(1.0 / np.float32(np.sqrt(np.float32(QKD))))
SC_FFN = float(np.float32(1.0 / 127.0))

_MMC = [0]
D_SEQ = [False]
ENGS = ("sync", "tensor", "vector", "scalar", "gpsimd")


def _transp(t, out, in_, ident, start):
    return t.matmul(out, in_, ident, start=start, stop=True, is_transpose=True)


def _mm_noldw(t, out, lhsT, rhs, start, stop, tile_position=None):
    inst = mybir.InstMatmult(
        name=f"mmnl_{_MMC[0]}", engine=mybir.EngineType.PE,
        ins=(t.lower_ap(rhs), t.lower_ap(lhsT)), outs=(t.lower_ap(out),),
        replication_resolution=0, replication_shift_amnt=0, replication_num_rows=0,
        start_tensor_calc=start, stop_tensor_calc=stop, ldweights=False,
        tile_position=tile_position)
    _MMC[0] += 1
    return t.add_instruction(inst)


class Sem:
    def __init__(self, ctx, nc, name):
        self.h = ctx.enter_context(nc.semaphore(name))
        self.v = 0


class Prog:

    def __init__(self, nc):
        self.nc = nc
        self.q = {e: [] for e in ENGS}

    def wait(self, eng, sm, val=None):
        v = sm.v if val is None else val
        if v > 0:
            self.q[eng].append(lambda e, _h=sm.h, _v=v: e.wait_ge(_h, _v))

    def op(self, eng, fn, incs=()):
        assert len(incs) <= 1, "hardware: one semaphore update per instruction"
        cap = []
        for sm, n in incs:
            sm.v += n
            cap.append((sm.h, n))

        def run(e, _fn=fn, _cap=cap):
            inst = _fn(e)
            for h, n in _cap:
                inst.then_inc(h, n)
        self.q[eng].append(run)

    def run(self):
        with self.nc.Block() as blk:
            for name in ENGS:
                lst = self.q[name]
                if lst:
                    getattr(blk, name)(lambda e, _l=lst: [f(e) for f in _l])


class Banks:

    def __init__(self, ctx, nc, banks, tag):
        self.banks = banks
        self.sems = [Sem(ctx, nc, f"{tag}{i}") for i in range(len(banks))]
        self.used = [0] * len(banks)
        self.g = 0

    def acquire(self, P, eng="tensor"):
        i = self.g % len(self.banks)
        self.g += 1
        P.wait(eng, self.sems[i], self.used[i])
        return self.banks[i], self.sems[i]

    def release(self, sm):
        i = self.sems.index(sm)
        self.used[i] += 1
        return self.used[i]

    def wait_all(self, P, eng):
        for i in range(len(self.sems)):
            P.wait(eng, self.sems[i], self.used[i])


class Arena:
    def __init__(self, ctx, nc, nbytes):
        self.t = ctx.enter_context(nc.sbuf_tensor("arena", [128, nbytes], U8))
        self.nbytes = nbytes
        self.cur = 0
        self.peak = 0
        self.marks = []

    def alloc(self, cols, dtype, parts=128, align=64):
        isz = mybir.dt.size(dtype)
        off = (self.cur + align - 1) // align * align
        self.cur = off + cols * isz
        self.peak = max(self.peak, self.cur)
        assert self.cur <= self.nbytes, f"SBUF arena overflow {self.cur}"
        return self.t.ap().bitcast(dtype)[0:parts, off // isz: off // isz + cols]

    def mark(self):
        self.marks.append(self.cur)

    def release(self):
        self.cur = self.marks.pop()


def _ap(t, offset, dims):
    return bass.AP(t, offset, dims)


def build_program(taps=(), stop_after=None):
    tapset = set(taps)
    nc = bass.Bass("TRN2", target_bir_lowering=False)

    def dram(name, shape, dtype, kind=None):
        if kind is None:
            return nc.dram_tensor(name, shape, dtype)
        return nc.dram_tensor(name, shape, dtype, kind=kind)

    x_in = dram("x", [S, D], F32, "ExternalInput")
    wq = dram("Wq", [D, D], F32, "ExternalInput")
    bq = dram("bq", [D], F32, "ExternalInput")
    wk = dram("Wk", [D, D], F32, "ExternalInput")
    bk = dram("bk", [D], F32, "ExternalInput")
    wv = dram("Wv", [512, D], F32, "ExternalInput")
    bv = dram("bv", [512], F32, "ExternalInput")
    wqv = dram("Wqv", [512, D], F32, "ExternalInput")
    bqv = dram("bqv", [512], F32, "ExternalInput")
    ws = dram("Ws", [1, 96], F32, "ExternalInput")
    bs_in = dram("bs", [1], F32, "ExternalInput")
    wf = dram("Wf", [D, D], F32, "ExternalInput")
    bf_in = dram("bf", [D], F32, "ExternalInput")
    wr1 = dram("Wr1", [DR, D], F32, "ExternalInput")
    br1 = dram("br1", [DR], F32, "ExternalInput")
    wr2 = dram("Wr2", [D, DR], F32, "ExternalInput")
    br2 = dram("br2", [D], F32, "ExternalInput")
    g1 = dram("g1", [D], F32, "ExternalInput")
    b1 = dram("b1", [D], F32, "ExternalInput")
    g2 = dram("g2", [D], F32, "ExternalInput")
    b2 = dram("b2", [D], F32, "ExternalInput")
    wg = dram("wg", [DF, D], I32, "ExternalInput")
    wu = dram("wu", [DF, D], I32, "ExternalInput")
    wd = dram("wd", [D, DF], I32, "ExternalInput")
    out = dram("out", [S, D], F32, "ExternalOutput")

    def scr(name, shape, dtype, tag):
        return dram(name, shape, dtype, "ExternalOutput" if tag in tapset else None)

    y_sp = scr("y_sp", [S, D], F32, "A")
    qt_sp = scr("qt_sp", [8, 128, S], BF16, "B")
    mix_sp = scr("mix_sp", [S, D], BF16, "C")
    x1_sp = scr("x1_sp", [S, D], F32, "D")
    zt_sp = scr("zt_sp", [8, 128, S], BF16, "D")
    wq_bf = dram("wq_bf", [D, D], BF16)
    wk_bf = dram("wk_bf", [D, D], BF16)
    wv_bf = dram("wv_bf", [512, D], BF16)
    wqv_bf = dram("wqv_bf", [512, D], BF16)
    wf_bf = dram("wf_bf", [D, D], BF16)
    wr1_bf = dram("wr1_bf", [DR, D], BF16)
    wr2_bf = dram("wr2_bf", [D, DR], BF16)
    wg_bf = dram("wg_bf", [DF, D], BF16)
    wu_bf = dram("wu_bf", [DF, D], BF16)
    wd_bf = dram("wd_bf", [D, DF], BF16)
    kt_tap = scr("kt_tap", [128, 8 * S], BF16, "B") if "B" in tapset else None
    v_tap = scr("v_tap", [128, NT * 512], BF16, "B") if "B" in tapset else None
    cqv_tap = scr("cqv_tap", [128, NT * H], F32, "B") if "B" in tapset else None
    wqt_tap = scr("wqt_tap", [128, 8 * D], BF16, "B") if "B" in tapset else None
    ht_tap = scr("ht_tap", [128, 24 * 512], BF16, "E") if "E" in tapset else None
    zb_tap = scr("zb_tap", [128, 8 * 512], BF16, "E") if "E" in tapset else None
    wdt_tap = scr("wdt_tap", [128, 24 * D], BF16, "E") if "E" in tapset else None
    fb_tap = scr("fb_tap", [128, D], F32, "E") if "E" in tapset else None
    xnt_tap = scr("xnt_tap", [128, 8 * 512], BF16, "B") if "B" in tapset else None

    ctx = contextlib.ExitStack()
    with ctx:
        A = Arena(ctx, nc, 206208)

        id_bf = A.alloc(128, BF16)
        id_f32 = A.alloc(128, F32)
        ones_bf = A.alloc(8, BF16)
        wsdr = A.alloc(512, BF16)
        wsqr = A.alloc(512, BF16)
        bss = A.alloc(1, F32)
        bqs = A.alloc(8, F32)
        bks = A.alloc(8, F32)
        br1s = A.alloc(16, F32)
        eps_t = A.alloc(1, F32)
        A.mark()
        kt_sb = A.alloc(8 * S, BF16)
        v_sb = A.alloc(NT * 512, BF16)
        cqv_sb = A.alloc(NT * H, F32)
        A.mark()

        TAIL_OFF = A.nbytes - 32 * 1024
        assert TAIL_OFF % 64 == 0

        def tail_ap(off_bytes, cols, dtype):
            isz = mybir.dt.size(dtype)
            return A.t.ap().bitcast(dtype)[0:128, off_bytes // isz: off_bytes // isz + cols]

        wr1t_sb = tail_ap(TAIL_OFF, 8 * DR, BF16)

        pcl = ctx.enter_context(nc.psum_tensor("pcl", [128, 1024], F32))
        pclb = pcl[:].bitcast(BF16)
        pb = [ctx.enter_context(nc.psum_tensor(f"pb{i}", [128, 512], F32)) for i in range(6)]

        def sems(*names):
            return [Sem(ctx, nc, n) for n in names]

        A.mark()
        iota_j = A.alloc(128, F32)
        iota_p = A.alloc(128, F32)
        btmp = A.alloc(128, F32)
        P = Prog(nc)
        d0, g0, t0, v0 = sems("p0d", "p0g", "p0t", "p0v")
        P.op("gpsimd", lambda e: e.dma_start(out=wsdr[:, :], in_=_ap(ws, 32, [[0, 128], [0, 8], [1, 64]])), [(d0, 16)])
        P.op("gpsimd", lambda e: e.dma_start(out=wsqr[:, :], in_=_ap(ws, 0, [[0, 128], [0, 16], [1, 32]])), [(d0, 16)])
        P.op("sync", lambda e: e.dma_start(out=bss[:, :], in_=_ap(bs_in, 0, [[0, 128], [1, 1]])), [(d0, 16)])
        P.op("sync", lambda e: e.dma_start(out=btmp[0:8, :], in_=_ap(bq, 0, [[128, 8], [1, 128]])), [(d0, 16)])
        P.op("sync", lambda e: e.dma_start(out=btmp[8:16, :], in_=_ap(bk, 0, [[128, 8], [1, 128]])), [(d0, 16)])
        P.op("sync", lambda e: e.dma_start(out=btmp[16:32, :], in_=_ap(br1, 0, [[128, 16], [1, 128]])), [(d0, 16)])
        P.op("gpsimd", lambda e: e.iota(iota_j[:, :], [[1, 128]], channel_multiplier=0,
                                        allow_small_or_imprecise_dtypes=True))
        P.op("gpsimd", lambda e: e.iota(iota_p[:, :], [[0, 128]], channel_multiplier=1,
                                        allow_small_or_imprecise_dtypes=True), [(g0, 1)])
        P.op("gpsimd", lambda e: e.memset(ones_bf[:, :], 1.0), [(g0, 1)])
        P.op("gpsimd", lambda e: e.memset(eps_t[:, :], EPS), [(g0, 1)])
        P.wait("vector", g0)
        P.op("vector", lambda e: e.tensor_tensor(out=id_f32[:, :], in0=iota_j[:, :], in1=iota_p[:, :], op=ALU.is_equal))
        P.op("vector", lambda e: e.tensor_tensor(out=id_bf[:, :], in0=iota_j[:, :], in1=iota_p[:, :], op=ALU.is_equal), [(v0, 1)])
        P.wait("tensor", d0)
        P.wait("tensor", v0)
        P.op("tensor", lambda e: e.transpose(pcl[:, 0:128], btmp[:, :], id_f32[:, :]), [(t0, 1)])
        P.wait("vector", t0)
        P.op("vector", lambda e: e.tensor_scalar(out=bqs[:, :], in0=pcl[:, 0:8], scalar1=SC_Q, scalar2=None, op0=ALU.mult))
        P.op("vector", lambda e: e.tensor_copy(out=bks[:, :], in_=pcl[:, 8:16]))
        P.op("vector", lambda e: e.tensor_copy(out=br1s[:, :], in_=pcl[:, 16:32]))
        P.run()
        A.release()

        A.mark()
        wqt_sb = A.alloc(8 * D, BF16)
        wkt_sb = A.alloc(8 * D, BF16)
        wvt_sb = A.alloc(8 * 512, BF16)
        wqvt_sb = A.alloc(8 * 512, BF16)
        g1r = A.alloc(D, F32)
        b1r = A.alloc(D, F32)
        bvr = A.alloc(512, F32)
        bqvr_s = A.alloc(512, F32)
        xt_b = [A.alloc(D, F32) for _ in range(2)]
        xn_v = [b.bitcast(BF16) for b in xt_b]
        ln1_b = A.alloc(D, F32)
        yb_ab = [A.alloc(D, F32) for _ in range(2)]
        st_b = A.alloc(64, F32)
        xnt_b = A.alloc(8 * 512, BF16)
        qk_st = [A.alloc(512, BF16) for _ in range(2)]
        qv_b = A.alloc(512, BF16)
        cq_b = A.alloc(512, F32)
        cqt_b = A.alloc(16, F32)

        P = Prog(nc)
        (s_dx, s_dw, s_dy, s_dq, s_dwp, s_vv, s_sv, s_vs, s_vt, s_ts, s_st,
         s_xc, s_xcs, s_wpb) = sems(
            "abdx", "abdw", "abdy", "abdq", "abdwp", "abvv", "absv", "abvs",
            "abvt", "abts", "abst", "abxc", "abxcs", "abwpb")
        pclB = Banks(ctx, nc, [0, 1], "abpcl")
        mmB = Banks(ctx, nc, pb[0:4], "abmm")

        def ln_math(base):
            b = base
            P.op("vector", lambda e: e.tensor_scalar(out=st_b[:, b + 4:b + 6], in0=st_b[:, b + 0:b + 2],
                                                     scalar1=1.0 / D, scalar2=None, op0=ALU.mult), [(s_vv, 1)])
            P.op("vector", lambda e: e.tensor_scalar(out=st_b[:, b + 6:b + 8], in0=st_b[:, b + 2:b + 4],
                                                     scalar1=1.0 / D, scalar2=None, op0=ALU.mult), [(s_vv, 1)])
            P.wait("vector", s_vv)
            P.op("vector", lambda e: e.tensor_tensor(out=st_b[:, b + 8:b + 10], in0=st_b[:, b + 4:b + 6],
                                                     in1=st_b[:, b + 4:b + 6], op=ALU.mult), [(s_vv, 1)])
            P.wait("vector", s_vv)
            P.op("vector", lambda e: e.tensor_tensor(out=st_b[:, b + 10:b + 12], in0=st_b[:, b + 6:b + 8],
                                                     in1=st_b[:, b + 8:b + 10], op=ALU.subtract), [(s_vs, 1)])
            P.wait("scalar", s_vs)
            P.op("scalar", lambda e: e.activation(out=st_b[:, b + 12:b + 14], in_=st_b[:, b + 10:b + 12],
                                                  func=AF.Sqrt, bias=eps_t[:, :]), [(s_sv, 1)])
            P.wait("vector", s_sv)
            P.op("vector", lambda e: e.reciprocal(out=st_b[:, b + 14:b + 16], in_=st_b[:, b + 12:b + 14]), [(s_vv, 1)])
            P.wait("vector", s_vv)

        for dst, src in ((g1r, g1), (b1r, b1), (bvr, bv)):
            P.op("sync", lambda e, _d=dst, _s=src: e.dma_start(
                out=_d[:, :], in_=_s.ap().unsqueeze(0).partition_broadcast(128)), [(s_dw, 16)])
        P.op("sync", lambda e: e.dma_start(
            out=ln1_b[:, 0:512], in_=bqv.ap().unsqueeze(0).partition_broadcast(128)), [(s_dw, 16)])
        P.wait("vector", s_dw)
        P.op("vector", lambda e: e.tensor_scalar(out=bqvr_s[:, :], in0=ln1_b[:, 0:512],
                                                 scalar1=SC_QV, scalar2=None, op0=ALU.mult), [(s_vv, 1)])

        c_wc = Sem(ctx, nc, "abwc")
        for wsrc, wdst_bf in ((wq, wq_bf), (wk, wk_bf), (wv, wv_bf), (wqv, wqv_bf),
                              (wf, wf_bf), (wr1, wr1_bf), (wr2, wr2_bf),
                              (wg, wg_bf), (wu, wu_bf), (wd, wd_bf)):
            P.op("gpsimd", lambda e, _s=wsrc, _d=wdst_bf: e.dma_start(
                out=_d.ap()[:, :], in_=_s.ap()[:, :]), [(c_wc, 16)])
        s_dwT = Sem(ctx, nc, "abdwT")
        for wi, (wbf, wdst, rows, cols_t) in enumerate(
                ((wq_bf, wqt_sb, D, D), (wk_bf, wkt_sb, D, D),
                 (wv_bf, wvt_sb, 512, 512), (wqv_bf, wqvt_sb, 512, 512))):
            P.wait("sync", c_wc, (wi + 1) * 16)
            for s in range(8):
                P.op("sync", lambda e, _w=wbf, _d=wdst, _s=s, _r=rows, _ct=cols_t:
                     e.dma_start_transpose(_d[:, _s * _ct:(_s + 1) * _ct],
                                           _w.ap()[:, _s * 128:(_s + 1) * 128]), [(s_dwT, 16)])

        def emit_xload(gi):
            sl = gi % 2
            if gi >= 32:
                return
            if gi >= 2:
                P.wait("scalar", s_vt, x_marks[gi - 2][0])
                P.wait("scalar", s_ts, x_marks[gi - 2][2])
            P.op("scalar", lambda e, _t=gi, _sl=sl: e.dma_start(
                out=xt_b[_sl][:, :], in_=_ap(x_in, _t * 128 * D, [[D, 128], [1, D]])),
                [(s_dx, 16)])

        x_marks = {}
        emit_xload(0)
        emit_xload(1)
        for n in range(8):
            for grp in range(4):
                gi = n * 4 + grp
                sl = gi % 2
                t0g = 4 * n + grp
                P.wait("scalar", s_dx, 16 * (gi + 1))
                P.wait("scalar", s_vv)
                P.wait("scalar", s_vt)
                P.op("scalar", lambda e, _sl=sl: e.activation(
                    out=ln1_b[:, :], in_=xt_b[_sl][:, :],
                    func=AF.Square, accum_out=st_b[:, 2:3]), [(s_sv, 1)])
                xsq_mark = s_sv.v
                P.wait("vector", s_dx, 16 * (gi + 1))
                P.op("vector", lambda e, _sl=sl: e.reduce_sum(
                    out=st_b[:, 0:1], in_=xt_b[_sl][:, :], axis=AX.X), [(s_vv, 1)])
                P.wait("vector", s_sv)
                P.wait("vector", s_vv)
                ln_math(0)
                P.wait("vector", s_dy, 16 * max(0, gi - 1))
                P.op("vector", lambda e, _sl=sl: e.tensor_scalar(
                    out=yb_ab[_sl][:, :], in0=xt_b[_sl][:, :],
                    scalar1=st_b[:, 4:5], scalar2=st_b[:, 14:15],
                    op0=ALU.subtract, op1=ALU.mult), [(s_vv, 1)])
                yw_mark = s_vv.v
                P.wait("scalar", s_vv)
                P.wait("scalar", s_ts)
                P.op("scalar", lambda e, _sl=sl: e.activation(
                    out=xt_b[_sl][:, :], in_=yb_ab[_sl][:, :],
                    func=AF.Square, accum_out=st_b[:, 18:19]), [(s_sv, 1)])
                P.op("scalar", lambda e, _t=t0g, _sl=sl: e.dma_start(
                    out=_ap(y_sp, _t * 128 * D, [[D, 128], [1, D]]), in_=yb_ab[_sl][:, :]),
                    [(s_dy, 16)])
                P.op("vector", lambda e, _sl=sl: e.reduce_sum(
                    out=st_b[:, 16:17], in_=yb_ab[_sl][:, :], axis=AX.X), [(s_vv, 1)])
                P.wait("vector", s_sv)
                P.wait("vector", s_vv)
                ln_math(16)
                P.wait("vector", s_ts)
                P.op("vector", lambda e, _sl=sl: e.tensor_scalar(
                    out=xt_b[_sl][:, :], in0=yb_ab[_sl][:, :],
                    scalar1=st_b[:, 20:21], scalar2=st_b[:, 30:31],
                    op0=ALU.subtract, op1=ALU.mult), [(s_vv, 1)])
                P.wait("vector", s_vv)
                P.op("vector", lambda e, _sl=sl: e.tensor_tensor(
                    out=ln1_b[:, :], in0=xt_b[_sl][:, :],
                    in1=g1r[:, :], op=ALU.mult), [(s_vv, 1)])
                P.wait("vector", s_vv)
                P.op("vector", lambda e, _sl=sl: e.tensor_tensor(
                    out=xn_v[_sl][:, 0:D], in0=ln1_b[:, :],
                    in1=b1r[:, :], op=ALU.add), [(s_vt, 1)])
                P.wait("tensor", s_vt)
                _, hsm = pclB.acquire(P, "tensor")
                half = (pclB.g - 1) % 2
                for k in range(8):
                    P.op("tensor", lambda e, _k=k, _h=half, _sl=sl: _transp(e,
                        pclb[:, _h * 1024 + _k * 128: _h * 1024 + (_k + 1) * 128],
                        xn_v[_sl][:, _k * 128:(_k + 1) * 128], id_bf[:, :], _k == 0), [(s_ts, 1)])
                P.wait("vector", s_ts)
                P.op("vector", lambda e, _tc=grp, _h=half: e.tensor_copy(
                    out=xnt_b[:, :].rearrange("p (s c) -> p s c", c=512)[:, :, _tc * 128:(_tc + 1) * 128],
                    in_=pclb[:, _h * 1024:(_h + 1) * 1024].rearrange("p (s c) -> p s c", c=128)),
                    [(hsm, 1)])
                pclB.release(hsm)
                x_marks[gi] = (s_vt.v, xsq_mark, s_ts.v)
                emit_xload(gi + 2)
            pclB.wait_all(P, "tensor")
            for is_k, wt_sb in ((False, wqt_sb), (True, wkt_sb)):
                if n == 0:
                    P.wait("tensor", s_dwT, (16 if is_k else 8) * 16)
                for dt in range(8):
                    bank, bsm = mmB.acquire(P, "tensor")
                    for s in range(8):
                        P.op("tensor", lambda e, _w=wt_sb, _s=s, _dt=dt, _b=bank: e.matmul(
                            _b[:, :], _w[:, _s * D + _dt * 128: _s * D + (_dt + 1) * 128],
                            xnt_b[:, _s * 512:(_s + 1) * 512], start=(_s == 0), stop=(_s == 7)),
                            [(s_ts, 1)] if s == 7 else ())
                    P.wait("scalar", s_ts)
                    if is_k:
                        P.op("scalar", lambda e, _dt=dt, _n=n, _b=bank: e.activation(
                            out=kt_sb[:, _dt * S + _n * 512: _dt * S + _n * 512 + 512],
                            in_=_b[:, :], func=AF.Identity, bias=bks[:, _dt:_dt + 1]), [(bsm, 1)])
                    else:
                        qj = n * 8 + dt
                        qsl = qj % 2
                        P.wait("scalar", s_dq, max(0, (qj - 1) * 16))
                        P.op("scalar", lambda e, _dt=dt, _b=bank, _sl=qsl: e.activation(
                            out=qk_st[_sl][:, :], in_=_b[:, :], func=AF.Identity,
                            scale=SC_Q, bias=bqs[:, _dt:_dt + 1]), [(bsm, 1)])
                        P.op("scalar", lambda e, _dt=dt, _n=n, _sl=qsl: e.dma_start(
                            out=_ap(qt_sp, _dt * 128 * S + _n * 512, [[S, 128], [1, 512]]),
                            in_=qk_st[_sl][:, :]), [(s_dq, 16)])
                    mmB.release(bsm)
            for is_qv, wt_sb in ((False, wvt_sb), (True, wqvt_sb)):
                if n == 0:
                    P.wait("tensor", s_dwT, (32 if is_qv else 24) * 16)
                for tc in range(4):
                    t_abs = 4 * n + tc
                    bank, bsm = mmB.acquire(P, "tensor")
                    for s in range(8):
                        P.op("tensor", lambda e, _w=wt_sb, _s=s, _tc=tc, _b=bank: e.matmul(
                            _b[:, :], xnt_b[:, _s * 512 + _tc * 128: _s * 512 + (_tc + 1) * 128],
                            _w[:, _s * 512:(_s + 1) * 512], start=(_s == 0), stop=(_s == 7)),
                            [(s_ts, 1)] if s == 7 else ())
                    P.wait("vector", s_ts)
                    if not is_qv:
                        P.op("vector", lambda e, _t=t_abs, _b=bank: e.tensor_tensor(
                            out=v_sb[:, _t * 512:(_t + 1) * 512], in0=_b[:, :], in1=bvr[:, :],
                            op=ALU.add), [(bsm, 1)])
                        mmB.release(bsm)
                    else:
                        P.op("vector", lambda e, _b=bank: e.scalar_tensor_tensor(
                            out=qv_b[:, :], in0=_b[:, :], scalar=SC_QV, in1=bqvr_s[:, :],
                            op0=ALU.mult, op1=ALU.add), [(bsm, 1)])
                        bcnt = mmB.release(bsm)
                        P.wait("vector", bsm, bcnt)
                        P.op("vector", lambda e: e.tensor_tensor(
                            out=cq_b[:, :], in0=qv_b[:, :], in1=wsqr[:, :], op=ALU.mult), [(s_vv, 1)])
                        P.wait("vector", s_vv)
                        P.op("vector", lambda e: e.reduce_sum(
                            out=cqt_b[:, :], in_=cq_b[:, :].rearrange("p (h q) -> p h q", q=32),
                            axis=AX.X), [(s_vv, 1)])
                        P.wait("vector", s_vv)
                        P.op("vector", lambda e, _t=t_abs: e.tensor_scalar(
                            out=cqv_sb[:, _t * H:(_t + 1) * H], in0=cqt_b[:, :],
                            scalar1=bss[:, 0:1], scalar2=None, op0=ALU.add), [(s_vv, 1)])

        P.wait("gpsimd", s_dy)
        P.wait("gpsimd", s_dq)
        P.wait("gpsimd", c_wc)
        if kt_tap is not None:
            s_tp, = sems("abtap")
            P.op("gpsimd", lambda e: e.dma_start(out=kt_tap.ap()[:, :], in_=kt_sb[:, :]), [(s_tp, 16)])
            P.op("gpsimd", lambda e: e.dma_start(out=v_tap.ap()[:, :], in_=v_sb[:, :]), [(s_tp, 16)])
            P.op("gpsimd", lambda e: e.dma_start(out=cqv_tap.ap()[:, :], in_=cqv_sb[:, :]), [(s_tp, 16)])
            P.op("gpsimd", lambda e: e.dma_start(out=wqt_tap.ap()[:, :], in_=wqt_sb[:, :]), [(s_tp, 16)])
            P.op("gpsimd", lambda e: e.dma_start(out=xnt_tap.ap()[:, :], in_=xnt_b[:, :]), [(s_tp, 16)])
            P.wait("gpsimd", s_tp)
        P.run()
        A.release()
        if stop_after == "AB":
            return nc

        A.mark()
        qbuf = [A.alloc(8 * 256, BF16) for _ in range(2)]
        ptb = A.alloc(16 * 1024, BF16)
        stg = [A.alloc(1024, BF16) for _ in range(2)]
        prod = A.alloc(1024, F32)
        mixst = A.alloc(2 * 2048, BF16)
        csc = A.alloc(64, F32)
        macc = [A.alloc(64, F32) for _ in range(2)]
        assert A.cur <= TAIL_OFF, f"C frame {A.cur} overlaps tail {TAIL_OFF}"

        P = Prog(nc)
        (c_st, c_exp, c_pv1, c_pv2, c_oev1, c_oev2, c_q, c_mix, c_mixd,
         c_vv, c_vs, c_sv) = sems("cst", "cexp", "cpv1", "cpv2", "coev1",
                                  "coev2", "cq", "cmix", "cmixd", "cvv", "cvs", "csv")

        NB = 2048
        LAG1, LAG2 = 2, 8

        for ic in range(16):
            if ic >= 2:
                P.wait("sync", c_st, (ic - 1) * 128)
            P.op("sync", lambda e, _ic=ic: e.dma_start(
                out=qbuf[_ic % 2][:, :],
                in_=_ap(qt_sp, _ic * 256, [[S, 128], [128 * S, 8], [1, 256]])), [(c_q, 16)])
        c_pw = Sem(ctx, nc, "cpw")
        P.wait("sync", c_wc, 6 * 16)
        for s in range(8):
            P.op("sync", lambda e, _s=s: e.dma_start_transpose(
                wr1t_sb[:, _s * DR:(_s + 1) * DR], wr1_bf.ap()[:, _s * 128:(_s + 1) * 128]), [(c_pw, 16)])

        def pv_batch(kb, head, flush=False):
            if kb < 0 or kb >= NB:
                return
            ub, bb = divmod(kb, 16)
            icb, pairb = divmod(ub, 8)
            oev = c_oev1 if head == 0 else c_oev2
            pvs = c_pv1 if head == 0 else c_pv2
            obk = (pb[0], pb[1]) if head == 0 else (pb[2], pb[3])
            dnb = pb[4] if head == 0 else pb[5]
            if bb == 0:
                P.wait("tensor", oev, ub)
            slot = kb % 16
            last_inst = [None]
            for jl in range(2):
                jt = 2 * bb + jl
                for isub in range(2):
                    lbase = slot * 1024 + head * 512 + jl * 256 + isub * 128
                    P.op("tensor", lambda e, _lb=lbase, _jt=jt, _o=obk[isub],
                         _st=(bb == 0 and jl == 0), _sp=(bb == 15 and jl == 1):
                         e.matmul(_o[:, :], ptb[:, _lb:_lb + 128], v_sb[:, _jt * 512:(_jt + 1) * 512],
                                  start=_st, stop=_sp))
                    incs = [(pvs, 1)] if (jl == 1 and isub == 1) else ()
                    P.op("tensor", lambda e, _lb=lbase, _d=dnb, _is=isub,
                         _st=(bb == 0 and jl == 0 and isub == 0),
                         _sp=(bb == 15 and jl == 1 and isub == 1):
                         _mm_noldw(e, _d[:, _is:_is + 1], ptb[:, _lb:_lb + 128], ones_bf[:, 0:1],
                                   _st, _sp), incs)

        pend_cexp = {}

        def emit_pending_cexp(kk):
            for fn in pend_cexp.pop(kk, []):
                fn()

        for k in range(NB + LAG2 + 4):
            emit_pending_cexp(k)
            if k == NB:
                P.wait("tensor", c_exp, NB)
            if k >= NB + LAG2:
                continue
            if k < NB:
                u, b = divmod(k, 16)
                ic, pair = divmod(u, 8)
                if b == 0 and pair == 0:
                    P.wait("tensor", c_q, 16 * (ic + 1))
                P.wait("tensor", c_exp, k)
                for jl in range(2):
                    jt = 2 * b + jl
                    for hh in range(2):
                        ccol = jl * 256 + hh * 512
                        incs = [(c_st, 1)] if (jl == 1 and hh == 1) else ()
                        P.op("tensor", lambda e, _p=pair, _jt=jt, _ic=ic, _hh=hh, _cc=ccol, _st=(jl == 0):
                             e.matmul(pcl[:, _cc:_cc + 256],
                                      kt_sb[_hh * 64:(_hh + 1) * 64, _p * S + _jt * 128: _p * S + (_jt + 1) * 128],
                                      qbuf[_ic % 2][_hh * 64:(_hh + 1) * 64, _p * 256:(_p + 1) * 256],
                                      start=_st, stop=True,
                                      tile_position=(_hh * 64, 0)), incs)
                P.wait("scalar", c_st, k + 1)
                if k >= 16:
                    P.wait("scalar", c_pv2, k - 15)
                P.op("scalar", lambda e, _sl=(k % 16): e.activation(
                    out=ptb[:, _sl * 1024:(_sl + 1) * 1024], in_=pcl[:, 0:1024], func=AF.Exp),
                    [(c_exp, 1)])
            pv_batch(k - LAG2, 1)
            pv_batch(k - LAG1, 0)
            for head, lag in ((0, LAG1), (1, LAG2)):
                kb = k - lag
                if kb < 0 or kb >= NB or kb % 16 != 15:
                    continue
                ub = kb // 16
                icb, pairb = divmod(ub, 8)
                pvs = c_pv1 if head == 0 else c_pv2
                oev = c_oev1 if head == 0 else c_oev2
                obk = (pb[0], pb[1]) if head == 0 else (pb[2], pb[3])
                dnb = pb[4] if head == 0 else pb[5]
                hg = pairb * 2 + head
                P.wait("vector", pvs, 16 * (ub + 1))
                if icb > 0 or pairb > 0 or head > 0:
                    if pairb == 0 and head == 0:
                        P.wait("vector", c_mixd, max(0, (icb - 1) * 16))
                for isub in range(2):
                    P.op("vector", lambda e, _h=head, _is=isub, _o=obk[isub]: e.tensor_copy(
                        out=stg[_h][:, _is * 512:(_is + 1) * 512], in_=_o[:, :]),
                        [] if isub == 0 else [(c_vv, 1)])
                P.op("vector", lambda e, _d=dnb: e.reciprocal(out=csc[:, 0:2], in_=_d[:, 0:2]), [(oev, 1)])
                P.wait("vector", c_vv)
                P.wait("vector", oev)
                P.op("vector", lambda e, _h=head: e.tensor_tensor(
                    out=prod[:, :].rearrange("p (s c) -> p s c", c=512),
                    in0=stg[_h][:, :].rearrange("p (s c) -> p s c", c=512),
                    in1=wsdr[:, :].unsqueeze(1).broadcast_to([128, 2, 512]),
                    op=ALU.mult), [(c_vv, 1)])
                P.wait("vector", c_vv)
                P.op("vector", lambda e: e.reduce_sum(
                    out=csc[:, 16:32].rearrange("p (s r) -> p s r", r=8),
                    in_=prod[:, :].rearrange("p (s r d) -> p s r d", r=8, d=64),
                    axis=AX.X), [(c_vv, 1)])
                P.wait("vector", c_vv)
                P.op("vector", lambda e: e.tensor_tensor(
                    out=csc[:, 16:32].rearrange("p (s r) -> p s r", r=8),
                    in0=csc[:, 16:32].rearrange("p (s r) -> p s r", r=8),
                    in1=csc[:, 0:2].unsqueeze(2).broadcast_to([128, 2, 8]),
                    op=ALU.mult), [(c_vv, 1)])
                P.wait("vector", c_vv)
                P.op("vector", lambda e, _ic=icb, _hg=hg: e.tensor_tensor(
                    out=csc[:, 16:32].rearrange("p (s r) -> p s r", r=8),
                    in0=csc[:, 16:32].rearrange("p (s r) -> p s r", r=8),
                    in1=cqv_sb[:, :].rearrange("p (t h) -> p t h", h=H)
                        [:, 2 * _ic:2 * _ic + 2, _hg:_hg + 1].broadcast_to([128, 2, 8]),
                    op=ALU.add), [(c_vs, 1)])
                vs_val = c_vs.v

                def defer(_vs=None):
                    _vsv = vs_val

                    def go():
                        P.wait("scalar", c_vs, _vsv)
                        P.op("scalar", lambda e: e.activation(out=csc[:, 32:48], in_=csc[:, 16:32],
                                                              func=AF.Exp), [(c_sv, 1)])
                    return go
                pend_cexp.setdefault(k + 2, []).append(defer())
                P.wait("vector", c_sv, c_sv.v + 1)
                c_sv.v += 0

                P.op("vector", lambda e: e.reduce_sum(
                    out=csc[:, 2:4], in_=csc[:, 32:48].rearrange("p (s r) -> p s r", r=8),
                    axis=AX.X), [(c_vv, 1)])
                P.wait("vector", c_vv)
                P.op("vector", lambda e: e.reciprocal(out=csc[:, 4:6], in_=csc[:, 2:4]), [(c_vv, 1)])
                P.wait("vector", c_vv)
                P.op("vector", lambda e: e.tensor_tensor(
                    out=csc[:, 48:64].rearrange("p (s r) -> p s r", r=8),
                    in0=csc[:, 32:48].rearrange("p (s r) -> p s r", r=8),
                    in1=csc[:, 4:6].unsqueeze(2).broadcast_to([128, 2, 8]),
                    op=ALU.mult), [(c_vv, 1)])
                P.wait("vector", c_vv)
                P.op("vector", lambda e: e.tensor_tensor(
                    out=csc[:, 48:64].rearrange("p (s r) -> p s r", r=8),
                    in0=csc[:, 48:64].rearrange("p (s r) -> p s r", r=8),
                    in1=csc[:, 0:2].unsqueeze(2).broadcast_to([128, 2, 8]),
                    op=ALU.mult), [(c_vv, 1)])
                P.wait("vector", c_vv)
                for isub in range(2):
                    for r in range(R):
                        src = lambda _h=head, _is=isub, _r=r: stg[_h][:, _is * 512 + _r * 64: _is * 512 + _r * 64 + 64]
                        wcol = lambda _is=isub, _r=r: csc[:, 48 + _is * 8 + _r: 49 + _is * 8 + _r]
                        if r == 0:
                            P.op("vector", lambda e, _s=src, _w=wcol, _is=isub: e.tensor_scalar(
                                out=macc[_is][:, :], in0=_s(), scalar1=_w(), scalar2=None,
                                op0=ALU.mult), [(c_vv, 1)])
                        elif r < R - 1:
                            P.wait("vector", c_vv)
                            P.op("vector", lambda e, _s=src, _w=wcol, _is=isub, _r=r: e.scalar_tensor_tensor(
                                out=macc[_is][:, :] if _r % 2 == 0 else macc[_is][:, :],
                                in0=_s(), scalar=_w(), in1=macc[_is][:, :],
                                op0=ALU.mult, op1=ALU.add), [(c_vv, 1)])
                        else:
                            P.wait("vector", c_vv)
                            dst_col = (icb % 2) * 2048 + isub * 1024 + hg * 64
                            if pairb == 7 and head == 1 and isub == 1:
                                incs = [(c_mix, 1)]
                            else:
                                incs = [(c_vv, 1)]
                            P.op("vector", lambda e, _s=src, _w=wcol, _is=isub, _dc=dst_col: e.scalar_tensor_tensor(
                                out=mixst[:, _dc:_dc + 64], in0=_s(), scalar=_w(), in1=macc[_is][:, :],
                                op0=ALU.mult, op1=ALU.add), incs)
        for ic in range(16):
            P.wait("gpsimd", c_mix, ic + 1)
            P.op("gpsimd", lambda e, _ic=ic: e.dma_start(
                out=_ap(mix_sp, _ic * 256 * D, [[D, 128], [128 * D, 2], [1, D]]),
                in_=mixst[:, (_ic % 2) * 2048:(_ic % 2) * 2048 + 2048].bitcast(BF16)), [(c_mixd, 16)])
        P.wait("gpsimd", c_mixd)
        P.run()
        A.release()
        if stop_after == "C":
            return nc

        A.release()
        A.release()
        A.mark()

        wft_sb = A.alloc(8 * D, BF16)
        wr2t_sb = A.alloc(16 * D, BF16)
        g2r = A.alloc(D, F32)
        b2r = A.alloc(D, F32)
        bfr = A.alloc(D, F32)
        br2r = A.alloc(D, F32)
        mixin = [A.alloc(2 * D, BF16) for _ in range(2)]
        yb = [A.alloc(2 * D, F32) for _ in range(2)]
        x0b = [A.alloc(2 * D, F32) for _ in range(2)]
        mixT = A.alloc(8 * 256, BF16)
        onT = A.alloc(8 * 256, BF16)
        r1T = A.alloc(16 * 256, BF16)
        sq_scr = r1T.bitcast(F32)
        o1b = [A.alloc(2 * D, F32) for _ in range(2)]
        ab = A.alloc(2 * D, F32)
        onb = A.alloc(2 * D, BF16)
        zb = [A.alloc(2 * D, BF16) for _ in range(2)]
        tmpd = A.alloc(2 * D, F32)
        ztst = ab.bitcast(BF16)[:, 0:2 * D]
        std = A.alloc(64, F32)
        assert A.cur <= TAIL_OFF, f"D frame {A.cur} overlaps tail {TAIL_OFF}"

        P = Prog(nc)
        (d_dm, d_dy, d_dx0, d_dx1, d_dzt, d_vv, d_sv, d_vs, d_vt, d_ts, d_dw2) = sems(
            "ddm", "ddy", "ddx0", "ddx1", "ddzt", "dvv", "dsv", "dvs", "dvt", "dts", "ddw2")
        pclB2 = Banks(ctx, nc, [0, 1], "dpcl")
        mmB2 = Banks(ctx, nc, pb[0:4], "dmm")

        d_dwf = Sem(ctx, nc, "ddwf")
        for s in range(8):
            P.op("sync", lambda e, _s=s: e.dma_start_transpose(
                wft_sb[:, _s * D:(_s + 1) * D], wf_bf.ap()[:, _s * 128:(_s + 1) * 128]), [(d_dwf, 16)])
        for dst, srcb in ((g2r, g2), (b2r, b2), (bfr, bf_in), (br2r, br2)):
            P.op("sync", lambda e, _d=dst, _s=srcb: e.dma_start(
                out=_d[:, :], in_=_s.ap().unsqueeze(0).partition_broadcast(128)), [(d_dw2, 16)])
        for s in range(16):
            P.op("sync", lambda e, _s=s: e.dma_start_transpose(
                wr2t_sb[:, _s * D:(_s + 1) * D], wr2_bf.ap()[:, _s * 128:(_s + 1) * 128]), [(d_dw2, 16)])

        def ln_math_d(base):
            b = base
            P.op("vector", lambda e: e.tensor_scalar(out=std[:, b + 4:b + 6], in0=std[:, b + 0:b + 2],
                                                     scalar1=1.0 / D, scalar2=None, op0=ALU.mult), [(d_vv, 1)])
            P.op("vector", lambda e: e.tensor_scalar(out=std[:, b + 6:b + 8], in0=std[:, b + 2:b + 4],
                                                     scalar1=1.0 / D, scalar2=None, op0=ALU.mult), [(d_vv, 1)])
            P.wait("vector", d_vv)
            P.op("vector", lambda e: e.tensor_tensor(out=std[:, b + 8:b + 10], in0=std[:, b + 4:b + 6],
                                                     in1=std[:, b + 4:b + 6], op=ALU.mult), [(d_vv, 1)])
            P.wait("vector", d_vv)
            P.op("vector", lambda e: e.tensor_tensor(out=std[:, b + 10:b + 12], in0=std[:, b + 6:b + 8],
                                                     in1=std[:, b + 8:b + 10], op=ALU.subtract), [(d_vs, 1)])
            P.wait("scalar", d_vs)
            P.op("scalar", lambda e: e.activation(out=std[:, b + 12:b + 14], in_=std[:, b + 10:b + 12],
                                                  func=AF.Sqrt, bias=eps_t[:, :]), [(d_sv, 1)])
            P.wait("vector", d_sv)
            P.op("vector", lambda e: e.reciprocal(out=std[:, b + 14:b + 16], in_=std[:, b + 12:b + 14]), [(d_vv, 1)])
            P.wait("vector", d_vv)

        def stats(src_fn, base, pe_mark=None):
            P.wait("scalar", d_ts, pe_mark)
            P.wait("scalar", d_vv)
            for i2 in range(2):
                P.op("scalar", lambda e, _i=i2: e.activation(
                    out=sq_scr[:, _i * D:(_i + 1) * D], in_=src_fn(_i), func=AF.Square,
                    accum_out=std[:, base + 2 + _i:base + 3 + _i]), [(d_sv, 1)])
                P.op("vector", lambda e, _i=i2: e.reduce_sum(
                    out=std[:, base + _i:base + 1 + _i], in_=src_fn(_i), axis=AX.X), [(d_vv, 1)])
            P.wait("vector", d_sv)
            P.wait("vector", d_vv)
            ln_math_d(base)

        pclviews = [pclb[:, 0:1024], pclb[:, 1024:2048]]
        zviews = [pb[4][:].bitcast(BF16), pb[5][:].bitcast(BF16)]
        zB = Banks(ctx, nc, [4, 5], "dzb")

        def transpose_in(src_bf16, dstT, pool=None, views=None):
            pool = pool or pclB2
            views = views or pclviews
            for i2 in range(2):
                _, hsm = pool.acquire(P, "tensor")
                vw = views[(pool.g - 1) % 2]
                for kk in range(8):
                    P.op("tensor", lambda e, _i=i2, _k=kk, _v=vw: _transp(e,
                        _v[:, _k * 128:(_k + 1) * 128],
                        src_bf16[:, _i * D + _k * 128: _i * D + (_k + 1) * 128], id_bf[:, :], _k == 0), [(d_ts, 1)])
                P.wait("vector", d_ts)
                P.op("vector", lambda e, _i=i2, _v=vw, _d=dstT: e.tensor_copy(
                    out=_d[:, :].rearrange("p (s c) -> p s c", c=256)[:, :, _i * 128:(_i + 1) * 128],
                    in_=_v[:, :].rearrange("p (s c) -> p s c", c=128)),
                    [(hsm, 1)])
                pool.release(hsm)

        d_marks = {}

        p1_mark = {}
        p2_mark = {}

        def emit_loads(c):
            sl = c % 2
            if c >= 2:
                P.wait("sync", d_ts, p1_mark[c - 2])
                P.wait("sync", d_vv, d_marks[c - 2][0])
            P.op("sync", lambda e, _t=2 * c, _sl=sl: e.dma_start(
                out=mixin[_sl][:, :], in_=_ap(mix_sp, _t * 128 * D, [[D, 128], [128 * D, 2], [1, D]])), [(d_dm, 16)])
            P.op("sync", lambda e, _t=2 * c, _sl=sl: e.dma_start(
                out=yb[_sl][:, :], in_=_ap(y_sp, _t * 128 * D, [[D, 128], [128 * D, 2], [1, D]])), [(d_dy, 16)])
            P.op("sync", lambda e, _t=2 * c, _sl=sl: e.dma_start(
                out=x0b[_sl][:, :], in_=_ap(x_in, _t * 128 * D, [[D, 128], [128 * D, 2], [1, D]])), [(d_dx0, 16)])

        def emit_P1_pe(c):
            P.wait("tensor", d_dm, (c + 1) * 16)
            transpose_in(mixin[c % 2][:, :], mixT)
            pclB2.wait_all(P, "tensor")
            P.wait("tensor", d_dwf, 8 * 16)
            P.wait("tensor", d_dw2, 4 * 16)
            P.wait("tensor", d_dy, (c + 1) * 16)
            for tc in range(2):
                for nc2 in range(2):
                    bank, bsm = mmB2.acquire(P, "tensor")
                    for s in range(8):
                        P.op("tensor", lambda e, _s=s, _tc=tc, _n2=nc2, _b=bank: e.matmul(
                            _b[:, :], mixT[:, _s * 256 + _tc * 128: _s * 256 + (_tc + 1) * 128],
                            wft_sb[:, _s * D + _n2 * 512: _s * D + (_n2 + 1) * 512],
                            start=(_s == 0), stop=(_s == 7)), [(d_ts, 1)] if s == 7 else ())
                    emit_P1_pe.banks.append((tc, nc2, bank, bsm, d_ts.v))
            p1_mark[c] = d_ts.v

        def emit_o1_v(c):
            sl = c % 2
            if c >= 2:
                P.wait("vector", d_dx1, (c - 1) * 16)
            for (tc, nc2, bank, bsm, mmk) in emit_P1_pe.banks:
                P.wait("vector", d_ts, mmk)
                P.op("vector", lambda e, _b=bank, _n2=nc2: e.scalar_tensor_tensor(
                    out=tmpd[:, _n2 * 512:(_n2 + 1) * 512], in0=_b[:, :], scalar=1.0,
                    in1=bfr[:, _n2 * 512:(_n2 + 1) * 512], op0=ALU.mult, op1=ALU.add),
                    [(bsm, 1)])
                bcnt = mmB2.release(bsm)
                P.wait("vector", bsm, bcnt)
                P.op("vector", lambda e, _sl=sl, _tc=tc, _n2=nc2: e.tensor_tensor(
                    out=o1b[_sl][:, _tc * D + _n2 * 512: _tc * D + (_n2 + 1) * 512],
                    in0=tmpd[:, _n2 * 512:(_n2 + 1) * 512],
                    in1=yb[_sl][:, _tc * D + _n2 * 512: _tc * D + (_n2 + 1) * 512],
                    op=ALU.add), [(d_vv, 1)])
            emit_P1_pe.banks = []

        def emit_LN_v(c):
            sl = c % 2
            stats(lambda i, _sl=sl: o1b[_sl][:, i * D:(i + 1) * D], 0,
                  pe_mark=p2_mark.get(c - 1))
            if c >= 1:
                P.wait("vector", d_ts, p2_mark[c - 1])
            for i2 in range(2):
                P.op("vector", lambda e, _i=i2: e.scalar_tensor_tensor(
                    out=std[:, 48 + _i:49 + _i], in0=std[:, 4 + _i:5 + _i], scalar=-1.0,
                    in1=std[:, 14 + _i:15 + _i], op0=ALU.mult, op1=ALU.mult), [(d_vs, 1)])
            P.wait("scalar", d_vv)
            P.wait("scalar", d_vt)
            P.wait("scalar", d_vs)
            for i2 in range(2):
                P.op("scalar", lambda e, _i=i2, _sl=sl: e.activation(
                    out=tmpd[:, _i * D:(_i + 1) * D], in_=o1b[_sl][:, _i * D:(_i + 1) * D],
                    func=AF.Identity, scale=std[:, 14 + _i:15 + _i],
                    bias=std[:, 48 + _i:49 + _i]), [(d_sv, 1)])
            P.wait("vector", d_sv)
            for i2 in range(2):
                P.op("vector", lambda e, _i=i2: e.tensor_tensor(
                    out=tmpd[:, _i * D:(_i + 1) * D], in0=tmpd[:, _i * D:(_i + 1) * D],
                    in1=g2r[:, :], op=ALU.mult), [(d_vv, 1)])
                P.wait("vector", d_vv)
                P.op("vector", lambda e, _i=i2: e.tensor_tensor(
                    out=onb[:, _i * D:(_i + 1) * D], in0=tmpd[:, _i * D:(_i + 1) * D],
                    in1=b2r[:, :], op=ALU.add), [(d_vt, 1)])

        def emit_P2_pe(c):
            P.wait("tensor", d_vt)
            transpose_in(onb[:, :], onT)
            pclB2.wait_all(P, "tensor")
            P.wait("tensor", c_pw, 8 * 16)
            for dt in range(16):
                bank, bsm = mmB2.acquire(P, "tensor")
                for s in range(8):
                    P.op("tensor", lambda e, _s=s, _dt=dt, _b=bank: e.matmul(
                        _b[:, 0:256], wr1t_sb[:, _s * DR + _dt * 128: _s * DR + (_dt + 1) * 128],
                        onT[:, _s * 256:(_s + 1) * 256], start=(_s == 0), stop=(_s == 7)),
                        [(d_ts, 1)] if s == 7 else ())
                P.wait("scalar", d_ts)
                P.op("scalar", lambda e, _dt=dt, _b=bank: e.activation(
                    out=r1T[:, _dt * 256:(_dt + 1) * 256], in_=_b[:, 0:256], func=AF.Relu,
                    bias=br1s[:, _dt:_dt + 1]), [(bsm, 1)])
                mmB2.release(bsm)
            mmB2.wait_all(P, "tensor")
            P.wait("tensor", d_dw2, (4 + 16) * 16)
            for tc in range(2):
                for nc2 in range(2):
                    bank, bsm = mmB2.acquire(P, "tensor")
                    for s in range(16):
                        P.op("tensor", lambda e, _s=s, _tc=tc, _n2=nc2, _b=bank: e.matmul(
                            _b[:, :], r1T[:, _s * 256 + _tc * 128: _s * 256 + (_tc + 1) * 128],
                            wr2t_sb[:, _s * D + _n2 * 512: _s * D + (_n2 + 1) * 512],
                            start=(_s == 0), stop=(_s == 15)), [(d_ts, 1)] if s == 15 else ())
                    emit_P2_pe.banks.append((tc, nc2, bank, bsm, d_ts.v))
            p2_mark[c] = d_ts.v

        def emit_a_v(c):
            sl = c % 2
            P.wait("vector", d_dzt, max(0, c * 16))
            for (tc, nc2, bank, bsm, mmk) in emit_P2_pe.banks:
                P.wait("vector", d_ts, mmk)
                P.op("vector", lambda e, _b=bank, _n2=nc2: e.scalar_tensor_tensor(
                    out=tmpd[:, _n2 * 512:(_n2 + 1) * 512], in0=_b[:, :], scalar=1.0,
                    in1=br2r[:, _n2 * 512:(_n2 + 1) * 512], op0=ALU.mult, op1=ALU.add),
                    [(bsm, 1)])
                bcnt = mmB2.release(bsm)
                P.wait("vector", bsm, bcnt)
                P.op("vector", lambda e, _sl=sl, _tc=tc, _n2=nc2: e.tensor_tensor(
                    out=ab[:, _tc * D + _n2 * 512: _tc * D + (_n2 + 1) * 512],
                    in0=tmpd[:, _n2 * 512:(_n2 + 1) * 512],
                    in1=o1b[_sl][:, _tc * D + _n2 * 512: _tc * D + (_n2 + 1) * 512],
                    op=ALU.add), [(d_vv, 1)])
            emit_P2_pe.banks = []

        def emit_chains_v(c):
            sl = c % 2
            stats(lambda i: ab[:, i * D:(i + 1) * D], 16)
            for i2 in range(2):
                P.op("vector", lambda e, _i=i2: e.scalar_tensor_tensor(
                    out=std[:, 50 + _i:51 + _i], in0=std[:, 20 + _i:21 + _i], scalar=-1.0,
                    in1=std[:, 30 + _i:31 + _i], op0=ALU.mult, op1=ALU.mult), [(d_vs, 1)])
            P.wait("scalar", d_vv)
            P.wait("scalar", d_vt)
            P.wait("scalar", d_vs)
            for i2 in range(2):
                P.op("scalar", lambda e, _i=i2: e.activation(
                    out=tmpd[:, _i * D:(_i + 1) * D], in_=ab[:, _i * D:(_i + 1) * D],
                    func=AF.Identity, scale=std[:, 30 + _i:31 + _i],
                    bias=std[:, 50 + _i:51 + _i]), [(d_sv, 1)])
            P.wait("vector", d_dx0, (c + 1) * 16)
            P.wait("vector", d_sv)
            for i2 in range(2):
                P.op("vector", lambda e, _i=i2, _sl=sl: e.tensor_tensor(
                    out=o1b[_sl][:, _i * D:(_i + 1) * D], in0=tmpd[:, _i * D:(_i + 1) * D],
                    in1=x0b[_sl][:, _i * D:(_i + 1) * D], op=ALU.add), [(d_vv, 1)])
            P.wait("gpsimd", d_vv)
            P.op("gpsimd", lambda e, _t=2 * c, _sl=sl: e.dma_start(
                out=_ap(x1_sp, _t * 128 * D, [[D, 128], [128 * D, 2], [1, D]]), in_=o1b[_sl][:, :]), [(d_dx1, 16)])
            stats(lambda i, _sl=sl: o1b[_sl][:, i * D:(i + 1) * D], 32)
            for i2 in range(2):
                P.op("vector", lambda e, _i=i2: e.scalar_tensor_tensor(
                    out=std[:, 52 + _i:53 + _i], in0=std[:, 36 + _i:37 + _i], scalar=-1.0,
                    in1=std[:, 46 + _i:47 + _i], op0=ALU.mult, op1=ALU.mult), [(d_vs, 1)])
            P.wait("scalar", d_vv)
            P.wait("scalar", d_vs)
            P.wait("scalar", d_ts)
            for i2 in range(2):
                P.op("scalar", lambda e, _i=i2, _sl=sl: e.activation(
                    out=zb[_sl][:, _i * D:(_i + 1) * D], in_=o1b[_sl][:, _i * D:(_i + 1) * D],
                    func=AF.Identity, scale=std[:, 46 + _i:47 + _i],
                    bias=std[:, 52 + _i:53 + _i]), [(d_sv, 1)])
            d_marks[c] = (d_vv.v, d_ts.v)

        def emit_P3_pe(c):
            P.wait("tensor", d_sv)
            P.wait("tensor", d_dzt)
            transpose_in(zb[c % 2][:, :], ztst)
            pclB2.wait_all(P, "gpsimd")
            P.op("gpsimd", lambda e, _c=c: e.dma_start(
                out=_ap(zt_sp, _c * 256, [[S, 128], [128 * S, 8], [1, 256]]), in_=ztst[:, :]), [(d_dzt, 16)])

        emit_P1_pe.banks = []
        emit_P2_pe.banks = []
        emit_loads(0)
        emit_loads(1)
        if D_SEQ[0]:
            for c in range(16):
                emit_P1_pe(c)
                emit_o1_v(c)
                emit_LN_v(c)
                emit_P2_pe(c)
                emit_a_v(c)
                emit_chains_v(c)
                if 2 <= c + 2 < 16:
                    emit_loads(c + 2)
                emit_P3_pe(c)
        else:
            for c in range(18):
                if c < 16:
                    emit_P1_pe(c)
                if c >= 2:
                    emit_P3_pe(c - 2)
                if c < 16:
                    emit_o1_v(c)
                if 1 <= c <= 16:
                    emit_P2_pe(c - 1)
                if c < 16:
                    emit_LN_v(c)
                if 1 <= c <= 16:
                    emit_a_v(c - 1)
                    emit_chains_v(c - 1)
                if 2 <= c + 1 < 16:
                    emit_loads(c + 1)
        P.wait("gpsimd", d_dx1)
        P.wait("gpsimd", d_dzt)
        P.run()
        A.release()
        if stop_after == "D":
            return nc

        A.mark()
        wgt_sb = A.alloc(8 * DF, BF16)
        wut_sb = A.alloc(8 * DF, BF16)
        wdt_sb = A.alloc(24 * D, BF16)
        ztb = A.alloc(8 * 512, BF16)
        htb = A.alloc(24 * 512, BF16)
        silb = A.alloc(512, F32)
        x1b = A.alloc(D, F32)
        fb = A.alloc(D, F32)
        lnfb = A.alloc(D, F32)
        outb = A.alloc(D, F32)
        ste = A.alloc(32, F32)
        tmpe = A.alloc(D, F32)

        P = Prog(nc)
        (e_dz, e_dx1, e_do, e_vv, e_sv, e_vs, e_ts, e_dwg, e_dwu, e_dwd, e_hc) = sems(
            "edz", "edx1", "edo", "evv", "esv", "evs", "ets", "edwg", "edwu", "edwd", "ehc")
        mmB3 = Banks(ctx, nc, pb[0:4], "emm")

        P.op("sync", lambda e: e.dma_start(
            out=ztb[:, :], in_=_ap(zt_sp, 0, [[S, 128], [128 * S, 8], [1, 512]])), [(e_dz, 16)])
        for sb, wbf, wsem in ((wgt_sb, wg_bf, e_dwg), (wut_sb, wu_bf, e_dwu)):
            for s in range(8):
                P.op("sync", lambda e, _w=wbf, _sb=sb, _s=s: e.dma_start_transpose(
                    _sb[:, _s * DF:(_s + 1) * DF], _w.ap()[:, _s * 128:(_s + 1) * 128]), [(wsem, 16)])
        for cs in range(24):
            P.op("sync", lambda e, _cs=cs: e.dma_start_transpose(
                wdt_sb[:, _cs * D:(_cs + 1) * D], wd_bf.ap()[:, _cs * 128:(_cs + 1) * 128]), [(e_dwd, 16)])

        def ln_math_e():
            P.op("vector", lambda e: e.tensor_scalar(out=ste[:, 4:5], in0=ste[:, 0:1],
                                                     scalar1=1.0 / D, scalar2=None, op0=ALU.mult), [(e_vv, 1)])
            P.op("vector", lambda e: e.tensor_scalar(out=ste[:, 6:7], in0=ste[:, 2:3],
                                                     scalar1=1.0 / D, scalar2=None, op0=ALU.mult), [(e_vv, 1)])
            P.wait("vector", e_vv)
            P.op("vector", lambda e: e.tensor_tensor(out=ste[:, 8:9], in0=ste[:, 4:5],
                                                     in1=ste[:, 4:5], op=ALU.mult), [(e_vv, 1)])
            P.wait("vector", e_vv)
            P.op("vector", lambda e: e.tensor_tensor(out=ste[:, 10:11], in0=ste[:, 6:7],
                                                     in1=ste[:, 8:9], op=ALU.subtract), [(e_vs, 1)])
            P.wait("scalar", e_vs)
            P.op("scalar", lambda e: e.activation(out=ste[:, 12:13], in_=ste[:, 10:11],
                                                  func=AF.Sqrt, bias=eps_t[:, :]), [(e_sv, 1)])
            P.wait("vector", e_sv)
            P.op("vector", lambda e: e.reciprocal(out=ste[:, 14:15], in_=ste[:, 12:13]), [(e_vv, 1)])
            P.wait("vector", e_vv)

        e_marks = {}
        for n in range(8):
            if n >= 1:
                P.wait("sync", e_ts, e_marks[n - 1])
                P.op("sync", lambda e, _n=n: e.dma_start(
                    out=ztb[:, :], in_=_ap(zt_sp, _n * 512, [[S, 128], [128 * S, 8], [1, 512]])), [(e_dz, 16)])
            P.wait("tensor", e_dz, 16 * (n + 1))
            if n == 0:
                P.wait("tensor", e_dwg, 8 * 16)
            for dt in range(24):
                bank_g, bsm_g = mmB3.acquire(P, "tensor")
                for s in range(8):
                    P.op("tensor", lambda e, _s=s, _dt=dt, _b=bank_g: e.matmul(
                        _b[:, :], wgt_sb[:, _s * DF + _dt * 128: _s * DF + (_dt + 1) * 128],
                        ztb[:, _s * 512:(_s + 1) * 512], start=(_s == 0), stop=(_s == 7)),
                        [(e_ts, 1)] if s == 7 else ())
                if n == 0 and dt == 0:
                    P.wait("tensor", e_dwu, 8 * 16)
                bank_u, bsm_u = mmB3.acquire(P, "tensor")
                for s in range(8):
                    P.op("tensor", lambda e, _s=s, _dt=dt, _b=bank_u: e.matmul(
                        _b[:, :], wut_sb[:, _s * DF + _dt * 128: _s * DF + (_dt + 1) * 128],
                        ztb[:, _s * 512:(_s + 1) * 512], start=(_s == 0), stop=(_s == 7)),
                        [(e_ts, 1)] if s == 7 else ())
                P.wait("scalar", e_ts)
                P.op("scalar", lambda e, _b=bank_g: e.activation(
                    out=silb[:, :], in_=_b[:, :], func=AF.Silu, scale=SC_FFN), [(bsm_g, 1)])
                gcnt = mmB3.release(bsm_g)
                P.wait("vector", bsm_g, gcnt)
                P.wait("vector", e_hc, n)
                P.op("vector", lambda e, _b=bank_u, _dt=dt: e.scalar_tensor_tensor(
                    out=htb[:, _dt * 512:(_dt + 1) * 512], in0=_b[:, :], scalar=SC_FFN,
                    in1=silb[:, :], op0=ALU.mult, op1=ALU.mult), [(bsm_u, 1)])
                mmB3.release(bsm_u)
            e_marks[n] = e_ts.v
            mmB3.wait_all(P, "tensor")
            if n == 0:
                P.wait("tensor", e_dwd, 24 * 16)
            for tc in range(4):
                t_abs = 4 * n + tc
                P.wait("sync", e_do, max(0, t_abs * 16))
                P.op("sync", lambda e, _t=t_abs: e.dma_start(
                    out=x1b[:, :], in_=_ap(x1_sp, _t * 128 * D, [[D, 128], [1, D]])), [(e_dx1, 16)])
                for nc2 in range(2):
                    bank, bsm = mmB3.acquire(P, "tensor")
                    last_of_chunk = (tc == 3 and nc2 == 1)
                    for s in range(24):
                        if s < 23:
                            incs = []
                        else:
                            incs = [(e_hc, 1)] if last_of_chunk else [(e_ts, 1)]
                        P.op("tensor", lambda e, _s=s, _tc=tc, _n2=nc2, _b=bank: e.matmul(
                            _b[:, :], htb[:, _s * 512 + _tc * 128: _s * 512 + (_tc + 1) * 128],
                            wdt_sb[:, _s * D + _n2 * 512: _s * D + (_n2 + 1) * 512],
                            start=(_s == 0), stop=(_s == 23)), incs)
                    if last_of_chunk:
                        P.wait("vector", e_hc, n + 1)
                    else:
                        P.wait("vector", e_ts)
                    P.op("vector", lambda e, _b=bank, _n2=nc2: e.tensor_scalar(
                        out=fb[:, _n2 * 512:(_n2 + 1) * 512], in0=_b[:, :],
                        scalar1=SC_FFN, scalar2=None, op0=ALU.mult), [(bsm, 1)])
                    mmB3.release(bsm)
                mmB3.wait_all(P, "scalar")
                P.op("scalar", lambda e: e.activation(
                    out=tmpe[:, :], in_=fb[:, :], func=AF.Square, accum_out=ste[:, 2:3]), [(e_sv, 1)])
                P.op("vector", lambda e: e.reduce_sum(out=ste[:, 0:1], in_=fb[:, :], axis=AX.X), [(e_vv, 1)])
                P.wait("vector", e_sv)
                P.wait("vector", e_vv)
                ln_math_e()
                P.wait("vector", e_dx1)
                P.op("vector", lambda e: e.tensor_scalar(
                    out=tmpe[:, :], in0=fb[:, :], scalar1=ste[:, 4:5], scalar2=ste[:, 14:15],
                    op0=ALU.subtract, op1=ALU.mult), [(e_vv, 1)])
                P.wait("vector", e_vv)
                P.op("vector", lambda e: e.tensor_tensor(
                    out=outb[:, :], in0=tmpe[:, :], in1=x1b[:, :], op=ALU.add), [(e_vv, 1)])
                P.wait("gpsimd", e_vv)
                P.op("gpsimd", lambda e, _t=t_abs: e.dma_start(
                    out=_ap(out, _t * 128 * D, [[D, 128], [1, D]]), in_=outb[:, :]), [(e_do, 16)])
        if ht_tap is not None:
            s_tp2, = sems("etap")
            P.wait("gpsimd", e_vv)
            P.op("gpsimd", lambda e: e.dma_start(out=ht_tap.ap()[:, :], in_=htb[:, :]), [(s_tp2, 16)])
            P.op("gpsimd", lambda e: e.dma_start(out=zb_tap.ap()[:, :], in_=ztb[:, :]), [(s_tp2, 16)])
            P.op("gpsimd", lambda e: e.dma_start(out=wdt_tap.ap()[:, :], in_=wdt_sb[:, :]), [(s_tp2, 16)])
            P.op("gpsimd", lambda e: e.dma_start(out=fb_tap.ap()[:, :], in_=fb[:, :]), [(s_tp2, 16)])
            P.wait("gpsimd", s_tp2)
        P.wait("gpsimd", e_do)
        P.run()
        A.release()

        assert A.peak <= A.nbytes

    return nc


_NC_CACHE = {}


def _get_program(taps=()):
    key = tuple(sorted(taps))
    if key not in _NC_CACHE:
        _NC_CACHE[key] = build_program(taps)
    return _NC_CACHE[key]


def kernel(L0_attn_x, L0_attn_Wq, L0_attn_bq, L0_attn_Wk, L0_attn_bk,
           L0_attn_Wv, L0_attn_bv, L0_attn_Wqv, L0_attn_bqv, L0_attn_Ws,
           L0_attn_bs, L0_attn_Wf, L0_attn_bf, L0_attn_Wr1, L0_attn_br1,
           L0_attn_Wr2, L0_attn_br2, L0_attn_g1, L0_attn_b1, L0_attn_g2,
           L0_attn_b2, L0_ffn_w_gate, L0_ffn_w_up, L0_ffn_w_down):
    x = np.ascontiguousarray(np.asarray(L0_attn_x, dtype=np.float32))
    weights = {
        "Wq": L0_attn_Wq, "bq": L0_attn_bq, "Wk": L0_attn_Wk, "bk": L0_attn_bk,
        "Wv": L0_attn_Wv, "bv": L0_attn_bv, "Wqv": L0_attn_Wqv, "bqv": L0_attn_bqv,
        "Ws": L0_attn_Ws, "bs": L0_attn_bs, "Wf": L0_attn_Wf, "bf": L0_attn_bf,
        "Wr1": L0_attn_Wr1, "br1": L0_attn_br1, "Wr2": L0_attn_Wr2, "br2": L0_attn_br2,
        "g1": L0_attn_g1, "b1": L0_attn_b1, "g2": L0_attn_g2, "b2": L0_attn_b2,
        "wg": L0_ffn_w_gate, "wu": L0_ffn_w_up, "wd": L0_ffn_w_down,
    }
    weights = {k: np.ascontiguousarray(np.asarray(v)) for k, v in weights.items()}
    nc = _get_program()
    in_maps = [dict(weights, x=np.ascontiguousarray(x[c])) for c in range(8)]
    res = run_bass_kernel_spmd(nc, in_maps, core_ids=list(range(8)))
    return np.stack([np.asarray(r["out"], dtype=np.float32) for r in res.results], axis=0)


# version 07
# speedup vs baseline: 1.0400x; 1.0303x over previous
import contextlib

import numpy as np

import concourse.bass as bass
import concourse.mybir as mybir
from concourse.bass_utils import run_bass_kernel_spmd

F32 = mybir.dt.float32
BF16 = mybir.dt.bfloat16
I32 = mybir.dt.int32
U8 = mybir.dt.uint8
AF = mybir.ActivationFunctionType
ALU = mybir.AluOpType
AX = mybir.AxisListType

S = 4096
D = 1024
H = 16
R = 8
QKD = 32
DR = 2048
DF = 3072
EPS = 1e-5
NT = S // 128
SC_Q = 0.125
SC_QV = float(1.0 / np.float32(np.sqrt(np.float32(QKD))))
SC_FFN = float(np.float32(1.0 / 127.0))

_MMC = [0]
D_SEQ = [False]
ENGS = ("sync", "tensor", "vector", "scalar", "gpsimd")


def _transp(t, out, in_, ident, start):
    return t.matmul(out, in_, ident, start=start, stop=True, is_transpose=True)


def _mm_noldw(t, out, lhsT, rhs, start, stop, tile_position=None):
    inst = mybir.InstMatmult(
        name=f"mmnl_{_MMC[0]}", engine=mybir.EngineType.PE,
        ins=(t.lower_ap(rhs), t.lower_ap(lhsT)), outs=(t.lower_ap(out),),
        replication_resolution=0, replication_shift_amnt=0, replication_num_rows=0,
        start_tensor_calc=start, stop_tensor_calc=stop, ldweights=False,
        tile_position=tile_position)
    _MMC[0] += 1
    return t.add_instruction(inst)


class Sem:
    def __init__(self, ctx, nc, name):
        self.h = ctx.enter_context(nc.semaphore(name))
        self.v = 0


class Prog:

    def __init__(self, nc):
        self.nc = nc
        self.q = {e: [] for e in ENGS}

    def wait(self, eng, sm, val=None):
        v = sm.v if val is None else val
        if v > 0:
            self.q[eng].append(lambda e, _h=sm.h, _v=v: e.wait_ge(_h, _v))

    def op(self, eng, fn, incs=()):
        assert len(incs) <= 1, "hardware: one semaphore update per instruction"
        cap = []
        for sm, n in incs:
            sm.v += n
            cap.append((sm.h, n))

        def run(e, _fn=fn, _cap=cap):
            inst = _fn(e)
            for h, n in _cap:
                inst.then_inc(h, n)
        self.q[eng].append(run)

    def run(self):
        with self.nc.Block() as blk:
            for name in ENGS:
                lst = self.q[name]
                if lst:
                    getattr(blk, name)(lambda e, _l=lst: [f(e) for f in _l])


class Banks:

    def __init__(self, ctx, nc, banks, tag):
        self.banks = banks
        self.sems = [Sem(ctx, nc, f"{tag}{i}") for i in range(len(banks))]
        self.used = [0] * len(banks)
        self.g = 0

    def acquire(self, P, eng="tensor"):
        i = self.g % len(self.banks)
        self.g += 1
        P.wait(eng, self.sems[i], self.used[i])
        return self.banks[i], self.sems[i]

    def release(self, sm):
        i = self.sems.index(sm)
        self.used[i] += 1
        return self.used[i]

    def wait_all(self, P, eng):
        for i in range(len(self.sems)):
            P.wait(eng, self.sems[i], self.used[i])


class Arena:
    def __init__(self, ctx, nc, nbytes):
        self.t = ctx.enter_context(nc.sbuf_tensor("arena", [128, nbytes], U8))
        self.nbytes = nbytes
        self.cur = 0
        self.peak = 0
        self.marks = []

    def alloc(self, cols, dtype, parts=128, align=64):
        isz = mybir.dt.size(dtype)
        off = (self.cur + align - 1) // align * align
        self.cur = off + cols * isz
        self.peak = max(self.peak, self.cur)
        assert self.cur <= self.nbytes, f"SBUF arena overflow {self.cur}"
        return self.t.ap().bitcast(dtype)[0:parts, off // isz: off // isz + cols]

    def mark(self):
        self.marks.append(self.cur)

    def release(self):
        self.cur = self.marks.pop()


def _ap(t, offset, dims):
    return bass.AP(t, offset, dims)


def build_program(taps=(), stop_after=None):
    tapset = set(taps)
    nc = bass.Bass("TRN2", target_bir_lowering=False)

    def dram(name, shape, dtype, kind=None):
        if kind is None:
            return nc.dram_tensor(name, shape, dtype)
        return nc.dram_tensor(name, shape, dtype, kind=kind)

    x_in = dram("x", [S, D], F32, "ExternalInput")
    wq = dram("Wq", [D, D], F32, "ExternalInput")
    bq = dram("bq", [D], F32, "ExternalInput")
    wk = dram("Wk", [D, D], F32, "ExternalInput")
    bk = dram("bk", [D], F32, "ExternalInput")
    wv = dram("Wv", [512, D], F32, "ExternalInput")
    bv = dram("bv", [512], F32, "ExternalInput")
    wqv = dram("Wqv", [512, D], F32, "ExternalInput")
    bqv = dram("bqv", [512], F32, "ExternalInput")
    ws = dram("Ws", [1, 96], F32, "ExternalInput")
    bs_in = dram("bs", [1], F32, "ExternalInput")
    wf = dram("Wf", [D, D], F32, "ExternalInput")
    bf_in = dram("bf", [D], F32, "ExternalInput")
    wr1 = dram("Wr1", [DR, D], F32, "ExternalInput")
    br1 = dram("br1", [DR], F32, "ExternalInput")
    wr2 = dram("Wr2", [D, DR], F32, "ExternalInput")
    br2 = dram("br2", [D], F32, "ExternalInput")
    g1 = dram("g1", [D], F32, "ExternalInput")
    b1 = dram("b1", [D], F32, "ExternalInput")
    g2 = dram("g2", [D], F32, "ExternalInput")
    b2 = dram("b2", [D], F32, "ExternalInput")
    wg = dram("wg", [DF, D], I32, "ExternalInput")
    wu = dram("wu", [DF, D], I32, "ExternalInput")
    wd = dram("wd", [D, DF], I32, "ExternalInput")
    out = dram("out", [S, D], F32, "ExternalOutput")

    def scr(name, shape, dtype, tag):
        return dram(name, shape, dtype, "ExternalOutput" if tag in tapset else None)

    y_sp = scr("y_sp", [S, D], F32, "A")
    qt_sp = scr("qt_sp", [8, 128, S], BF16, "B")
    mix_sp = scr("mix_sp", [S, D], BF16, "C")
    x1_sp = scr("x1_sp", [S, D], F32, "D")
    zt_sp = scr("zt_sp", [8, 128, S], BF16, "D")
    wq_bf = dram("wq_bf", [D, D], BF16)
    wk_bf = dram("wk_bf", [D, D], BF16)
    wv_bf = dram("wv_bf", [512, D], BF16)
    wqv_bf = dram("wqv_bf", [512, D], BF16)
    wf_bf = dram("wf_bf", [D, D], BF16)
    wr1_bf = dram("wr1_bf", [DR, D], BF16)
    wr2_bf = dram("wr2_bf", [D, DR], BF16)
    wg_bf = dram("wg_bf", [DF, D], BF16)
    wu_bf = dram("wu_bf", [DF, D], BF16)
    wd_bf = dram("wd_bf", [D, DF], BF16)
    kt_tap = scr("kt_tap", [128, 8 * S], BF16, "B") if "B" in tapset else None
    v_tap = scr("v_tap", [128, NT * 512], BF16, "B") if "B" in tapset else None
    cqv_tap = scr("cqv_tap", [128, NT * H], F32, "B") if "B" in tapset else None
    wqt_tap = scr("wqt_tap", [128, 8 * D], BF16, "B") if "B" in tapset else None
    ht_tap = scr("ht_tap", [128, 24 * 512], BF16, "E") if "E" in tapset else None
    zb_tap = scr("zb_tap", [128, 8 * 512], BF16, "E") if "E" in tapset else None
    wdt_tap = scr("wdt_tap", [128, 24 * D], BF16, "E") if "E" in tapset else None
    fb_tap = scr("fb_tap", [128, D], F32, "E") if "E" in tapset else None
    xnt_tap = scr("xnt_tap", [128, 8 * 512], BF16, "B") if "B" in tapset else None

    ctx = contextlib.ExitStack()
    with ctx:
        A = Arena(ctx, nc, 206208)

        id_bf = A.alloc(128, BF16)
        id_f32 = A.alloc(128, F32)
        ones_bf = A.alloc(8, BF16)
        wsdr = A.alloc(512, BF16)
        wsqr = A.alloc(512, BF16)
        bss = A.alloc(1, F32)
        bqs = A.alloc(8, F32)
        bks = A.alloc(8, F32)
        br1s = A.alloc(16, F32)
        eps_t = A.alloc(1, F32)
        A.mark()
        kt_sb = A.alloc(8 * S, BF16)
        v_sb = A.alloc(NT * 512, BF16)
        cqv_sb = A.alloc(NT * H, F32)
        A.mark()

        TAIL_OFF = A.nbytes - 32 * 1024
        assert TAIL_OFF % 64 == 0

        def tail_ap(off_bytes, cols, dtype):
            isz = mybir.dt.size(dtype)
            return A.t.ap().bitcast(dtype)[0:128, off_bytes // isz: off_bytes // isz + cols]

        wr1t_sb = tail_ap(TAIL_OFF, 8 * DR, BF16)

        pcl = ctx.enter_context(nc.psum_tensor("pcl", [128, 1024], F32))
        pclb = pcl[:].bitcast(BF16)
        pb = [ctx.enter_context(nc.psum_tensor(f"pb{i}", [128, 512], F32)) for i in range(6)]

        def sems(*names):
            return [Sem(ctx, nc, n) for n in names]

        A.mark()
        iota_j = A.alloc(128, F32)
        iota_p = A.alloc(128, F32)
        btmp = A.alloc(128, F32)
        P = Prog(nc)
        d0, g0, t0, v0 = sems("p0d", "p0g", "p0t", "p0v")
        P.op("gpsimd", lambda e: e.dma_start(out=wsdr[:, :], in_=_ap(ws, 32, [[0, 128], [0, 8], [1, 64]])), [(d0, 16)])
        P.op("gpsimd", lambda e: e.dma_start(out=wsqr[:, :], in_=_ap(ws, 0, [[0, 128], [0, 16], [1, 32]])), [(d0, 16)])
        P.op("sync", lambda e: e.dma_start(out=bss[:, :], in_=_ap(bs_in, 0, [[0, 128], [1, 1]])), [(d0, 16)])
        P.op("sync", lambda e: e.dma_start(out=btmp[0:8, :], in_=_ap(bq, 0, [[128, 8], [1, 128]])), [(d0, 16)])
        P.op("sync", lambda e: e.dma_start(out=btmp[8:16, :], in_=_ap(bk, 0, [[128, 8], [1, 128]])), [(d0, 16)])
        P.op("sync", lambda e: e.dma_start(out=btmp[16:32, :], in_=_ap(br1, 0, [[128, 16], [1, 128]])), [(d0, 16)])
        P.op("gpsimd", lambda e: e.iota(iota_j[:, :], [[1, 128]], channel_multiplier=0,
                                        allow_small_or_imprecise_dtypes=True))
        P.op("gpsimd", lambda e: e.iota(iota_p[:, :], [[0, 128]], channel_multiplier=1,
                                        allow_small_or_imprecise_dtypes=True), [(g0, 1)])
        P.op("gpsimd", lambda e: e.memset(ones_bf[:, :], 1.0), [(g0, 1)])
        P.op("gpsimd", lambda e: e.memset(eps_t[:, :], EPS), [(g0, 1)])
        P.wait("vector", g0)
        P.op("vector", lambda e: e.tensor_tensor(out=id_f32[:, :], in0=iota_j[:, :], in1=iota_p[:, :], op=ALU.is_equal))
        P.op("vector", lambda e: e.tensor_tensor(out=id_bf[:, :], in0=iota_j[:, :], in1=iota_p[:, :], op=ALU.is_equal), [(v0, 1)])
        P.wait("tensor", d0)
        P.wait("tensor", v0)
        P.op("tensor", lambda e: e.transpose(pcl[:, 0:128], btmp[:, :], id_f32[:, :]), [(t0, 1)])
        P.wait("vector", t0)
        P.op("vector", lambda e: e.tensor_scalar(out=bqs[:, :], in0=pcl[:, 0:8], scalar1=SC_Q, scalar2=None, op0=ALU.mult))
        P.op("vector", lambda e: e.tensor_copy(out=bks[:, :], in_=pcl[:, 8:16]))
        P.op("vector", lambda e: e.tensor_copy(out=br1s[:, :], in_=pcl[:, 16:32]))
        P.run()
        A.release()

        A.mark()
        wqt_sb = A.alloc(8 * D, BF16)
        wkt_sb = A.alloc(8 * D, BF16)
        wvt_sb = A.alloc(8 * 512, BF16)
        wqvt_sb = A.alloc(8 * 512, BF16)
        g1r = A.alloc(D, F32)
        b1r = A.alloc(D, F32)
        bvr = A.alloc(512, F32)
        bqvr_s = A.alloc(512, F32)
        xt_b = [A.alloc(D, F32) for _ in range(2)]
        xn_v = [b.bitcast(BF16) for b in xt_b]
        ln1_b = A.alloc(D, F32)
        yb_ab = [A.alloc(D, F32) for _ in range(2)]
        st_b = A.alloc(64, F32)
        xnt_b = A.alloc(8 * 512, BF16)
        qk_st = [A.alloc(512, BF16) for _ in range(2)]
        qv_b = A.alloc(512, BF16)
        cq_b = A.alloc(512, F32)
        cqt_b = A.alloc(16, F32)

        P = Prog(nc)
        (s_dx, s_dw, s_dy, s_dq, s_dwp, s_vv, s_sv, s_vs, s_vt, s_ts, s_st,
         s_xc, s_xcs, s_wpb) = sems(
            "abdx", "abdw", "abdy", "abdq", "abdwp", "abvv", "absv", "abvs",
            "abvt", "abts", "abst", "abxc", "abxcs", "abwpb")
        pclB = Banks(ctx, nc, [0, 1], "abpcl")
        mmB = Banks(ctx, nc, pb[0:4], "abmm")

        def ln_math(base):
            b = base
            P.op("vector", lambda e: e.tensor_scalar(out=st_b[:, b + 4:b + 6], in0=st_b[:, b + 0:b + 2],
                                                     scalar1=1.0 / D, scalar2=None, op0=ALU.mult), [(s_vv, 1)])
            P.op("vector", lambda e: e.tensor_scalar(out=st_b[:, b + 6:b + 8], in0=st_b[:, b + 2:b + 4],
                                                     scalar1=1.0 / D, scalar2=None, op0=ALU.mult), [(s_vv, 1)])
            P.wait("vector", s_vv)
            P.op("vector", lambda e: e.tensor_tensor(out=st_b[:, b + 8:b + 10], in0=st_b[:, b + 4:b + 6],
                                                     in1=st_b[:, b + 4:b + 6], op=ALU.mult), [(s_vv, 1)])
            P.wait("vector", s_vv)
            P.op("vector", lambda e: e.tensor_tensor(out=st_b[:, b + 10:b + 12], in0=st_b[:, b + 6:b + 8],
                                                     in1=st_b[:, b + 8:b + 10], op=ALU.subtract), [(s_vs, 1)])
            P.wait("scalar", s_vs)
            P.op("scalar", lambda e: e.activation(out=st_b[:, b + 12:b + 14], in_=st_b[:, b + 10:b + 12],
                                                  func=AF.Sqrt, bias=eps_t[:, :]), [(s_sv, 1)])
            P.wait("vector", s_sv)
            P.op("vector", lambda e: e.reciprocal(out=st_b[:, b + 14:b + 16], in_=st_b[:, b + 12:b + 14]), [(s_vv, 1)])
            P.wait("vector", s_vv)

        for dst, src in ((g1r, g1), (b1r, b1), (bvr, bv)):
            P.op("sync", lambda e, _d=dst, _s=src: e.dma_start(
                out=_d[:, :], in_=_s.ap().unsqueeze(0).partition_broadcast(128)), [(s_dw, 16)])
        P.op("sync", lambda e: e.dma_start(
            out=ln1_b[:, 0:512], in_=bqv.ap().unsqueeze(0).partition_broadcast(128)), [(s_dw, 16)])
        P.wait("vector", s_dw)
        P.op("vector", lambda e: e.tensor_scalar(out=bqvr_s[:, :], in0=ln1_b[:, 0:512],
                                                 scalar1=SC_QV, scalar2=None, op0=ALU.mult), [(s_vv, 1)])

        c_wc = Sem(ctx, nc, "abwc")
        for wsrc, wdst_bf in ((wq, wq_bf), (wk, wk_bf), (wv, wv_bf), (wqv, wqv_bf),
                              (wf, wf_bf), (wr1, wr1_bf), (wr2, wr2_bf),
                              (wg, wg_bf), (wu, wu_bf), (wd, wd_bf)):
            P.op("gpsimd", lambda e, _s=wsrc, _d=wdst_bf: e.dma_start(
                out=_d.ap()[:, :], in_=_s.ap()[:, :]), [(c_wc, 16)])
        s_dwT = Sem(ctx, nc, "abdwT")
        for wi, (wbf, wdst, rows, cols_t) in enumerate(
                ((wq_bf, wqt_sb, D, D), (wk_bf, wkt_sb, D, D),
                 (wv_bf, wvt_sb, 512, 512), (wqv_bf, wqvt_sb, 512, 512))):
            P.wait("sync", c_wc, (wi + 1) * 16)
            for s in range(8):
                P.op("sync", lambda e, _w=wbf, _d=wdst, _s=s, _r=rows, _ct=cols_t:
                     e.dma_start_transpose(_d[:, _s * _ct:(_s + 1) * _ct],
                                           _w.ap()[:, _s * 128:(_s + 1) * 128]), [(s_dwT, 16)])

        def emit_xload(gi):
            sl = gi % 2
            if gi >= 32:
                return
            if gi >= 2:
                P.wait("scalar", s_vt, x_marks[gi - 2][0])
                P.wait("scalar", s_ts, x_marks[gi - 2][2])
            P.op("scalar", lambda e, _t=gi, _sl=sl: e.dma_start(
                out=xt_b[_sl][:, :], in_=_ap(x_in, _t * 128 * D, [[D, 128], [1, D]])),
                [(s_dx, 16)])

        x_marks = {}
        emit_xload(0)
        emit_xload(1)
        for n in range(8):
            for grp in range(4):
                gi = n * 4 + grp
                sl = gi % 2
                t0g = 4 * n + grp
                P.wait("scalar", s_dx, 16 * (gi + 1))
                P.wait("scalar", s_vv)
                P.wait("scalar", s_vt)
                P.op("scalar", lambda e, _sl=sl: e.activation(
                    out=ln1_b[:, :], in_=xt_b[_sl][:, :],
                    func=AF.Square, accum_out=st_b[:, 2:3]), [(s_sv, 1)])
                xsq_mark = s_sv.v
                P.wait("vector", s_dx, 16 * (gi + 1))
                P.op("vector", lambda e, _sl=sl: e.reduce_sum(
                    out=st_b[:, 0:1], in_=xt_b[_sl][:, :], axis=AX.X), [(s_vv, 1)])
                P.wait("vector", s_sv)
                P.wait("vector", s_vv)
                ln_math(0)
                P.wait("vector", s_dy, 16 * max(0, gi - 1))
                P.op("vector", lambda e, _sl=sl: e.tensor_scalar(
                    out=yb_ab[_sl][:, :], in0=xt_b[_sl][:, :],
                    scalar1=st_b[:, 4:5], scalar2=st_b[:, 14:15],
                    op0=ALU.subtract, op1=ALU.mult), [(s_vv, 1)])
                yw_mark = s_vv.v
                P.wait("scalar", s_vv)
                P.wait("scalar", s_ts)
                P.op("scalar", lambda e, _sl=sl: e.activation(
                    out=xt_b[_sl][:, :], in_=yb_ab[_sl][:, :],
                    func=AF.Square, accum_out=st_b[:, 18:19]), [(s_sv, 1)])
                P.op("scalar", lambda e, _t=t0g, _sl=sl: e.dma_start(
                    out=_ap(y_sp, _t * 128 * D, [[D, 128], [1, D]]), in_=yb_ab[_sl][:, :]),
                    [(s_dy, 16)])
                P.op("vector", lambda e, _sl=sl: e.reduce_sum(
                    out=st_b[:, 16:17], in_=yb_ab[_sl][:, :], axis=AX.X), [(s_vv, 1)])
                P.wait("vector", s_sv)
                P.wait("vector", s_vv)
                ln_math(16)
                P.wait("vector", s_ts)
                P.op("vector", lambda e, _sl=sl: e.tensor_scalar(
                    out=xt_b[_sl][:, :], in0=yb_ab[_sl][:, :],
                    scalar1=st_b[:, 20:21], scalar2=st_b[:, 30:31],
                    op0=ALU.subtract, op1=ALU.mult), [(s_vv, 1)])
                P.wait("vector", s_vv)
                P.op("vector", lambda e, _sl=sl: e.tensor_tensor(
                    out=ln1_b[:, :], in0=xt_b[_sl][:, :],
                    in1=g1r[:, :], op=ALU.mult), [(s_vv, 1)])
                P.wait("vector", s_vv)
                P.op("vector", lambda e, _sl=sl: e.tensor_tensor(
                    out=xn_v[_sl][:, 0:D], in0=ln1_b[:, :],
                    in1=b1r[:, :], op=ALU.add), [(s_vt, 1)])
                P.wait("tensor", s_vt)
                _, hsm = pclB.acquire(P, "tensor")
                half = (pclB.g - 1) % 2
                for k in range(8):
                    P.op("tensor", lambda e, _k=k, _h=half, _sl=sl: _transp(e,
                        pclb[:, _h * 1024 + _k * 128: _h * 1024 + (_k + 1) * 128],
                        xn_v[_sl][:, _k * 128:(_k + 1) * 128], id_bf[:, :], _k == 0), [(s_ts, 1)])
                P.wait("vector", s_ts)
                P.op("vector", lambda e, _tc=grp, _h=half: e.tensor_copy(
                    out=xnt_b[:, :].rearrange("p (s c) -> p s c", c=512)[:, :, _tc * 128:(_tc + 1) * 128],
                    in_=pclb[:, _h * 1024:(_h + 1) * 1024].rearrange("p (s c) -> p s c", c=128)),
                    [(hsm, 1)])
                pclB.release(hsm)
                x_marks[gi] = (s_vt.v, xsq_mark, s_ts.v)
                emit_xload(gi + 2)
            pclB.wait_all(P, "tensor")
            for is_k, wt_sb in ((False, wqt_sb), (True, wkt_sb)):
                if n == 0:
                    P.wait("tensor", s_dwT, (16 if is_k else 8) * 16)
                for dt in range(8):
                    bank, bsm = mmB.acquire(P, "tensor")
                    for s in range(8):
                        P.op("tensor", lambda e, _w=wt_sb, _s=s, _dt=dt, _b=bank: e.matmul(
                            _b[:, :], _w[:, _s * D + _dt * 128: _s * D + (_dt + 1) * 128],
                            xnt_b[:, _s * 512:(_s + 1) * 512], start=(_s == 0), stop=(_s == 7)),
                            [(s_ts, 1)] if s == 7 else ())
                    P.wait("scalar", s_ts)
                    if is_k:
                        P.op("scalar", lambda e, _dt=dt, _n=n, _b=bank: e.activation(
                            out=kt_sb[:, _dt * S + _n * 512: _dt * S + _n * 512 + 512],
                            in_=_b[:, :], func=AF.Identity, bias=bks[:, _dt:_dt + 1]), [(bsm, 1)])
                    else:
                        qj = n * 8 + dt
                        qsl = qj % 2
                        P.wait("scalar", s_dq, max(0, (qj - 1) * 16))
                        P.op("scalar", lambda e, _dt=dt, _b=bank, _sl=qsl: e.activation(
                            out=qk_st[_sl][:, :], in_=_b[:, :], func=AF.Identity,
                            scale=SC_Q, bias=bqs[:, _dt:_dt + 1]), [(bsm, 1)])
                        P.op("scalar", lambda e, _dt=dt, _n=n, _sl=qsl: e.dma_start(
                            out=_ap(qt_sp, _dt * 128 * S + _n * 512, [[S, 128], [1, 512]]),
                            in_=qk_st[_sl][:, :]), [(s_dq, 16)])
                    mmB.release(bsm)
            for is_qv, wt_sb in ((False, wvt_sb), (True, wqvt_sb)):
                if n == 0:
                    P.wait("tensor", s_dwT, (32 if is_qv else 24) * 16)
                for tc in range(4):
                    t_abs = 4 * n + tc
                    bank, bsm = mmB.acquire(P, "tensor")
                    for s in range(8):
                        P.op("tensor", lambda e, _w=wt_sb, _s=s, _tc=tc, _b=bank: e.matmul(
                            _b[:, :], xnt_b[:, _s * 512 + _tc * 128: _s * 512 + (_tc + 1) * 128],
                            _w[:, _s * 512:(_s + 1) * 512], start=(_s == 0), stop=(_s == 7)),
                            [(s_ts, 1)] if s == 7 else ())
                    P.wait("vector", s_ts)
                    if not is_qv:
                        P.op("vector", lambda e, _t=t_abs, _b=bank: e.tensor_tensor(
                            out=v_sb[:, _t * 512:(_t + 1) * 512], in0=_b[:, :], in1=bvr[:, :],
                            op=ALU.add), [(bsm, 1)])
                        mmB.release(bsm)
                    else:
                        P.op("vector", lambda e, _b=bank: e.scalar_tensor_tensor(
                            out=qv_b[:, :], in0=_b[:, :], scalar=SC_QV, in1=bqvr_s[:, :],
                            op0=ALU.mult, op1=ALU.add), [(bsm, 1)])
                        bcnt = mmB.release(bsm)
                        P.wait("vector", bsm, bcnt)
                        P.op("vector", lambda e: e.tensor_tensor(
                            out=cq_b[:, :], in0=qv_b[:, :], in1=wsqr[:, :], op=ALU.mult), [(s_vv, 1)])
                        P.wait("vector", s_vv)
                        P.op("vector", lambda e: e.reduce_sum(
                            out=cqt_b[:, :], in_=cq_b[:, :].rearrange("p (h q) -> p h q", q=32),
                            axis=AX.X), [(s_vv, 1)])
                        P.wait("vector", s_vv)
                        P.op("vector", lambda e, _t=t_abs: e.tensor_scalar(
                            out=cqv_sb[:, _t * H:(_t + 1) * H], in0=cqt_b[:, :],
                            scalar1=bss[:, 0:1], scalar2=None, op0=ALU.add), [(s_vv, 1)])

        P.wait("gpsimd", s_dy)
        P.wait("gpsimd", s_dq)
        P.wait("gpsimd", c_wc)
        if kt_tap is not None:
            s_tp, = sems("abtap")
            P.op("gpsimd", lambda e: e.dma_start(out=kt_tap.ap()[:, :], in_=kt_sb[:, :]), [(s_tp, 16)])
            P.op("gpsimd", lambda e: e.dma_start(out=v_tap.ap()[:, :], in_=v_sb[:, :]), [(s_tp, 16)])
            P.op("gpsimd", lambda e: e.dma_start(out=cqv_tap.ap()[:, :], in_=cqv_sb[:, :]), [(s_tp, 16)])
            P.op("gpsimd", lambda e: e.dma_start(out=wqt_tap.ap()[:, :], in_=wqt_sb[:, :]), [(s_tp, 16)])
            P.op("gpsimd", lambda e: e.dma_start(out=xnt_tap.ap()[:, :], in_=xnt_b[:, :]), [(s_tp, 16)])
            P.wait("gpsimd", s_tp)
        P.run()
        A.release()
        if stop_after == "AB":
            return nc

        A.mark()
        qbuf = [A.alloc(8 * 256, BF16) for _ in range(2)]
        ptb = A.alloc(16 * 1024, BF16)
        stg = [A.alloc(1024, BF16) for _ in range(2)]
        prod = A.alloc(1024, F32)
        mixst = A.alloc(2 * 2048, BF16)
        csc = A.alloc(64, F32)
        macc = [A.alloc(64, F32) for _ in range(2)]
        assert A.cur <= TAIL_OFF, f"C frame {A.cur} overlaps tail {TAIL_OFF}"

        P = Prog(nc)
        (c_st, c_exp, c_pv1, c_pv2, c_oev1, c_oev2, c_q, c_mix, c_mixd,
         c_vv, c_vs, c_sv) = sems("cst", "cexp", "cpv1", "cpv2", "coev1",
                                  "coev2", "cq", "cmix", "cmixd", "cvv", "cvs", "csv")

        NB = 2048
        LAG1, LAG2 = 2, 8

        for ic in range(16):
            if ic >= 2:
                P.wait("sync", c_st, (ic - 1) * 128)
            P.op("sync", lambda e, _ic=ic: e.dma_start(
                out=qbuf[_ic % 2][:, :],
                in_=_ap(qt_sp, _ic * 256, [[S, 128], [128 * S, 8], [1, 256]])), [(c_q, 16)])
        c_pw = Sem(ctx, nc, "cpw")
        P.wait("sync", c_wc, 6 * 16)
        for s in range(8):
            P.op("sync", lambda e, _s=s: e.dma_start_transpose(
                wr1t_sb[:, _s * DR:(_s + 1) * DR], wr1_bf.ap()[:, _s * 128:(_s + 1) * 128]), [(c_pw, 16)])

        def pv_batch(kb, head, flush=False):
            if kb < 0 or kb >= NB:
                return
            ub, bb = divmod(kb, 16)
            icb, pairb = divmod(ub, 8)
            oev = c_oev1 if head == 0 else c_oev2
            pvs = c_pv1 if head == 0 else c_pv2
            obk = (pb[0], pb[1]) if head == 0 else (pb[2], pb[3])
            dnb = pb[4] if head == 0 else pb[5]
            if bb == 0:
                P.wait("tensor", oev, ub)
            slot = kb % 16
            last_inst = [None]
            for jl in range(2):
                jt = 2 * bb + jl
                for isub in range(2):
                    lbase = slot * 1024 + head * 512 + jl * 256 + isub * 128
                    P.op("tensor", lambda e, _lb=lbase, _jt=jt, _o=obk[isub],
                         _st=(bb == 0 and jl == 0), _sp=(bb == 15 and jl == 1):
                         e.matmul(_o[:, :], ptb[:, _lb:_lb + 128], v_sb[:, _jt * 512:(_jt + 1) * 512],
                                  start=_st, stop=_sp))
                    incs = [(pvs, 1)] if (jl == 1 and isub == 1) else ()
                    P.op("tensor", lambda e, _lb=lbase, _d=dnb, _is=isub,
                         _st=(bb == 0 and jl == 0 and isub == 0),
                         _sp=(bb == 15 and jl == 1 and isub == 1):
                         _mm_noldw(e, _d[:, _is:_is + 1], ptb[:, _lb:_lb + 128], ones_bf[:, 0:1],
                                   _st, _sp), incs)

        pend_cexp = {}

        def emit_pending_cexp(kk):
            for fn in pend_cexp.pop(kk, []):
                fn()

        for k in range(NB + LAG2 + 4):
            emit_pending_cexp(k)
            if k == NB:
                P.wait("tensor", c_exp, NB)
            if k >= NB + LAG2:
                continue
            if k < NB:
                u, b = divmod(k, 16)
                ic, pair = divmod(u, 8)
                if b == 0 and pair == 0:
                    P.wait("tensor", c_q, 16 * (ic + 1))
                P.wait("tensor", c_exp, k)
                for jl in range(2):
                    jt = 2 * b + jl
                    for hh in range(2):
                        ccol = jl * 256 + hh * 512
                        incs = [(c_st, 1)] if (jl == 1 and hh == 1) else ()
                        P.op("tensor", lambda e, _p=pair, _jt=jt, _ic=ic, _hh=hh, _cc=ccol, _st=(jl == 0):
                             e.matmul(pcl[:, _cc:_cc + 256],
                                      kt_sb[_hh * 64:(_hh + 1) * 64, _p * S + _jt * 128: _p * S + (_jt + 1) * 128],
                                      qbuf[_ic % 2][_hh * 64:(_hh + 1) * 64, _p * 256:(_p + 1) * 256],
                                      start=_st, stop=True,
                                      tile_position=(_hh * 64, 0)), incs)
                P.wait("scalar", c_st, k + 1)
                if k >= 16:
                    P.wait("scalar", c_pv2, k - 15)
                P.op("scalar", lambda e, _sl=(k % 16): e.activation(
                    out=ptb[:, _sl * 1024:(_sl + 1) * 1024], in_=pcl[:, 0:1024], func=AF.Exp),
                    [(c_exp, 1)])
            pv_batch(k - LAG2, 1)
            pv_batch(k - LAG1, 0)
            for head, lag in ((0, LAG1), (1, LAG2)):
                kb = k - lag
                if kb < 0 or kb >= NB or kb % 16 != 15:
                    continue
                ub = kb // 16
                icb, pairb = divmod(ub, 8)
                pvs = c_pv1 if head == 0 else c_pv2
                oev = c_oev1 if head == 0 else c_oev2
                obk = (pb[0], pb[1]) if head == 0 else (pb[2], pb[3])
                dnb = pb[4] if head == 0 else pb[5]
                hg = pairb * 2 + head
                P.wait("vector", pvs, 16 * (ub + 1))
                if icb > 0 or pairb > 0 or head > 0:
                    if pairb == 0 and head == 0:
                        P.wait("vector", c_mixd, max(0, (icb - 1) * 16))
                for isub in range(2):
                    P.op("vector", lambda e, _h=head, _is=isub, _o=obk[isub]: e.tensor_copy(
                        out=stg[_h][:, _is * 512:(_is + 1) * 512], in_=_o[:, :]),
                        [] if isub == 0 else [(c_vv, 1)])
                P.op("vector", lambda e, _d=dnb: e.reciprocal(out=csc[:, 0:2], in_=_d[:, 0:2]), [(oev, 1)])
                P.wait("vector", c_vv)
                P.wait("vector", oev)
                P.op("vector", lambda e, _h=head: e.tensor_tensor(
                    out=prod[:, :].rearrange("p (s c) -> p s c", c=512),
                    in0=stg[_h][:, :].rearrange("p (s c) -> p s c", c=512),
                    in1=wsdr[:, :].unsqueeze(1).broadcast_to([128, 2, 512]),
                    op=ALU.mult), [(c_vv, 1)])
                P.wait("vector", c_vv)
                P.op("vector", lambda e: e.reduce_sum(
                    out=csc[:, 16:32].rearrange("p (s r) -> p s r", r=8),
                    in_=prod[:, :].rearrange("p (s r d) -> p s r d", r=8, d=64),
                    axis=AX.X), [(c_vv, 1)])
                P.wait("vector", c_vv)
                P.op("vector", lambda e: e.tensor_tensor(
                    out=csc[:, 16:32].rearrange("p (s r) -> p s r", r=8),
                    in0=csc[:, 16:32].rearrange("p (s r) -> p s r", r=8),
                    in1=csc[:, 0:2].unsqueeze(2).broadcast_to([128, 2, 8]),
                    op=ALU.mult), [(c_vv, 1)])
                P.wait("vector", c_vv)
                P.op("vector", lambda e, _ic=icb, _hg=hg: e.tensor_tensor(
                    out=csc[:, 16:32].rearrange("p (s r) -> p s r", r=8),
                    in0=csc[:, 16:32].rearrange("p (s r) -> p s r", r=8),
                    in1=cqv_sb[:, :].rearrange("p (t h) -> p t h", h=H)
                        [:, 2 * _ic:2 * _ic + 2, _hg:_hg + 1].broadcast_to([128, 2, 8]),
                    op=ALU.add), [(c_vs, 1)])
                vs_val = c_vs.v

                def defer(_vs=None):
                    _vsv = vs_val

                    def go():
                        P.wait("scalar", c_vs, _vsv)
                        P.op("scalar", lambda e: e.activation(out=csc[:, 32:48], in_=csc[:, 16:32],
                                                              func=AF.Exp), [(c_sv, 1)])
                    return go
                pend_cexp.setdefault(k + 2, []).append(defer())
                P.wait("vector", c_sv, c_sv.v + 1)
                c_sv.v += 0

                P.op("vector", lambda e: e.reduce_sum(
                    out=csc[:, 2:4], in_=csc[:, 32:48].rearrange("p (s r) -> p s r", r=8),
                    axis=AX.X), [(c_vv, 1)])
                P.wait("vector", c_vv)
                P.op("vector", lambda e: e.reciprocal(out=csc[:, 4:6], in_=csc[:, 2:4]), [(c_vv, 1)])
                P.wait("vector", c_vv)
                P.op("vector", lambda e: e.tensor_tensor(
                    out=csc[:, 48:64].rearrange("p (s r) -> p s r", r=8),
                    in0=csc[:, 32:48].rearrange("p (s r) -> p s r", r=8),
                    in1=csc[:, 4:6].unsqueeze(2).broadcast_to([128, 2, 8]),
                    op=ALU.mult), [(c_vv, 1)])
                P.wait("vector", c_vv)
                P.op("vector", lambda e: e.tensor_tensor(
                    out=csc[:, 48:64].rearrange("p (s r) -> p s r", r=8),
                    in0=csc[:, 48:64].rearrange("p (s r) -> p s r", r=8),
                    in1=csc[:, 0:2].unsqueeze(2).broadcast_to([128, 2, 8]),
                    op=ALU.mult), [(c_vv, 1)])
                P.wait("vector", c_vv)
                for isub in range(2):
                    for r in range(R):
                        src = lambda _h=head, _is=isub, _r=r: stg[_h][:, _is * 512 + _r * 64: _is * 512 + _r * 64 + 64]
                        wcol = lambda _is=isub, _r=r: csc[:, 48 + _is * 8 + _r: 49 + _is * 8 + _r]
                        if r == 0:
                            P.op("vector", lambda e, _s=src, _w=wcol, _is=isub: e.tensor_scalar(
                                out=macc[_is][:, :], in0=_s(), scalar1=_w(), scalar2=None,
                                op0=ALU.mult), [(c_vv, 1)])
                        elif r < R - 1:
                            P.wait("vector", c_vv)
                            P.op("vector", lambda e, _s=src, _w=wcol, _is=isub, _r=r: e.scalar_tensor_tensor(
                                out=macc[_is][:, :] if _r % 2 == 0 else macc[_is][:, :],
                                in0=_s(), scalar=_w(), in1=macc[_is][:, :],
                                op0=ALU.mult, op1=ALU.add), [(c_vv, 1)])
                        else:
                            P.wait("vector", c_vv)
                            dst_col = (icb % 2) * 2048 + isub * 1024 + hg * 64
                            if pairb == 7 and head == 1 and isub == 1:
                                incs = [(c_mix, 1)]
                            else:
                                incs = [(c_vv, 1)]
                            P.op("vector", lambda e, _s=src, _w=wcol, _is=isub, _dc=dst_col: e.scalar_tensor_tensor(
                                out=mixst[:, _dc:_dc + 64], in0=_s(), scalar=_w(), in1=macc[_is][:, :],
                                op0=ALU.mult, op1=ALU.add), incs)
        for ic in range(16):
            P.wait("gpsimd", c_mix, ic + 1)
            P.op("gpsimd", lambda e, _ic=ic: e.dma_start(
                out=_ap(mix_sp, _ic * 256 * D, [[D, 128], [128 * D, 2], [1, D]]),
                in_=mixst[:, (_ic % 2) * 2048:(_ic % 2) * 2048 + 2048].bitcast(BF16)), [(c_mixd, 16)])
        P.wait("gpsimd", c_mixd)
        P.run()
        A.release()
        if stop_after == "C":
            return nc

        A.release()
        A.release()
        A.mark()

        wft_sb = A.alloc(8 * D, BF16)
        wr2t_sb = A.alloc(16 * D, BF16)
        g2r = A.alloc(D, F32)
        b2r = A.alloc(D, F32)
        bfr = A.alloc(D, F32)
        br2r = A.alloc(D, F32)
        mixin = [A.alloc(2 * D, BF16) for _ in range(2)]
        yb = [A.alloc(2 * D, F32) for _ in range(2)]
        x0b = [A.alloc(2 * D, F32) for _ in range(2)]
        mixT = A.alloc(8 * 256, BF16)
        onT = A.alloc(8 * 256, BF16)
        r1T = A.alloc(16 * 256, BF16)
        sq_scr = A.alloc(512, F32)
        o1b = [A.alloc(2 * D, F32) for _ in range(2)]
        ab = A.alloc(2 * D, F32)
        onb = A.alloc(2 * D, BF16)
        zb = [A.alloc(2 * D, BF16) for _ in range(2)]
        tmpd = A.alloc(2 * D, F32)
        ztst = ab.bitcast(BF16)[:, 0:2 * D]
        std = A.alloc(64, F32)
        assert A.cur <= TAIL_OFF, f"D frame {A.cur} overlaps tail {TAIL_OFF}"

        P = Prog(nc)
        (d_dm, d_dy, d_dx0, d_dx1, d_dzt, d_vv, d_sv, d_vs, d_vt, d_ts, d_dw2) = sems(
            "ddm", "ddy", "ddx0", "ddx1", "ddzt", "dvv", "dsv", "dvs", "dvt", "dts", "ddw2")
        pclB2 = Banks(ctx, nc, [0, 1], "dpcl")
        mmB2 = Banks(ctx, nc, pb[0:4], "dmm")

        d_dwf = Sem(ctx, nc, "ddwf")
        for s in range(8):
            P.op("sync", lambda e, _s=s: e.dma_start_transpose(
                wft_sb[:, _s * D:(_s + 1) * D], wf_bf.ap()[:, _s * 128:(_s + 1) * 128]), [(d_dwf, 16)])
        for dst, srcb in ((g2r, g2), (b2r, b2), (bfr, bf_in), (br2r, br2)):
            P.op("sync", lambda e, _d=dst, _s=srcb: e.dma_start(
                out=_d[:, :], in_=_s.ap().unsqueeze(0).partition_broadcast(128)), [(d_dw2, 16)])
        for s in range(16):
            P.op("sync", lambda e, _s=s: e.dma_start_transpose(
                wr2t_sb[:, _s * D:(_s + 1) * D], wr2_bf.ap()[:, _s * 128:(_s + 1) * 128]), [(d_dw2, 16)])

        def ln_math_d(base):
            b = base
            P.op("vector", lambda e: e.tensor_scalar(out=std[:, b + 4:b + 6], in0=std[:, b + 0:b + 2],
                                                     scalar1=1.0 / D, scalar2=None, op0=ALU.mult), [(d_vv, 1)])
            P.op("vector", lambda e: e.tensor_scalar(out=std[:, b + 6:b + 8], in0=std[:, b + 2:b + 4],
                                                     scalar1=1.0 / D, scalar2=None, op0=ALU.mult), [(d_vv, 1)])
            P.wait("vector", d_vv)
            P.op("vector", lambda e: e.tensor_tensor(out=std[:, b + 8:b + 10], in0=std[:, b + 4:b + 6],
                                                     in1=std[:, b + 4:b + 6], op=ALU.mult), [(d_vv, 1)])
            P.wait("vector", d_vv)
            P.op("vector", lambda e: e.tensor_tensor(out=std[:, b + 10:b + 12], in0=std[:, b + 6:b + 8],
                                                     in1=std[:, b + 8:b + 10], op=ALU.subtract), [(d_vs, 1)])
            P.wait("scalar", d_vs)
            P.op("scalar", lambda e: e.activation(out=std[:, b + 12:b + 14], in_=std[:, b + 10:b + 12],
                                                  func=AF.Sqrt, bias=eps_t[:, :]), [(d_sv, 1)])
            P.wait("vector", d_sv)
            P.op("vector", lambda e: e.reciprocal(out=std[:, b + 14:b + 16], in_=std[:, b + 12:b + 14]), [(d_vv, 1)])
            P.wait("vector", d_vv)

        def stats(src_fn, base, pe_mark=None):
            P.wait("scalar", d_vv)
            for i2 in range(2):
                P.op("scalar", lambda e, _i=i2: e.activation(
                    out=sq_scr[:, :], in_=src_fn(_i)[:, 0:512], func=AF.Square,
                    accum_out=std[:, base + 2 + _i:base + 3 + _i]), [(d_sv, 1)])
                P.op("scalar", lambda e, _i=i2: e.activation(
                    out=sq_scr[:, :], in_=src_fn(_i)[:, 512:1024], func=AF.Square,
                    accum_out=std[:, base + 8 + _i:base + 9 + _i]), [(d_sv, 1)])
                P.op("vector", lambda e, _i=i2: e.reduce_sum(
                    out=std[:, base + _i:base + 1 + _i], in_=src_fn(_i), axis=AX.X), [(d_vv, 1)])
            P.wait("vector", d_sv)
            P.op("vector", lambda e: e.tensor_tensor(
                out=std[:, base + 2:base + 4], in0=std[:, base + 2:base + 4],
                in1=std[:, base + 8:base + 10], op=ALU.add), [(d_vv, 1)])
            P.wait("vector", d_vv)
            ln_math_d(base)

        pclviews = [pclb[:, 0:1024], pclb[:, 1024:2048]]
        zviews = [pb[4][:].bitcast(BF16), pb[5][:].bitcast(BF16)]
        zB = Banks(ctx, nc, [4, 5], "dzb")

        def transpose_in(src_bf16, dstT, pool=None, views=None):
            pool = pool or pclB2
            views = views or pclviews
            for i2 in range(2):
                _, hsm = pool.acquire(P, "tensor")
                vw = views[(pool.g - 1) % 2]
                for kk in range(8):
                    P.op("tensor", lambda e, _i=i2, _k=kk, _v=vw: _transp(e,
                        _v[:, _k * 128:(_k + 1) * 128],
                        src_bf16[:, _i * D + _k * 128: _i * D + (_k + 1) * 128], id_bf[:, :], _k == 0), [(d_ts, 1)])
                P.wait("vector", d_ts)
                P.op("vector", lambda e, _i=i2, _v=vw, _d=dstT: e.tensor_copy(
                    out=_d[:, :].rearrange("p (s c) -> p s c", c=256)[:, :, _i * 128:(_i + 1) * 128],
                    in_=_v[:, :].rearrange("p (s c) -> p s c", c=128)),
                    [(hsm, 1)])
                pool.release(hsm)

        d_marks = {}

        p1_mark = {}
        p2_mark = {}

        def emit_loads(c):
            sl = c % 2
            if c >= 2:
                P.wait("sync", d_ts, p1_mark[c - 2])
                P.wait("sync", d_vv, d_marks[c - 2][0])
            P.op("sync", lambda e, _t=2 * c, _sl=sl: e.dma_start(
                out=mixin[_sl][:, :], in_=_ap(mix_sp, _t * 128 * D, [[D, 128], [128 * D, 2], [1, D]])), [(d_dm, 16)])
            P.op("sync", lambda e, _t=2 * c, _sl=sl: e.dma_start(
                out=yb[_sl][:, :], in_=_ap(y_sp, _t * 128 * D, [[D, 128], [128 * D, 2], [1, D]])), [(d_dy, 16)])
            P.op("sync", lambda e, _t=2 * c, _sl=sl: e.dma_start(
                out=x0b[_sl][:, :], in_=_ap(x_in, _t * 128 * D, [[D, 128], [128 * D, 2], [1, D]])), [(d_dx0, 16)])

        def emit_P1_pe(c):
            P.wait("tensor", d_dm, (c + 1) * 16)
            transpose_in(mixin[c % 2][:, :], mixT)
            pclB2.wait_all(P, "tensor")
            P.wait("tensor", d_dwf, 8 * 16)
            P.wait("tensor", d_dw2, 4 * 16)
            P.wait("tensor", d_dy, (c + 1) * 16)
            for tc in range(2):
                for nc2 in range(2):
                    bank, bsm = mmB2.acquire(P, "tensor")
                    for s in range(8):
                        P.op("tensor", lambda e, _s=s, _tc=tc, _n2=nc2, _b=bank: e.matmul(
                            _b[:, :], mixT[:, _s * 256 + _tc * 128: _s * 256 + (_tc + 1) * 128],
                            wft_sb[:, _s * D + _n2 * 512: _s * D + (_n2 + 1) * 512],
                            start=(_s == 0), stop=(_s == 7)), [(d_ts, 1)] if s == 7 else ())
                    emit_P1_pe.banks.append((tc, nc2, bank, bsm, d_ts.v))
            p1_mark[c] = d_ts.v

        def emit_o1_v(c):
            sl = c % 2
            if c >= 2:
                P.wait("vector", d_dx1, (c - 1) * 16)
            for (tc, nc2, bank, bsm, mmk) in emit_P1_pe.banks:
                P.wait("vector", d_ts, mmk)
                P.op("vector", lambda e, _b=bank, _n2=nc2: e.scalar_tensor_tensor(
                    out=tmpd[:, _n2 * 512:(_n2 + 1) * 512], in0=_b[:, :], scalar=1.0,
                    in1=bfr[:, _n2 * 512:(_n2 + 1) * 512], op0=ALU.mult, op1=ALU.add),
                    [(bsm, 1)])
                bcnt = mmB2.release(bsm)
                P.wait("vector", bsm, bcnt)
                P.op("vector", lambda e, _sl=sl, _tc=tc, _n2=nc2: e.tensor_tensor(
                    out=o1b[_sl][:, _tc * D + _n2 * 512: _tc * D + (_n2 + 1) * 512],
                    in0=tmpd[:, _n2 * 512:(_n2 + 1) * 512],
                    in1=yb[_sl][:, _tc * D + _n2 * 512: _tc * D + (_n2 + 1) * 512],
                    op=ALU.add), [(d_vv, 1)])
            emit_P1_pe.banks = []

        def emit_LN_v(c):
            sl = c % 2
            stats(lambda i, _sl=sl: o1b[_sl][:, i * D:(i + 1) * D], 0,
                  pe_mark=p2_mark.get(c - 1))
            if c >= 1:
                P.wait("vector", d_ts, p2_mark[c - 1])
            for i2 in range(2):
                P.op("vector", lambda e, _i=i2: e.scalar_tensor_tensor(
                    out=std[:, 48 + _i:49 + _i], in0=std[:, 4 + _i:5 + _i], scalar=-1.0,
                    in1=std[:, 14 + _i:15 + _i], op0=ALU.mult, op1=ALU.mult), [(d_vs, 1)])
            P.wait("scalar", d_vv)
            P.wait("scalar", d_vt)
            P.wait("scalar", d_vs)
            for i2 in range(2):
                P.op("scalar", lambda e, _i=i2, _sl=sl: e.activation(
                    out=tmpd[:, _i * D:(_i + 1) * D], in_=o1b[_sl][:, _i * D:(_i + 1) * D],
                    func=AF.Identity, scale=std[:, 14 + _i:15 + _i],
                    bias=std[:, 48 + _i:49 + _i]), [(d_sv, 1)])
            P.wait("vector", d_sv)
            for i2 in range(2):
                P.op("vector", lambda e, _i=i2: e.tensor_tensor(
                    out=tmpd[:, _i * D:(_i + 1) * D], in0=tmpd[:, _i * D:(_i + 1) * D],
                    in1=g2r[:, :], op=ALU.mult), [(d_vv, 1)])
                P.wait("vector", d_vv)
                P.op("vector", lambda e, _i=i2: e.tensor_tensor(
                    out=onb[:, _i * D:(_i + 1) * D], in0=tmpd[:, _i * D:(_i + 1) * D],
                    in1=b2r[:, :], op=ALU.add), [(d_vt, 1)])

        def emit_P2_pe(c):
            P.wait("tensor", d_vt)
            transpose_in(onb[:, :], onT)
            pclB2.wait_all(P, "tensor")
            P.wait("tensor", c_pw, 8 * 16)
            for dt in range(16):
                bank, bsm = mmB2.acquire(P, "tensor")
                for s in range(8):
                    P.op("tensor", lambda e, _s=s, _dt=dt, _b=bank: e.matmul(
                        _b[:, 0:256], wr1t_sb[:, _s * DR + _dt * 128: _s * DR + (_dt + 1) * 128],
                        onT[:, _s * 256:(_s + 1) * 256], start=(_s == 0), stop=(_s == 7)),
                        [(d_ts, 1)] if s == 7 else ())
                P.wait("scalar", d_ts)
                P.op("scalar", lambda e, _dt=dt, _b=bank: e.activation(
                    out=r1T[:, _dt * 256:(_dt + 1) * 256], in_=_b[:, 0:256], func=AF.Relu,
                    bias=br1s[:, _dt:_dt + 1]), [(bsm, 1)])
                mmB2.release(bsm)
            mmB2.wait_all(P, "tensor")
            P.wait("tensor", d_dw2, (4 + 16) * 16)
            for tc in range(2):
                for nc2 in range(2):
                    bank, bsm = mmB2.acquire(P, "tensor")
                    for s in range(16):
                        P.op("tensor", lambda e, _s=s, _tc=tc, _n2=nc2, _b=bank: e.matmul(
                            _b[:, :], r1T[:, _s * 256 + _tc * 128: _s * 256 + (_tc + 1) * 128],
                            wr2t_sb[:, _s * D + _n2 * 512: _s * D + (_n2 + 1) * 512],
                            start=(_s == 0), stop=(_s == 15)), [(d_ts, 1)] if s == 15 else ())
                    emit_P2_pe.banks.append((tc, nc2, bank, bsm, d_ts.v))
            p2_mark[c] = d_ts.v

        def emit_a_v(c):
            sl = c % 2
            P.wait("vector", d_dzt, max(0, c * 16))
            for (tc, nc2, bank, bsm, mmk) in emit_P2_pe.banks:
                P.wait("vector", d_ts, mmk)
                P.op("vector", lambda e, _b=bank, _n2=nc2: e.scalar_tensor_tensor(
                    out=tmpd[:, _n2 * 512:(_n2 + 1) * 512], in0=_b[:, :], scalar=1.0,
                    in1=br2r[:, _n2 * 512:(_n2 + 1) * 512], op0=ALU.mult, op1=ALU.add),
                    [(bsm, 1)])
                bcnt = mmB2.release(bsm)
                P.wait("vector", bsm, bcnt)
                P.op("vector", lambda e, _sl=sl, _tc=tc, _n2=nc2: e.tensor_tensor(
                    out=ab[:, _tc * D + _n2 * 512: _tc * D + (_n2 + 1) * 512],
                    in0=tmpd[:, _n2 * 512:(_n2 + 1) * 512],
                    in1=o1b[_sl][:, _tc * D + _n2 * 512: _tc * D + (_n2 + 1) * 512],
                    op=ALU.add), [(d_vv, 1)])
            emit_P2_pe.banks = []

        def emit_chains_v(c):
            sl = c % 2
            stats(lambda i: ab[:, i * D:(i + 1) * D], 16)
            for i2 in range(2):
                P.op("vector", lambda e, _i=i2: e.scalar_tensor_tensor(
                    out=std[:, 50 + _i:51 + _i], in0=std[:, 20 + _i:21 + _i], scalar=-1.0,
                    in1=std[:, 30 + _i:31 + _i], op0=ALU.mult, op1=ALU.mult), [(d_vs, 1)])
            P.wait("scalar", d_vv)
            P.wait("scalar", d_vt)
            P.wait("scalar", d_vs)
            for i2 in range(2):
                P.op("scalar", lambda e, _i=i2: e.activation(
                    out=tmpd[:, _i * D:(_i + 1) * D], in_=ab[:, _i * D:(_i + 1) * D],
                    func=AF.Identity, scale=std[:, 30 + _i:31 + _i],
                    bias=std[:, 50 + _i:51 + _i]), [(d_sv, 1)])
            P.wait("vector", d_dx0, (c + 1) * 16)
            P.wait("vector", d_sv)
            for i2 in range(2):
                P.op("vector", lambda e, _i=i2, _sl=sl: e.tensor_tensor(
                    out=o1b[_sl][:, _i * D:(_i + 1) * D], in0=tmpd[:, _i * D:(_i + 1) * D],
                    in1=x0b[_sl][:, _i * D:(_i + 1) * D], op=ALU.add), [(d_vv, 1)])
            P.wait("gpsimd", d_vv)
            P.op("gpsimd", lambda e, _t=2 * c, _sl=sl: e.dma_start(
                out=_ap(x1_sp, _t * 128 * D, [[D, 128], [128 * D, 2], [1, D]]), in_=o1b[_sl][:, :]), [(d_dx1, 16)])
            stats(lambda i, _sl=sl: o1b[_sl][:, i * D:(i + 1) * D], 32)
            for i2 in range(2):
                P.op("vector", lambda e, _i=i2: e.scalar_tensor_tensor(
                    out=std[:, 52 + _i:53 + _i], in0=std[:, 36 + _i:37 + _i], scalar=-1.0,
                    in1=std[:, 46 + _i:47 + _i], op0=ALU.mult, op1=ALU.mult), [(d_vs, 1)])
            P.wait("scalar", d_vv)
            P.wait("scalar", d_vs)
            P.wait("scalar", d_ts)
            for i2 in range(2):
                P.op("scalar", lambda e, _i=i2, _sl=sl: e.activation(
                    out=zb[_sl][:, _i * D:(_i + 1) * D], in_=o1b[_sl][:, _i * D:(_i + 1) * D],
                    func=AF.Identity, scale=std[:, 46 + _i:47 + _i],
                    bias=std[:, 52 + _i:53 + _i]), [(d_sv, 1)])
            d_marks[c] = (d_vv.v, d_ts.v)

        def emit_P3_pe(c):
            P.wait("tensor", d_sv)
            P.wait("tensor", d_dzt)
            transpose_in(zb[c % 2][:, :], ztst)
            pclB2.wait_all(P, "gpsimd")
            P.op("gpsimd", lambda e, _c=c: e.dma_start(
                out=_ap(zt_sp, _c * 256, [[S, 128], [128 * S, 8], [1, 256]]), in_=ztst[:, :]), [(d_dzt, 16)])

        emit_P1_pe.banks = []
        emit_P2_pe.banks = []
        emit_loads(0)
        emit_loads(1)
        if D_SEQ[0]:
            for c in range(16):
                emit_P1_pe(c)
                emit_o1_v(c)
                emit_LN_v(c)
                emit_P2_pe(c)
                emit_a_v(c)
                emit_chains_v(c)
                if 2 <= c + 2 < 16:
                    emit_loads(c + 2)
                emit_P3_pe(c)
        else:
            for c in range(18):
                if c < 16:
                    emit_P1_pe(c)
                if c >= 2:
                    emit_P3_pe(c - 2)
                if c < 16:
                    emit_o1_v(c)
                if 1 <= c <= 16:
                    emit_P2_pe(c - 1)
                if c < 16:
                    emit_LN_v(c)
                if 1 <= c <= 16:
                    emit_a_v(c - 1)
                    emit_chains_v(c - 1)
                if 2 <= c + 1 < 16:
                    emit_loads(c + 1)
        P.wait("gpsimd", d_dx1)
        P.wait("gpsimd", d_dzt)
        P.run()
        A.release()
        if stop_after == "D":
            return nc

        A.mark()
        wgt_sb = A.alloc(8 * DF, BF16)
        wut_sb = A.alloc(8 * DF, BF16)
        wdt_sb = A.alloc(24 * D, BF16)
        ztb = A.alloc(8 * 512, BF16)
        htb = A.alloc(24 * 512, BF16)
        silb = A.alloc(512, F32)
        x1b = A.alloc(D, F32)
        fb = A.alloc(D, F32)
        lnfb = A.alloc(D, F32)
        outb = A.alloc(D, F32)
        ste = A.alloc(32, F32)
        tmpe = A.alloc(D, F32)

        P = Prog(nc)
        (e_dz, e_dx1, e_do, e_vv, e_sv, e_vs, e_ts, e_dwg, e_dwu, e_dwd, e_hc) = sems(
            "edz", "edx1", "edo", "evv", "esv", "evs", "ets", "edwg", "edwu", "edwd", "ehc")
        mmB3 = Banks(ctx, nc, pb[0:4], "emm")

        P.op("sync", lambda e: e.dma_start(
            out=ztb[:, :], in_=_ap(zt_sp, 0, [[S, 128], [128 * S, 8], [1, 512]])), [(e_dz, 16)])
        for sb, wbf, wsem in ((wgt_sb, wg_bf, e_dwg), (wut_sb, wu_bf, e_dwu)):
            for s in range(8):
                P.op("sync", lambda e, _w=wbf, _sb=sb, _s=s: e.dma_start_transpose(
                    _sb[:, _s * DF:(_s + 1) * DF], _w.ap()[:, _s * 128:(_s + 1) * 128]), [(wsem, 16)])
        for cs in range(24):
            P.op("sync", lambda e, _cs=cs: e.dma_start_transpose(
                wdt_sb[:, _cs * D:(_cs + 1) * D], wd_bf.ap()[:, _cs * 128:(_cs + 1) * 128]), [(e_dwd, 16)])

        def ln_math_e():
            P.op("vector", lambda e: e.tensor_scalar(out=ste[:, 4:5], in0=ste[:, 0:1],
                                                     scalar1=1.0 / D, scalar2=None, op0=ALU.mult), [(e_vv, 1)])
            P.op("vector", lambda e: e.tensor_scalar(out=ste[:, 6:7], in0=ste[:, 2:3],
                                                     scalar1=1.0 / D, scalar2=None, op0=ALU.mult), [(e_vv, 1)])
            P.wait("vector", e_vv)
            P.op("vector", lambda e: e.tensor_tensor(out=ste[:, 8:9], in0=ste[:, 4:5],
                                                     in1=ste[:, 4:5], op=ALU.mult), [(e_vv, 1)])
            P.wait("vector", e_vv)
            P.op("vector", lambda e: e.tensor_tensor(out=ste[:, 10:11], in0=ste[:, 6:7],
                                                     in1=ste[:, 8:9], op=ALU.subtract), [(e_vs, 1)])
            P.wait("scalar", e_vs)
            P.op("scalar", lambda e: e.activation(out=ste[:, 12:13], in_=ste[:, 10:11],
                                                  func=AF.Sqrt, bias=eps_t[:, :]), [(e_sv, 1)])
            P.wait("vector", e_sv)
            P.op("vector", lambda e: e.reciprocal(out=ste[:, 14:15], in_=ste[:, 12:13]), [(e_vv, 1)])
            P.wait("vector", e_vv)

        e_marks = {}
        for n in range(8):
            if n >= 1:
                P.wait("sync", e_ts, e_marks[n - 1])
                P.op("sync", lambda e, _n=n: e.dma_start(
                    out=ztb[:, :], in_=_ap(zt_sp, _n * 512, [[S, 128], [128 * S, 8], [1, 512]])), [(e_dz, 16)])
            P.wait("tensor", e_dz, 16 * (n + 1))
            if n == 0:
                P.wait("tensor", e_dwg, 8 * 16)
            for dt in range(24):
                bank_g, bsm_g = mmB3.acquire(P, "tensor")
                for s in range(8):
                    P.op("tensor", lambda e, _s=s, _dt=dt, _b=bank_g: e.matmul(
                        _b[:, :], wgt_sb[:, _s * DF + _dt * 128: _s * DF + (_dt + 1) * 128],
                        ztb[:, _s * 512:(_s + 1) * 512], start=(_s == 0), stop=(_s == 7)),
                        [(e_ts, 1)] if s == 7 else ())
                if n == 0 and dt == 0:
                    P.wait("tensor", e_dwu, 8 * 16)
                bank_u, bsm_u = mmB3.acquire(P, "tensor")
                for s in range(8):
                    P.op("tensor", lambda e, _s=s, _dt=dt, _b=bank_u: e.matmul(
                        _b[:, :], wut_sb[:, _s * DF + _dt * 128: _s * DF + (_dt + 1) * 128],
                        ztb[:, _s * 512:(_s + 1) * 512], start=(_s == 0), stop=(_s == 7)),
                        [(e_ts, 1)] if s == 7 else ())
                P.wait("scalar", e_ts)
                P.op("scalar", lambda e, _b=bank_g: e.activation(
                    out=silb[:, :], in_=_b[:, :], func=AF.Silu, scale=SC_FFN), [(bsm_g, 1)])
                gcnt = mmB3.release(bsm_g)
                P.wait("vector", bsm_g, gcnt)
                P.wait("vector", e_hc, n)
                P.op("vector", lambda e, _b=bank_u, _dt=dt: e.scalar_tensor_tensor(
                    out=htb[:, _dt * 512:(_dt + 1) * 512], in0=_b[:, :], scalar=SC_FFN,
                    in1=silb[:, :], op0=ALU.mult, op1=ALU.mult), [(bsm_u, 1)])
                mmB3.release(bsm_u)
            e_marks[n] = e_ts.v
            mmB3.wait_all(P, "tensor")
            if n == 0:
                P.wait("tensor", e_dwd, 24 * 16)
            for tc in range(4):
                t_abs = 4 * n + tc
                P.wait("sync", e_do, max(0, t_abs * 16))
                P.op("sync", lambda e, _t=t_abs: e.dma_start(
                    out=x1b[:, :], in_=_ap(x1_sp, _t * 128 * D, [[D, 128], [1, D]])), [(e_dx1, 16)])
                for nc2 in range(2):
                    bank, bsm = mmB3.acquire(P, "tensor")
                    last_of_chunk = (tc == 3 and nc2 == 1)
                    for s in range(24):
                        if s < 23:
                            incs = []
                        else:
                            incs = [(e_hc, 1)] if last_of_chunk else [(e_ts, 1)]
                        P.op("tensor", lambda e, _s=s, _tc=tc, _n2=nc2, _b=bank: e.matmul(
                            _b[:, :], htb[:, _s * 512 + _tc * 128: _s * 512 + (_tc + 1) * 128],
                            wdt_sb[:, _s * D + _n2 * 512: _s * D + (_n2 + 1) * 512],
                            start=(_s == 0), stop=(_s == 23)), incs)
                    if last_of_chunk:
                        P.wait("vector", e_hc, n + 1)
                    else:
                        P.wait("vector", e_ts)
                    P.op("vector", lambda e, _b=bank, _n2=nc2: e.tensor_scalar(
                        out=fb[:, _n2 * 512:(_n2 + 1) * 512], in0=_b[:, :],
                        scalar1=SC_FFN, scalar2=None, op0=ALU.mult), [(bsm, 1)])
                    mmB3.release(bsm)
                mmB3.wait_all(P, "scalar")
                P.op("scalar", lambda e: e.activation(
                    out=tmpe[:, :], in_=fb[:, :], func=AF.Square, accum_out=ste[:, 2:3]), [(e_sv, 1)])
                P.op("vector", lambda e: e.reduce_sum(out=ste[:, 0:1], in_=fb[:, :], axis=AX.X), [(e_vv, 1)])
                P.wait("vector", e_sv)
                P.wait("vector", e_vv)
                ln_math_e()
                P.wait("vector", e_dx1)
                P.op("vector", lambda e: e.tensor_scalar(
                    out=tmpe[:, :], in0=fb[:, :], scalar1=ste[:, 4:5], scalar2=ste[:, 14:15],
                    op0=ALU.subtract, op1=ALU.mult), [(e_vv, 1)])
                P.wait("vector", e_vv)
                P.op("vector", lambda e: e.tensor_tensor(
                    out=outb[:, :], in0=tmpe[:, :], in1=x1b[:, :], op=ALU.add), [(e_vv, 1)])
                P.wait("gpsimd", e_vv)
                P.op("gpsimd", lambda e, _t=t_abs: e.dma_start(
                    out=_ap(out, _t * 128 * D, [[D, 128], [1, D]]), in_=outb[:, :]), [(e_do, 16)])
        if ht_tap is not None:
            s_tp2, = sems("etap")
            P.wait("gpsimd", e_vv)
            P.op("gpsimd", lambda e: e.dma_start(out=ht_tap.ap()[:, :], in_=htb[:, :]), [(s_tp2, 16)])
            P.op("gpsimd", lambda e: e.dma_start(out=zb_tap.ap()[:, :], in_=ztb[:, :]), [(s_tp2, 16)])
            P.op("gpsimd", lambda e: e.dma_start(out=wdt_tap.ap()[:, :], in_=wdt_sb[:, :]), [(s_tp2, 16)])
            P.op("gpsimd", lambda e: e.dma_start(out=fb_tap.ap()[:, :], in_=fb[:, :]), [(s_tp2, 16)])
            P.wait("gpsimd", s_tp2)
        P.wait("gpsimd", e_do)
        P.run()
        A.release()

        assert A.peak <= A.nbytes

    return nc


_NC_CACHE = {}


def _get_program(taps=()):
    key = tuple(sorted(taps))
    if key not in _NC_CACHE:
        _NC_CACHE[key] = build_program(taps)
    return _NC_CACHE[key]


def kernel(L0_attn_x, L0_attn_Wq, L0_attn_bq, L0_attn_Wk, L0_attn_bk,
           L0_attn_Wv, L0_attn_bv, L0_attn_Wqv, L0_attn_bqv, L0_attn_Ws,
           L0_attn_bs, L0_attn_Wf, L0_attn_bf, L0_attn_Wr1, L0_attn_br1,
           L0_attn_Wr2, L0_attn_br2, L0_attn_g1, L0_attn_b1, L0_attn_g2,
           L0_attn_b2, L0_ffn_w_gate, L0_ffn_w_up, L0_ffn_w_down):
    x = np.ascontiguousarray(np.asarray(L0_attn_x, dtype=np.float32))
    weights = {
        "Wq": L0_attn_Wq, "bq": L0_attn_bq, "Wk": L0_attn_Wk, "bk": L0_attn_bk,
        "Wv": L0_attn_Wv, "bv": L0_attn_bv, "Wqv": L0_attn_Wqv, "bqv": L0_attn_bqv,
        "Ws": L0_attn_Ws, "bs": L0_attn_bs, "Wf": L0_attn_Wf, "bf": L0_attn_bf,
        "Wr1": L0_attn_Wr1, "br1": L0_attn_br1, "Wr2": L0_attn_Wr2, "br2": L0_attn_br2,
        "g1": L0_attn_g1, "b1": L0_attn_b1, "g2": L0_attn_g2, "b2": L0_attn_b2,
        "wg": L0_ffn_w_gate, "wu": L0_ffn_w_up, "wd": L0_ffn_w_down,
    }
    weights = {k: np.ascontiguousarray(np.asarray(v)) for k, v in weights.items()}
    nc = _get_program()
    in_maps = [dict(weights, x=np.ascontiguousarray(x[c])) for c in range(8)]
    res = run_bass_kernel_spmd(nc, in_maps, core_ids=list(range(8)))
    return np.stack([np.asarray(r["out"], dtype=np.float32) for r in res.results], axis=0)


# version 08 (final)
# speedup vs baseline: 1.0541x; 1.0135x over previous
import contextlib

import numpy as np

import concourse.bass as bass
import concourse.mybir as mybir
from concourse.bass_utils import run_bass_kernel_spmd

F32 = mybir.dt.float32
BF16 = mybir.dt.bfloat16
I32 = mybir.dt.int32
U8 = mybir.dt.uint8
AF = mybir.ActivationFunctionType
ALU = mybir.AluOpType
AX = mybir.AxisListType

S = 4096
D = 1024
H = 16
R = 8
QKD = 32
DR = 2048
DF = 3072
EPS = 1e-5
NT = S // 128
SC_Q = 0.125
SC_QV = float(1.0 / np.float32(np.sqrt(np.float32(QKD))))
SC_FFN = float(np.float32(1.0 / 127.0))

_MMC = [0]
D_SEQ = [False]
ENGS = ("sync", "tensor", "vector", "scalar", "gpsimd")


def _transp(t, out, in_, ident, start):
    return t.matmul(out, in_, ident, start=start, stop=True, is_transpose=True)


def _mm_noldw(t, out, lhsT, rhs, start, stop, tile_position=None):
    inst = mybir.InstMatmult(
        name=f"mmnl_{_MMC[0]}", engine=mybir.EngineType.PE,
        ins=(t.lower_ap(rhs), t.lower_ap(lhsT)), outs=(t.lower_ap(out),),
        replication_resolution=0, replication_shift_amnt=0, replication_num_rows=0,
        start_tensor_calc=start, stop_tensor_calc=stop, ldweights=False,
        tile_position=tile_position)
    _MMC[0] += 1
    return t.add_instruction(inst)


class Sem:
    def __init__(self, ctx, nc, name):
        self.h = ctx.enter_context(nc.semaphore(name))
        self.v = 0


class Prog:

    def __init__(self, nc):
        self.nc = nc
        self.q = {e: [] for e in ENGS}

    def wait(self, eng, sm, val=None):
        v = sm.v if val is None else val
        if v > 0:
            self.q[eng].append(lambda e, _h=sm.h, _v=v: e.wait_ge(_h, _v))

    def op(self, eng, fn, incs=()):
        assert len(incs) <= 1, "hardware: one semaphore update per instruction"
        cap = []
        for sm, n in incs:
            sm.v += n
            cap.append((sm.h, n))

        def run(e, _fn=fn, _cap=cap):
            inst = _fn(e)
            for h, n in _cap:
                inst.then_inc(h, n)
        self.q[eng].append(run)

    def run(self):
        with self.nc.Block() as blk:
            for name in ENGS:
                lst = self.q[name]
                if lst:
                    getattr(blk, name)(lambda e, _l=lst: [f(e) for f in _l])


class Banks:

    def __init__(self, ctx, nc, banks, tag):
        self.banks = banks
        self.sems = [Sem(ctx, nc, f"{tag}{i}") for i in range(len(banks))]
        self.used = [0] * len(banks)
        self.g = 0

    def acquire(self, P, eng="tensor"):
        i = self.g % len(self.banks)
        self.g += 1
        P.wait(eng, self.sems[i], self.used[i])
        return self.banks[i], self.sems[i]

    def release(self, sm):
        i = self.sems.index(sm)
        self.used[i] += 1
        return self.used[i]

    def wait_all(self, P, eng):
        for i in range(len(self.sems)):
            P.wait(eng, self.sems[i], self.used[i])


class Arena:
    def __init__(self, ctx, nc, nbytes):
        self.t = ctx.enter_context(nc.sbuf_tensor("arena", [128, nbytes], U8))
        self.nbytes = nbytes
        self.cur = 0
        self.peak = 0
        self.marks = []

    def alloc(self, cols, dtype, parts=128, align=64):
        isz = mybir.dt.size(dtype)
        off = (self.cur + align - 1) // align * align
        self.cur = off + cols * isz
        self.peak = max(self.peak, self.cur)
        assert self.cur <= self.nbytes, f"SBUF arena overflow {self.cur}"
        return self.t.ap().bitcast(dtype)[0:parts, off // isz: off // isz + cols]

    def mark(self):
        self.marks.append(self.cur)

    def release(self):
        self.cur = self.marks.pop()


def _ap(t, offset, dims):
    return bass.AP(t, offset, dims)


def build_program(taps=(), stop_after=None):
    tapset = set(taps)
    nc = bass.Bass("TRN2", target_bir_lowering=False)

    def dram(name, shape, dtype, kind=None):
        if kind is None:
            return nc.dram_tensor(name, shape, dtype)
        return nc.dram_tensor(name, shape, dtype, kind=kind)

    x_in = dram("x", [S, D], F32, "ExternalInput")
    wq = dram("Wq", [D, D], F32, "ExternalInput")
    bq = dram("bq", [D], F32, "ExternalInput")
    wk = dram("Wk", [D, D], F32, "ExternalInput")
    bk = dram("bk", [D], F32, "ExternalInput")
    wv = dram("Wv", [512, D], F32, "ExternalInput")
    bv = dram("bv", [512], F32, "ExternalInput")
    wqv = dram("Wqv", [512, D], F32, "ExternalInput")
    bqv = dram("bqv", [512], F32, "ExternalInput")
    ws = dram("Ws", [1, 96], F32, "ExternalInput")
    bs_in = dram("bs", [1], F32, "ExternalInput")
    wf = dram("Wf", [D, D], F32, "ExternalInput")
    bf_in = dram("bf", [D], F32, "ExternalInput")
    wr1 = dram("Wr1", [DR, D], F32, "ExternalInput")
    br1 = dram("br1", [DR], F32, "ExternalInput")
    wr2 = dram("Wr2", [D, DR], F32, "ExternalInput")
    br2 = dram("br2", [D], F32, "ExternalInput")
    g1 = dram("g1", [D], F32, "ExternalInput")
    b1 = dram("b1", [D], F32, "ExternalInput")
    g2 = dram("g2", [D], F32, "ExternalInput")
    b2 = dram("b2", [D], F32, "ExternalInput")
    wg = dram("wg", [DF, D], I32, "ExternalInput")
    wu = dram("wu", [DF, D], I32, "ExternalInput")
    wd = dram("wd", [D, DF], I32, "ExternalInput")
    out = dram("out", [S, D], F32, "ExternalOutput")

    def scr(name, shape, dtype, tag):
        return dram(name, shape, dtype, "ExternalOutput" if tag in tapset else None)

    y_sp = scr("y_sp", [S, D], F32, "A")
    qt_sp = scr("qt_sp", [8, 128, S], BF16, "B")
    mix_sp = scr("mix_sp", [S, D], BF16, "C")
    x1_sp = scr("x1_sp", [S, D], F32, "D")
    zt_sp = scr("zt_sp", [8, 128, S], BF16, "D")
    wq_bf = dram("wq_bf", [D, D], BF16)
    wk_bf = dram("wk_bf", [D, D], BF16)
    wv_bf = dram("wv_bf", [512, D], BF16)
    wqv_bf = dram("wqv_bf", [512, D], BF16)
    wf_bf = dram("wf_bf", [D, D], BF16)
    wr1_bf = dram("wr1_bf", [DR, D], BF16)
    wr2_bf = dram("wr2_bf", [D, DR], BF16)
    wg_bf = dram("wg_bf", [DF, D], BF16)
    wu_bf = dram("wu_bf", [DF, D], BF16)
    wd_bf = dram("wd_bf", [D, DF], BF16)
    kt_tap = scr("kt_tap", [128, 8 * S], BF16, "B") if "B" in tapset else None
    v_tap = scr("v_tap", [128, NT * 512], BF16, "B") if "B" in tapset else None
    cqv_tap = scr("cqv_tap", [128, NT * H], F32, "B") if "B" in tapset else None
    wqt_tap = scr("wqt_tap", [128, 8 * D], BF16, "B") if "B" in tapset else None
    ht_tap = scr("ht_tap", [128, 24 * 512], BF16, "E") if "E" in tapset else None
    zb_tap = scr("zb_tap", [128, 8 * 512], BF16, "E") if "E" in tapset else None
    wdt_tap = scr("wdt_tap", [128, 24 * D], BF16, "E") if "E" in tapset else None
    fb_tap = scr("fb_tap", [128, D], F32, "E") if "E" in tapset else None
    xnt_tap = scr("xnt_tap", [128, 8 * 512], BF16, "B") if "B" in tapset else None

    ctx = contextlib.ExitStack()
    with ctx:
        A = Arena(ctx, nc, 206208)

        id_bf = A.alloc(128, BF16)
        id_f32 = A.alloc(128, F32)
        ones_bf = A.alloc(8, BF16)
        wsdr = A.alloc(512, BF16)
        wsqr = A.alloc(512, BF16)
        bss = A.alloc(1, F32)
        bqs = A.alloc(8, F32)
        bks = A.alloc(8, F32)
        br1s = A.alloc(16, F32)
        eps_t = A.alloc(1, F32)
        A.mark()
        kt_sb = A.alloc(8 * S, BF16)
        v_sb = A.alloc(NT * 512, BF16)
        cqv_sb = A.alloc(NT * H, F32)
        A.mark()

        TAIL_OFF = A.nbytes - 32 * 1024
        assert TAIL_OFF % 64 == 0

        def tail_ap(off_bytes, cols, dtype):
            isz = mybir.dt.size(dtype)
            return A.t.ap().bitcast(dtype)[0:128, off_bytes // isz: off_bytes // isz + cols]

        wr1t_sb = tail_ap(TAIL_OFF, 8 * DR, BF16)

        pcl = ctx.enter_context(nc.psum_tensor("pcl", [128, 1024], F32))
        pclb = pcl[:].bitcast(BF16)
        pb = [ctx.enter_context(nc.psum_tensor(f"pb{i}", [128, 512], F32)) for i in range(6)]

        def sems(*names):
            return [Sem(ctx, nc, n) for n in names]

        A.mark()
        iota_j = A.alloc(128, F32)
        iota_p = A.alloc(128, F32)
        btmp = A.alloc(128, F32)
        P = Prog(nc)
        d0, g0, t0, v0 = sems("p0d", "p0g", "p0t", "p0v")
        P.op("gpsimd", lambda e: e.dma_start(out=wsdr[:, :], in_=_ap(ws, 32, [[0, 128], [0, 8], [1, 64]])), [(d0, 16)])
        P.op("gpsimd", lambda e: e.dma_start(out=wsqr[:, :], in_=_ap(ws, 0, [[0, 128], [0, 16], [1, 32]])), [(d0, 16)])
        P.op("sync", lambda e: e.dma_start(out=bss[:, :], in_=_ap(bs_in, 0, [[0, 128], [1, 1]])), [(d0, 16)])
        P.op("sync", lambda e: e.dma_start(out=btmp[0:8, :], in_=_ap(bq, 0, [[128, 8], [1, 128]])), [(d0, 16)])
        P.op("sync", lambda e: e.dma_start(out=btmp[8:16, :], in_=_ap(bk, 0, [[128, 8], [1, 128]])), [(d0, 16)])
        P.op("sync", lambda e: e.dma_start(out=btmp[16:32, :], in_=_ap(br1, 0, [[128, 16], [1, 128]])), [(d0, 16)])
        P.op("gpsimd", lambda e: e.iota(iota_j[:, :], [[1, 128]], channel_multiplier=0,
                                        allow_small_or_imprecise_dtypes=True))
        P.op("gpsimd", lambda e: e.iota(iota_p[:, :], [[0, 128]], channel_multiplier=1,
                                        allow_small_or_imprecise_dtypes=True), [(g0, 1)])
        P.op("gpsimd", lambda e: e.memset(ones_bf[:, :], 1.0), [(g0, 1)])
        P.op("gpsimd", lambda e: e.memset(eps_t[:, :], EPS), [(g0, 1)])
        P.wait("vector", g0)
        P.op("vector", lambda e: e.tensor_tensor(out=id_f32[:, :], in0=iota_j[:, :], in1=iota_p[:, :], op=ALU.is_equal))
        P.op("vector", lambda e: e.tensor_tensor(out=id_bf[:, :], in0=iota_j[:, :], in1=iota_p[:, :], op=ALU.is_equal), [(v0, 1)])
        P.wait("tensor", d0)
        P.wait("tensor", v0)
        P.op("tensor", lambda e: e.transpose(pcl[:, 0:128], btmp[:, :], id_f32[:, :]), [(t0, 1)])
        P.wait("vector", t0)
        P.op("vector", lambda e: e.tensor_scalar(out=bqs[:, :], in0=pcl[:, 0:8], scalar1=SC_Q, scalar2=None, op0=ALU.mult))
        P.op("vector", lambda e: e.tensor_copy(out=bks[:, :], in_=pcl[:, 8:16]))
        P.op("vector", lambda e: e.tensor_copy(out=br1s[:, :], in_=pcl[:, 16:32]))
        P.run()
        A.release()

        A.mark()
        wqt_sb = A.alloc(8 * D, BF16)
        wkt_sb = A.alloc(8 * D, BF16)
        wvt_sb = A.alloc(8 * 512, BF16)
        wqvt_sb = A.alloc(8 * 512, BF16)
        g1r = A.alloc(D, F32)
        b1r = A.alloc(D, F32)
        bvr = A.alloc(512, F32)
        bqvr_s = A.alloc(512, F32)
        xt_b = [A.alloc(D, F32) for _ in range(2)]
        xn_v = [b.bitcast(BF16) for b in xt_b]
        ln1_b = A.alloc(D, F32)
        yb_ab = [A.alloc(D, F32) for _ in range(2)]
        st_b = A.alloc(64, F32)
        xnt_b = A.alloc(8 * 512, BF16)
        qk_st = [A.alloc(512, BF16) for _ in range(2)]
        qv_b = A.alloc(512, BF16)
        cq_b = A.alloc(512, F32)
        cqt_b = A.alloc(16, F32)

        P = Prog(nc)
        (s_dx, s_dw, s_dy, s_dq, s_dwp, s_vv, s_sv, s_vs, s_vt, s_ts, s_st,
         s_xc, s_xcs, s_wpb) = sems(
            "abdx", "abdw", "abdy", "abdq", "abdwp", "abvv", "absv", "abvs",
            "abvt", "abts", "abst", "abxc", "abxcs", "abwpb")
        pclB = Banks(ctx, nc, [0, 1], "abpcl")
        mmB = Banks(ctx, nc, pb[0:4], "abmm")

        def ln_math(base):
            b = base
            P.op("vector", lambda e: e.tensor_scalar(out=st_b[:, b + 4:b + 6], in0=st_b[:, b + 0:b + 2],
                                                     scalar1=1.0 / D, scalar2=None, op0=ALU.mult), [(s_vv, 1)])
            P.op("vector", lambda e: e.tensor_scalar(out=st_b[:, b + 6:b + 8], in0=st_b[:, b + 2:b + 4],
                                                     scalar1=1.0 / D, scalar2=None, op0=ALU.mult), [(s_vv, 1)])
            P.wait("vector", s_vv)
            P.op("vector", lambda e: e.tensor_tensor(out=st_b[:, b + 8:b + 10], in0=st_b[:, b + 4:b + 6],
                                                     in1=st_b[:, b + 4:b + 6], op=ALU.mult), [(s_vv, 1)])
            P.wait("vector", s_vv)
            P.op("vector", lambda e: e.tensor_tensor(out=st_b[:, b + 10:b + 12], in0=st_b[:, b + 6:b + 8],
                                                     in1=st_b[:, b + 8:b + 10], op=ALU.subtract), [(s_vs, 1)])
            P.wait("scalar", s_vs)
            P.op("scalar", lambda e: e.activation(out=st_b[:, b + 12:b + 14], in_=st_b[:, b + 10:b + 12],
                                                  func=AF.Sqrt, bias=eps_t[:, :]), [(s_sv, 1)])
            P.wait("vector", s_sv)
            P.op("vector", lambda e: e.reciprocal(out=st_b[:, b + 14:b + 16], in_=st_b[:, b + 12:b + 14]), [(s_vv, 1)])
            P.wait("vector", s_vv)

        for dst, src in ((g1r, g1), (b1r, b1), (bvr, bv)):
            P.op("sync", lambda e, _d=dst, _s=src: e.dma_start(
                out=_d[:, :], in_=_s.ap().unsqueeze(0).partition_broadcast(128)), [(s_dw, 16)])
        P.op("sync", lambda e: e.dma_start(
            out=ln1_b[:, 0:512], in_=bqv.ap().unsqueeze(0).partition_broadcast(128)), [(s_dw, 16)])
        P.wait("vector", s_dw)
        P.op("vector", lambda e: e.tensor_scalar(out=bqvr_s[:, :], in0=ln1_b[:, 0:512],
                                                 scalar1=SC_QV, scalar2=None, op0=ALU.mult), [(s_vv, 1)])

        c_wc = Sem(ctx, nc, "abwc")
        for wsrc, wdst_bf in ((wq, wq_bf), (wk, wk_bf), (wv, wv_bf), (wqv, wqv_bf),
                              (wf, wf_bf), (wr1, wr1_bf), (wr2, wr2_bf),
                              (wg, wg_bf), (wu, wu_bf), (wd, wd_bf)):
            P.op("gpsimd", lambda e, _s=wsrc, _d=wdst_bf: e.dma_start(
                out=_d.ap()[:, :], in_=_s.ap()[:, :]), [(c_wc, 16)])
        s_dwT = Sem(ctx, nc, "abdwT")
        for wi, (wbf, wdst, rows, cols_t) in enumerate(
                ((wq_bf, wqt_sb, D, D), (wk_bf, wkt_sb, D, D),
                 (wv_bf, wvt_sb, 512, 512), (wqv_bf, wqvt_sb, 512, 512))):
            P.wait("sync", c_wc, (wi + 1) * 16)
            for s in range(8):
                P.op("sync", lambda e, _w=wbf, _d=wdst, _s=s, _r=rows, _ct=cols_t:
                     e.dma_start_transpose(_d[:, _s * _ct:(_s + 1) * _ct],
                                           _w.ap()[:, _s * 128:(_s + 1) * 128]), [(s_dwT, 16)])

        def emit_xload(gi):
            sl = gi % 2
            if gi >= 32:
                return
            if gi >= 2:
                P.wait("scalar", s_vt, x_marks[gi - 2][0])
                P.wait("scalar", s_ts, x_marks[gi - 2][2])
            P.op("scalar", lambda e, _t=gi, _sl=sl: e.dma_start(
                out=xt_b[_sl][:, :], in_=_ap(x_in, _t * 128 * D, [[D, 128], [1, D]])),
                [(s_dx, 16)])

        x_marks = {}
        emit_xload(0)
        emit_xload(1)
        for n in range(8):
            for grp in range(4):
                gi = n * 4 + grp
                sl = gi % 2
                t0g = 4 * n + grp
                P.wait("scalar", s_dx, 16 * (gi + 1))
                P.wait("scalar", s_vv)
                P.wait("scalar", s_vt)
                P.op("scalar", lambda e, _sl=sl: e.activation(
                    out=ln1_b[:, :], in_=xt_b[_sl][:, :],
                    func=AF.Square, accum_out=st_b[:, 2:3]), [(s_sv, 1)])
                xsq_mark = s_sv.v
                P.wait("vector", s_dx, 16 * (gi + 1))
                P.op("vector", lambda e, _sl=sl: e.reduce_sum(
                    out=st_b[:, 0:1], in_=xt_b[_sl][:, :], axis=AX.X), [(s_vv, 1)])
                P.wait("vector", s_sv)
                P.wait("vector", s_vv)
                ln_math(0)
                P.wait("vector", s_dy, 16 * max(0, gi - 1))
                P.op("vector", lambda e, _sl=sl: e.tensor_scalar(
                    out=yb_ab[_sl][:, :], in0=xt_b[_sl][:, :],
                    scalar1=st_b[:, 4:5], scalar2=st_b[:, 14:15],
                    op0=ALU.subtract, op1=ALU.mult), [(s_vv, 1)])
                yw_mark = s_vv.v
                P.wait("scalar", s_vv)
                P.wait("scalar", s_ts)
                P.op("scalar", lambda e, _sl=sl: e.activation(
                    out=xt_b[_sl][:, :], in_=yb_ab[_sl][:, :],
                    func=AF.Square, accum_out=st_b[:, 18:19]), [(s_sv, 1)])
                P.op("scalar", lambda e, _t=t0g, _sl=sl: e.dma_start(
                    out=_ap(y_sp, _t * 128 * D, [[D, 128], [1, D]]), in_=yb_ab[_sl][:, :]),
                    [(s_dy, 16)])
                P.op("vector", lambda e, _sl=sl: e.reduce_sum(
                    out=st_b[:, 16:17], in_=yb_ab[_sl][:, :], axis=AX.X), [(s_vv, 1)])
                P.wait("vector", s_sv)
                P.wait("vector", s_vv)
                ln_math(16)
                P.wait("vector", s_ts)
                P.op("vector", lambda e, _sl=sl: e.tensor_scalar(
                    out=xt_b[_sl][:, :], in0=yb_ab[_sl][:, :],
                    scalar1=st_b[:, 20:21], scalar2=st_b[:, 30:31],
                    op0=ALU.subtract, op1=ALU.mult), [(s_vv, 1)])
                P.wait("vector", s_vv)
                P.op("vector", lambda e, _sl=sl: e.tensor_tensor(
                    out=ln1_b[:, :], in0=xt_b[_sl][:, :],
                    in1=g1r[:, :], op=ALU.mult), [(s_vv, 1)])
                P.wait("vector", s_vv)
                P.op("vector", lambda e, _sl=sl: e.tensor_tensor(
                    out=xn_v[_sl][:, 0:D], in0=ln1_b[:, :],
                    in1=b1r[:, :], op=ALU.add), [(s_vt, 1)])
                P.wait("tensor", s_vt)
                _, hsm = pclB.acquire(P, "tensor")
                half = (pclB.g - 1) % 2
                for k in range(8):
                    P.op("tensor", lambda e, _k=k, _h=half, _sl=sl: _transp(e,
                        pclb[:, _h * 1024 + _k * 128: _h * 1024 + (_k + 1) * 128],
                        xn_v[_sl][:, _k * 128:(_k + 1) * 128], id_bf[:, :], _k == 0), [(s_ts, 1)])
                P.wait("vector", s_ts)
                P.op("vector", lambda e, _tc=grp, _h=half: e.tensor_copy(
                    out=xnt_b[:, :].rearrange("p (s c) -> p s c", c=512)[:, :, _tc * 128:(_tc + 1) * 128],
                    in_=pclb[:, _h * 1024:(_h + 1) * 1024].rearrange("p (s c) -> p s c", c=128)),
                    [(hsm, 1)])
                pclB.release(hsm)
                x_marks[gi] = (s_vt.v, xsq_mark, s_ts.v)
                emit_xload(gi + 2)
            pclB.wait_all(P, "tensor")
            for is_k, wt_sb in ((False, wqt_sb), (True, wkt_sb)):
                if n == 0:
                    P.wait("tensor", s_dwT, (16 if is_k else 8) * 16)
                for dt in range(8):
                    bank, bsm = mmB.acquire(P, "tensor")
                    for s in range(8):
                        P.op("tensor", lambda e, _w=wt_sb, _s=s, _dt=dt, _b=bank: e.matmul(
                            _b[:, :], _w[:, _s * D + _dt * 128: _s * D + (_dt + 1) * 128],
                            xnt_b[:, _s * 512:(_s + 1) * 512], start=(_s == 0), stop=(_s == 7)),
                            [(s_ts, 1)] if s == 7 else ())
                    P.wait("scalar", s_ts)
                    if is_k:
                        P.op("scalar", lambda e, _dt=dt, _n=n, _b=bank: e.activation(
                            out=kt_sb[:, _dt * S + _n * 512: _dt * S + _n * 512 + 512],
                            in_=_b[:, :], func=AF.Identity, bias=bks[:, _dt:_dt + 1]), [(bsm, 1)])
                    else:
                        qj = n * 8 + dt
                        qsl = qj % 2
                        P.wait("scalar", s_dq, max(0, (qj - 1) * 16))
                        P.op("scalar", lambda e, _dt=dt, _b=bank, _sl=qsl: e.activation(
                            out=qk_st[_sl][:, :], in_=_b[:, :], func=AF.Identity,
                            scale=SC_Q, bias=bqs[:, _dt:_dt + 1]), [(bsm, 1)])
                        P.op("scalar", lambda e, _dt=dt, _n=n, _sl=qsl: e.dma_start(
                            out=_ap(qt_sp, _dt * 128 * S + _n * 512, [[S, 128], [1, 512]]),
                            in_=qk_st[_sl][:, :]), [(s_dq, 16)])
                    mmB.release(bsm)
            for is_qv, wt_sb in ((False, wvt_sb), (True, wqvt_sb)):
                if n == 0:
                    P.wait("tensor", s_dwT, (32 if is_qv else 24) * 16)
                for tc in range(4):
                    t_abs = 4 * n + tc
                    bank, bsm = mmB.acquire(P, "tensor")
                    for s in range(8):
                        P.op("tensor", lambda e, _w=wt_sb, _s=s, _tc=tc, _b=bank: e.matmul(
                            _b[:, :], xnt_b[:, _s * 512 + _tc * 128: _s * 512 + (_tc + 1) * 128],
                            _w[:, _s * 512:(_s + 1) * 512], start=(_s == 0), stop=(_s == 7)),
                            [(s_ts, 1)] if s == 7 else ())
                    P.wait("vector", s_ts)
                    if not is_qv:
                        P.op("vector", lambda e, _t=t_abs, _b=bank: e.tensor_tensor(
                            out=v_sb[:, _t * 512:(_t + 1) * 512], in0=_b[:, :], in1=bvr[:, :],
                            op=ALU.add), [(bsm, 1)])
                        mmB.release(bsm)
                    else:
                        P.op("vector", lambda e, _b=bank: e.scalar_tensor_tensor(
                            out=qv_b[:, :], in0=_b[:, :], scalar=SC_QV, in1=bqvr_s[:, :],
                            op0=ALU.mult, op1=ALU.add), [(bsm, 1)])
                        bcnt = mmB.release(bsm)
                        P.wait("vector", bsm, bcnt)
                        P.op("vector", lambda e: e.tensor_tensor(
                            out=cq_b[:, :], in0=qv_b[:, :], in1=wsqr[:, :], op=ALU.mult), [(s_vv, 1)])
                        P.wait("vector", s_vv)
                        P.op("vector", lambda e: e.reduce_sum(
                            out=cqt_b[:, :], in_=cq_b[:, :].rearrange("p (h q) -> p h q", q=32),
                            axis=AX.X), [(s_vv, 1)])
                        P.wait("vector", s_vv)
                        P.op("vector", lambda e, _t=t_abs: e.tensor_scalar(
                            out=cqv_sb[:, _t * H:(_t + 1) * H], in0=cqt_b[:, :],
                            scalar1=bss[:, 0:1], scalar2=None, op0=ALU.add), [(s_vv, 1)])

        P.wait("gpsimd", s_dy)
        P.wait("gpsimd", s_dq)
        P.wait("gpsimd", c_wc)
        if kt_tap is not None:
            s_tp, = sems("abtap")
            P.op("gpsimd", lambda e: e.dma_start(out=kt_tap.ap()[:, :], in_=kt_sb[:, :]), [(s_tp, 16)])
            P.op("gpsimd", lambda e: e.dma_start(out=v_tap.ap()[:, :], in_=v_sb[:, :]), [(s_tp, 16)])
            P.op("gpsimd", lambda e: e.dma_start(out=cqv_tap.ap()[:, :], in_=cqv_sb[:, :]), [(s_tp, 16)])
            P.op("gpsimd", lambda e: e.dma_start(out=wqt_tap.ap()[:, :], in_=wqt_sb[:, :]), [(s_tp, 16)])
            P.op("gpsimd", lambda e: e.dma_start(out=xnt_tap.ap()[:, :], in_=xnt_b[:, :]), [(s_tp, 16)])
            P.wait("gpsimd", s_tp)
        P.run()
        A.release()
        if stop_after == "AB":
            return nc

        A.mark()
        qbuf = [A.alloc(8 * 256, BF16) for _ in range(2)]
        ptb = A.alloc(16 * 1024, BF16)
        stg = [A.alloc(1024, BF16) for _ in range(2)]
        prod = A.alloc(1024, F32)
        mixst = A.alloc(2 * 2048, BF16)
        csc = A.alloc(64, F32)
        macc = [A.alloc(64, F32) for _ in range(2)]
        assert A.cur <= TAIL_OFF, f"C frame {A.cur} overlaps tail {TAIL_OFF}"

        P = Prog(nc)
        (c_st, c_exp, c_pv1, c_pv2, c_oev1, c_oev2, c_q, c_mix, c_mixd,
         c_vv, c_vs, c_sv) = sems("cst", "cexp", "cpv1", "cpv2", "coev1",
                                  "coev2", "cq", "cmix", "cmixd", "cvv", "cvs", "csv")

        NB = 2048
        LAG1, LAG2 = 2, 8

        for ic in range(16):
            if ic >= 2:
                P.wait("sync", c_st, (ic - 1) * 128)
            P.op("sync", lambda e, _ic=ic: e.dma_start(
                out=qbuf[_ic % 2][:, :],
                in_=_ap(qt_sp, _ic * 256, [[S, 128], [128 * S, 8], [1, 256]])), [(c_q, 16)])
        c_pw = Sem(ctx, nc, "cpw")
        P.wait("sync", c_wc, 6 * 16)
        for s in range(8):
            P.op("sync", lambda e, _s=s: e.dma_start_transpose(
                wr1t_sb[:, _s * DR:(_s + 1) * DR], wr1_bf.ap()[:, _s * 128:(_s + 1) * 128]), [(c_pw, 16)])

        def pv_batch(kb, head, flush=False):
            if kb < 0 or kb >= NB:
                return
            ub, bb = divmod(kb, 16)
            icb, pairb = divmod(ub, 8)
            oev = c_oev1 if head == 0 else c_oev2
            pvs = c_pv1 if head == 0 else c_pv2
            obk = (pb[0], pb[1]) if head == 0 else (pb[2], pb[3])
            dnb = pb[4] if head == 0 else pb[5]
            if bb == 0:
                P.wait("tensor", oev, ub)
            slot = kb % 16
            last_inst = [None]
            for jl in range(2):
                jt = 2 * bb + jl
                for isub in range(2):
                    lbase = slot * 1024 + head * 512 + jl * 256 + isub * 128
                    P.op("tensor", lambda e, _lb=lbase, _jt=jt, _o=obk[isub],
                         _st=(bb == 0 and jl == 0), _sp=(bb == 15 and jl == 1):
                         e.matmul(_o[:, :], ptb[:, _lb:_lb + 128], v_sb[:, _jt * 512:(_jt + 1) * 512],
                                  start=_st, stop=_sp))
                    incs = [(pvs, 1)] if (jl == 1 and isub == 1) else ()
                    P.op("tensor", lambda e, _lb=lbase, _d=dnb, _is=isub,
                         _st=(bb == 0 and jl == 0 and isub == 0),
                         _sp=(bb == 15 and jl == 1 and isub == 1):
                         _mm_noldw(e, _d[:, _is:_is + 1], ptb[:, _lb:_lb + 128], ones_bf[:, 0:1],
                                   _st, _sp), incs)

        pend_cexp = {}

        def emit_pending_cexp(kk):
            for fn in pend_cexp.pop(kk, []):
                fn()

        for k in range(NB + LAG2 + 4):
            emit_pending_cexp(k)
            if k == NB:
                P.wait("tensor", c_exp, NB)
            if k >= NB + LAG2:
                continue
            if k < NB:
                u, b = divmod(k, 16)
                ic, pair = divmod(u, 8)
                if b == 0 and pair == 0:
                    P.wait("tensor", c_q, 16 * (ic + 1))
                P.wait("tensor", c_exp, k)
                for jl in range(2):
                    jt = 2 * b + jl
                    for hh in range(2):
                        ccol = jl * 256 + hh * 512
                        incs = [(c_st, 1)] if (jl == 1 and hh == 1) else ()
                        P.op("tensor", lambda e, _p=pair, _jt=jt, _ic=ic, _hh=hh, _cc=ccol, _st=(jl == 0):
                             e.matmul(pcl[:, _cc:_cc + 256],
                                      kt_sb[_hh * 64:(_hh + 1) * 64, _p * S + _jt * 128: _p * S + (_jt + 1) * 128],
                                      qbuf[_ic % 2][_hh * 64:(_hh + 1) * 64, _p * 256:(_p + 1) * 256],
                                      start=_st, stop=True,
                                      tile_position=(_hh * 64, 0)), incs)
                P.wait("scalar", c_st, k + 1)
                if k >= 16:
                    P.wait("scalar", c_pv2, k - 15)
                P.op("scalar", lambda e, _sl=(k % 16): e.activation(
                    out=ptb[:, _sl * 1024:(_sl + 1) * 1024], in_=pcl[:, 0:1024], func=AF.Exp),
                    [(c_exp, 1)])
            pv_batch(k - LAG2, 1)
            pv_batch(k - LAG1, 0)
            for head, lag in ((0, LAG1), (1, LAG2)):
                kb = k - lag
                if kb < 0 or kb >= NB or kb % 16 != 15:
                    continue
                ub = kb // 16
                icb, pairb = divmod(ub, 8)
                pvs = c_pv1 if head == 0 else c_pv2
                oev = c_oev1 if head == 0 else c_oev2
                obk = (pb[0], pb[1]) if head == 0 else (pb[2], pb[3])
                dnb = pb[4] if head == 0 else pb[5]
                hg = pairb * 2 + head
                P.wait("vector", pvs, 16 * (ub + 1))
                if icb > 0 or pairb > 0 or head > 0:
                    if pairb == 0 and head == 0:
                        P.wait("vector", c_mixd, max(0, (icb - 1) * 16))
                for isub in range(2):
                    P.op("vector", lambda e, _h=head, _is=isub, _o=obk[isub]: e.tensor_copy(
                        out=stg[_h][:, _is * 512:(_is + 1) * 512], in_=_o[:, :]),
                        [] if isub == 0 else [(c_vv, 1)])
                P.op("vector", lambda e, _d=dnb: e.reciprocal(out=csc[:, 0:2], in_=_d[:, 0:2]), [(oev, 1)])
                P.wait("vector", c_vv)
                P.wait("vector", oev)
                P.op("vector", lambda e, _h=head: e.tensor_tensor(
                    out=prod[:, :].rearrange("p (s c) -> p s c", c=512),
                    in0=stg[_h][:, :].rearrange("p (s c) -> p s c", c=512),
                    in1=wsdr[:, :].unsqueeze(1).broadcast_to([128, 2, 512]),
                    op=ALU.mult), [(c_vv, 1)])
                P.wait("vector", c_vv)
                P.op("vector", lambda e: e.reduce_sum(
                    out=csc[:, 16:32].rearrange("p (s r) -> p s r", r=8),
                    in_=prod[:, :].rearrange("p (s r d) -> p s r d", r=8, d=64),
                    axis=AX.X), [(c_vv, 1)])
                P.wait("vector", c_vv)
                P.op("vector", lambda e: e.tensor_tensor(
                    out=csc[:, 16:32].rearrange("p (s r) -> p s r", r=8),
                    in0=csc[:, 16:32].rearrange("p (s r) -> p s r", r=8),
                    in1=csc[:, 0:2].unsqueeze(2).broadcast_to([128, 2, 8]),
                    op=ALU.mult), [(c_vv, 1)])
                P.wait("vector", c_vv)
                P.op("vector", lambda e, _ic=icb, _hg=hg: e.tensor_tensor(
                    out=csc[:, 16:32].rearrange("p (s r) -> p s r", r=8),
                    in0=csc[:, 16:32].rearrange("p (s r) -> p s r", r=8),
                    in1=cqv_sb[:, :].rearrange("p (t h) -> p t h", h=H)
                        [:, 2 * _ic:2 * _ic + 2, _hg:_hg + 1].broadcast_to([128, 2, 8]),
                    op=ALU.add), [(c_vs, 1)])
                vs_val = c_vs.v

                def defer(_vs=None):
                    _vsv = vs_val

                    def go():
                        P.wait("scalar", c_vs, _vsv)
                        P.op("scalar", lambda e: e.activation(out=csc[:, 32:48], in_=csc[:, 16:32],
                                                              func=AF.Exp), [(c_sv, 1)])
                    return go
                pend_cexp.setdefault(k + 2, []).append(defer())
                P.wait("vector", c_sv, c_sv.v + 1)
                c_sv.v += 0

                P.op("vector", lambda e: e.reduce_sum(
                    out=csc[:, 2:4], in_=csc[:, 32:48].rearrange("p (s r) -> p s r", r=8),
                    axis=AX.X), [(c_vv, 1)])
                P.wait("vector", c_vv)
                P.op("vector", lambda e: e.reciprocal(out=csc[:, 4:6], in_=csc[:, 2:4]), [(c_vv, 1)])
                P.wait("vector", c_vv)
                P.op("vector", lambda e: e.tensor_tensor(
                    out=csc[:, 48:64].rearrange("p (s r) -> p s r", r=8),
                    in0=csc[:, 32:48].rearrange("p (s r) -> p s r", r=8),
                    in1=csc[:, 4:6].unsqueeze(2).broadcast_to([128, 2, 8]),
                    op=ALU.mult), [(c_vv, 1)])
                P.wait("vector", c_vv)
                P.op("vector", lambda e: e.tensor_tensor(
                    out=csc[:, 48:64].rearrange("p (s r) -> p s r", r=8),
                    in0=csc[:, 48:64].rearrange("p (s r) -> p s r", r=8),
                    in1=csc[:, 0:2].unsqueeze(2).broadcast_to([128, 2, 8]),
                    op=ALU.mult), [(c_vv, 1)])
                P.wait("vector", c_vv)
                for isub in range(2):
                    for r in range(R):
                        src = lambda _h=head, _is=isub, _r=r: stg[_h][:, _is * 512 + _r * 64: _is * 512 + _r * 64 + 64]
                        wcol = lambda _is=isub, _r=r: csc[:, 48 + _is * 8 + _r: 49 + _is * 8 + _r]
                        if r == 0:
                            P.op("vector", lambda e, _s=src, _w=wcol, _is=isub: e.tensor_scalar(
                                out=macc[_is][:, :], in0=_s(), scalar1=_w(), scalar2=None,
                                op0=ALU.mult), [(c_vv, 1)])
                        elif r < R - 1:
                            P.wait("vector", c_vv)
                            P.op("vector", lambda e, _s=src, _w=wcol, _is=isub, _r=r: e.scalar_tensor_tensor(
                                out=macc[_is][:, :] if _r % 2 == 0 else macc[_is][:, :],
                                in0=_s(), scalar=_w(), in1=macc[_is][:, :],
                                op0=ALU.mult, op1=ALU.add), [(c_vv, 1)])
                        else:
                            P.wait("vector", c_vv)
                            dst_col = (icb % 2) * 2048 + isub * 1024 + hg * 64
                            if pairb == 7 and head == 1 and isub == 1:
                                incs = [(c_mix, 1)]
                            else:
                                incs = [(c_vv, 1)]
                            P.op("vector", lambda e, _s=src, _w=wcol, _is=isub, _dc=dst_col: e.scalar_tensor_tensor(
                                out=mixst[:, _dc:_dc + 64], in0=_s(), scalar=_w(), in1=macc[_is][:, :],
                                op0=ALU.mult, op1=ALU.add), incs)
        for ic in range(16):
            P.wait("gpsimd", c_mix, ic + 1)
            P.op("gpsimd", lambda e, _ic=ic: e.dma_start(
                out=_ap(mix_sp, _ic * 256 * D, [[D, 128], [128 * D, 2], [1, D]]),
                in_=mixst[:, (_ic % 2) * 2048:(_ic % 2) * 2048 + 2048].bitcast(BF16)), [(c_mixd, 16)])
        P.wait("gpsimd", c_mixd)
        P.run()
        A.release()
        if stop_after == "C":
            return nc

        A.release()
        A.release()
        A.mark()

        wft_sb = A.alloc(8 * D, BF16)
        wr2t_sb = A.alloc(16 * D, BF16)
        g2r = A.alloc(D, F32)
        b2r = A.alloc(D, F32)
        bfr = A.alloc(D, F32)
        br2r = A.alloc(D, F32)
        mixin = [A.alloc(2 * D, BF16) for _ in range(2)]
        yb = [A.alloc(2 * D, F32) for _ in range(2)]
        x0b = [A.alloc(2 * D, F32) for _ in range(2)]
        mixT = A.alloc(8 * 256, BF16)
        onT = A.alloc(8 * 256, BF16)
        r1T = A.alloc(16 * 256, BF16)
        sq_scr = A.alloc(512, F32)
        o1b = [A.alloc(2 * D, F32) for _ in range(2)]
        ab = A.alloc(2 * D, F32)
        onb = A.alloc(2 * D, BF16)
        zb = [A.alloc(2 * D, BF16) for _ in range(2)]
        tmpd = A.alloc(2 * D, F32)
        ztst = ab.bitcast(BF16)[:, 0:2 * D]
        std = A.alloc(64, F32)
        assert A.cur <= TAIL_OFF, f"D frame {A.cur} overlaps tail {TAIL_OFF}"

        P = Prog(nc)
        (d_dm, d_dy, d_dx0, d_dx1, d_dzt, d_vv, d_sv, d_vs, d_vt, d_ts, d_dw2) = sems(
            "ddm", "ddy", "ddx0", "ddx1", "ddzt", "dvv", "dsv", "dvs", "dvt", "dts", "ddw2")
        pclB2 = Banks(ctx, nc, [0, 1], "dpcl")
        mmB2 = Banks(ctx, nc, pb[0:4], "dmm")

        d_dwf = Sem(ctx, nc, "ddwf")
        for s in range(8):
            P.op("sync", lambda e, _s=s: e.dma_start_transpose(
                wft_sb[:, _s * D:(_s + 1) * D], wf_bf.ap()[:, _s * 128:(_s + 1) * 128]), [(d_dwf, 16)])
        for dst, srcb in ((g2r, g2), (b2r, b2), (bfr, bf_in), (br2r, br2)):
            P.op("sync", lambda e, _d=dst, _s=srcb: e.dma_start(
                out=_d[:, :], in_=_s.ap().unsqueeze(0).partition_broadcast(128)), [(d_dw2, 16)])
        for s in range(16):
            P.op("sync", lambda e, _s=s: e.dma_start_transpose(
                wr2t_sb[:, _s * D:(_s + 1) * D], wr2_bf.ap()[:, _s * 128:(_s + 1) * 128]), [(d_dw2, 16)])

        def ln_math_d(base):
            b = base
            P.op("vector", lambda e: e.tensor_scalar(out=std[:, b + 4:b + 6], in0=std[:, b + 0:b + 2],
                                                     scalar1=1.0 / D, scalar2=None, op0=ALU.mult), [(d_vv, 1)])
            P.op("vector", lambda e: e.tensor_scalar(out=std[:, b + 6:b + 8], in0=std[:, b + 2:b + 4],
                                                     scalar1=1.0 / D, scalar2=None, op0=ALU.mult), [(d_vv, 1)])
            P.wait("vector", d_vv)
            P.op("vector", lambda e: e.tensor_tensor(out=std[:, b + 8:b + 10], in0=std[:, b + 4:b + 6],
                                                     in1=std[:, b + 4:b + 6], op=ALU.mult), [(d_vv, 1)])
            P.wait("vector", d_vv)
            P.op("vector", lambda e: e.tensor_tensor(out=std[:, b + 10:b + 12], in0=std[:, b + 6:b + 8],
                                                     in1=std[:, b + 8:b + 10], op=ALU.subtract), [(d_vs, 1)])
            P.wait("scalar", d_vs)
            P.op("scalar", lambda e: e.activation(out=std[:, b + 12:b + 14], in_=std[:, b + 10:b + 12],
                                                  func=AF.Sqrt, bias=eps_t[:, :]), [(d_sv, 1)])
            P.wait("vector", d_sv)
            P.op("vector", lambda e: e.reciprocal(out=std[:, b + 14:b + 16], in_=std[:, b + 12:b + 14]), [(d_vv, 1)])
            P.wait("vector", d_vv)

        def stats(src_fn, base, pe_mark=None):
            P.wait("scalar", d_vv)
            for i2 in range(2):
                P.op("scalar", lambda e, _i=i2: e.activation(
                    out=sq_scr[:, :], in_=src_fn(_i)[:, 0:512], func=AF.Square,
                    accum_out=std[:, base + 2 + _i:base + 3 + _i]), [(d_sv, 1)])
                P.op("scalar", lambda e, _i=i2: e.activation(
                    out=sq_scr[:, :], in_=src_fn(_i)[:, 512:1024], func=AF.Square,
                    accum_out=std[:, base + 8 + _i:base + 9 + _i]), [(d_sv, 1)])
                P.op("vector", lambda e, _i=i2: e.reduce_sum(
                    out=std[:, base + _i:base + 1 + _i], in_=src_fn(_i), axis=AX.X), [(d_vv, 1)])
            P.wait("vector", d_sv)
            P.op("vector", lambda e: e.tensor_tensor(
                out=std[:, base + 2:base + 4], in0=std[:, base + 2:base + 4],
                in1=std[:, base + 8:base + 10], op=ALU.add), [(d_vv, 1)])
            P.wait("vector", d_vv)
            ln_math_d(base)

        pclviews = [pclb[:, 0:1024], pclb[:, 1024:2048]]
        zviews = [pb[4][:].bitcast(BF16), pb[5][:].bitcast(BF16)]
        zB = Banks(ctx, nc, [4, 5], "dzb")

        def transpose_in(src_bf16, dstT, pool=None, views=None):
            pool = pool or pclB2
            views = views or pclviews
            for i2 in range(2):
                _, hsm = pool.acquire(P, "tensor")
                vw = views[(pool.g - 1) % 2]
                for kk in range(8):
                    P.op("tensor", lambda e, _i=i2, _k=kk, _v=vw: _transp(e,
                        _v[:, _k * 128:(_k + 1) * 128],
                        src_bf16[:, _i * D + _k * 128: _i * D + (_k + 1) * 128], id_bf[:, :], _k == 0), [(d_ts, 1)])
                P.wait("vector", d_ts)
                P.op("vector", lambda e, _i=i2, _v=vw, _d=dstT: e.tensor_copy(
                    out=_d[:, :].rearrange("p (s c) -> p s c", c=256)[:, :, _i * 128:(_i + 1) * 128],
                    in_=_v[:, :].rearrange("p (s c) -> p s c", c=128)),
                    [(hsm, 1)])
                pool.release(hsm)

        d_marks = {}

        p1_mark = {}
        p2_mark = {}
        zb_mark = {}

        def emit_loads(c):
            sl = c % 2
            if c >= 2:
                P.wait("sync", d_ts, p1_mark[c - 2])
            P.op("sync", lambda e, _t=2 * c, _sl=sl: e.dma_start(
                out=mixin[_sl][:, :], in_=_ap(mix_sp, _t * 128 * D, [[D, 128], [128 * D, 2], [1, D]])), [(d_dm, 16)])
            if c >= 2:
                P.wait("sync", d_vv, d_marks[c - 2][0])
            P.op("sync", lambda e, _t=2 * c, _sl=sl: e.dma_start(
                out=yb[_sl][:, :], in_=_ap(y_sp, _t * 128 * D, [[D, 128], [128 * D, 2], [1, D]])), [(d_dy, 16)])
            P.op("sync", lambda e, _t=2 * c, _sl=sl: e.dma_start(
                out=x0b[_sl][:, :], in_=_ap(x_in, _t * 128 * D, [[D, 128], [128 * D, 2], [1, D]])), [(d_dx0, 16)])

        def emit_P1_pe(c):
            P.wait("tensor", d_dm, (c + 1) * 16)
            transpose_in(mixin[c % 2][:, :], mixT)
            pclB2.wait_all(P, "tensor")
            P.wait("tensor", d_dwf, 8 * 16)
            P.wait("tensor", d_dw2, 4 * 16)
            for tc in range(2):
                for nc2 in range(2):
                    bank, bsm = mmB2.acquire(P, "tensor")
                    for s in range(8):
                        P.op("tensor", lambda e, _s=s, _tc=tc, _n2=nc2, _b=bank: e.matmul(
                            _b[:, :], mixT[:, _s * 256 + _tc * 128: _s * 256 + (_tc + 1) * 128],
                            wft_sb[:, _s * D + _n2 * 512: _s * D + (_n2 + 1) * 512],
                            start=(_s == 0), stop=(_s == 7)), [(d_ts, 1)] if s == 7 else ())
                    emit_P1_pe.banks.append((tc, nc2, bank, bsm, d_ts.v))
            p1_mark[c] = d_ts.v

        def emit_o1_v(c):
            sl = c % 2
            P.wait("vector", d_dy, (c + 1) * 16)
            if c >= 2:
                P.wait("vector", d_dx1, (c - 1) * 16)
            for (tc, nc2, bank, bsm, mmk) in emit_P1_pe.banks:
                P.wait("vector", d_ts, mmk)
                P.op("vector", lambda e, _b=bank, _n2=nc2: e.scalar_tensor_tensor(
                    out=tmpd[:, _n2 * 512:(_n2 + 1) * 512], in0=_b[:, :], scalar=1.0,
                    in1=bfr[:, _n2 * 512:(_n2 + 1) * 512], op0=ALU.mult, op1=ALU.add),
                    [(bsm, 1)])
                bcnt = mmB2.release(bsm)
                P.wait("vector", bsm, bcnt)
                P.op("vector", lambda e, _sl=sl, _tc=tc, _n2=nc2: e.tensor_tensor(
                    out=o1b[_sl][:, _tc * D + _n2 * 512: _tc * D + (_n2 + 1) * 512],
                    in0=tmpd[:, _n2 * 512:(_n2 + 1) * 512],
                    in1=yb[_sl][:, _tc * D + _n2 * 512: _tc * D + (_n2 + 1) * 512],
                    op=ALU.add), [(d_vv, 1)])
            emit_P1_pe.banks = []

        def emit_LN_v(c):
            sl = c % 2
            stats(lambda i, _sl=sl: o1b[_sl][:, i * D:(i + 1) * D], 0,
                  pe_mark=p2_mark.get(c - 1))
            if c >= 1:
                P.wait("vector", d_ts, p2_mark[c - 1])
            for i2 in range(2):
                P.op("vector", lambda e, _i=i2: e.scalar_tensor_tensor(
                    out=std[:, 48 + _i:49 + _i], in0=std[:, 4 + _i:5 + _i], scalar=-1.0,
                    in1=std[:, 14 + _i:15 + _i], op0=ALU.mult, op1=ALU.mult), [(d_vs, 1)])
            P.wait("scalar", d_vv)
            P.wait("scalar", d_vt)
            P.wait("scalar", d_vs)
            for i2 in range(2):
                P.op("scalar", lambda e, _i=i2, _sl=sl: e.activation(
                    out=tmpd[:, _i * D:(_i + 1) * D], in_=o1b[_sl][:, _i * D:(_i + 1) * D],
                    func=AF.Identity, scale=std[:, 14 + _i:15 + _i],
                    bias=std[:, 48 + _i:49 + _i]), [(d_sv, 1)])
            P.wait("vector", d_sv)
            for i2 in range(2):
                P.op("vector", lambda e, _i=i2: e.tensor_tensor(
                    out=tmpd[:, _i * D:(_i + 1) * D], in0=tmpd[:, _i * D:(_i + 1) * D],
                    in1=g2r[:, :], op=ALU.mult), [(d_vv, 1)])
                P.wait("vector", d_vv)
                P.op("vector", lambda e, _i=i2: e.tensor_tensor(
                    out=onb[:, _i * D:(_i + 1) * D], in0=tmpd[:, _i * D:(_i + 1) * D],
                    in1=b2r[:, :], op=ALU.add), [(d_vt, 1)])

        def emit_P2_pe(c):
            P.wait("tensor", d_vt)
            transpose_in(onb[:, :], onT)
            pclB2.wait_all(P, "tensor")
            P.wait("tensor", c_pw, 8 * 16)
            for dt in range(16):
                bank, bsm = mmB2.acquire(P, "tensor")
                for s in range(8):
                    P.op("tensor", lambda e, _s=s, _dt=dt, _b=bank: e.matmul(
                        _b[:, 0:256], wr1t_sb[:, _s * DR + _dt * 128: _s * DR + (_dt + 1) * 128],
                        onT[:, _s * 256:(_s + 1) * 256], start=(_s == 0), stop=(_s == 7)),
                        [(d_ts, 1)] if s == 7 else ())
                P.wait("scalar", d_ts)
                P.op("scalar", lambda e, _dt=dt, _b=bank: e.activation(
                    out=r1T[:, _dt * 256:(_dt + 1) * 256], in_=_b[:, 0:256], func=AF.Relu,
                    bias=br1s[:, _dt:_dt + 1]), [(bsm, 1)])
                mmB2.release(bsm)
            mmB2.wait_all(P, "tensor")
            P.wait("tensor", d_dw2, (4 + 16) * 16)
            for tc in range(2):
                for nc2 in range(2):
                    bank, bsm = mmB2.acquire(P, "tensor")
                    for s in range(16):
                        P.op("tensor", lambda e, _s=s, _tc=tc, _n2=nc2, _b=bank: e.matmul(
                            _b[:, :], r1T[:, _s * 256 + _tc * 128: _s * 256 + (_tc + 1) * 128],
                            wr2t_sb[:, _s * D + _n2 * 512: _s * D + (_n2 + 1) * 512],
                            start=(_s == 0), stop=(_s == 15)), [(d_ts, 1)] if s == 15 else ())
                    emit_P2_pe.banks.append((tc, nc2, bank, bsm, d_ts.v))
            p2_mark[c] = d_ts.v

        def emit_a_v(c):
            sl = c % 2
            P.wait("vector", d_dzt, max(0, c * 16))
            for (tc, nc2, bank, bsm, mmk) in emit_P2_pe.banks:
                P.wait("vector", d_ts, mmk)
                P.op("vector", lambda e, _b=bank, _n2=nc2: e.scalar_tensor_tensor(
                    out=tmpd[:, _n2 * 512:(_n2 + 1) * 512], in0=_b[:, :], scalar=1.0,
                    in1=br2r[:, _n2 * 512:(_n2 + 1) * 512], op0=ALU.mult, op1=ALU.add),
                    [(bsm, 1)])
                bcnt = mmB2.release(bsm)
                P.wait("vector", bsm, bcnt)
                P.op("vector", lambda e, _sl=sl, _tc=tc, _n2=nc2: e.tensor_tensor(
                    out=ab[:, _tc * D + _n2 * 512: _tc * D + (_n2 + 1) * 512],
                    in0=tmpd[:, _n2 * 512:(_n2 + 1) * 512],
                    in1=o1b[_sl][:, _tc * D + _n2 * 512: _tc * D + (_n2 + 1) * 512],
                    op=ALU.add), [(d_vv, 1)])
            emit_P2_pe.banks = []

        def emit_chains_v(c):
            sl = c % 2
            stats(lambda i: ab[:, i * D:(i + 1) * D], 16)
            for i2 in range(2):
                P.op("vector", lambda e, _i=i2: e.scalar_tensor_tensor(
                    out=std[:, 50 + _i:51 + _i], in0=std[:, 20 + _i:21 + _i], scalar=-1.0,
                    in1=std[:, 30 + _i:31 + _i], op0=ALU.mult, op1=ALU.mult), [(d_vs, 1)])
            P.wait("scalar", d_vv)
            P.wait("scalar", d_vt)
            P.wait("scalar", d_vs)
            for i2 in range(2):
                P.op("scalar", lambda e, _i=i2: e.activation(
                    out=tmpd[:, _i * D:(_i + 1) * D], in_=ab[:, _i * D:(_i + 1) * D],
                    func=AF.Identity, scale=std[:, 30 + _i:31 + _i],
                    bias=std[:, 50 + _i:51 + _i]), [(d_sv, 1)])
            P.wait("vector", d_dx0, (c + 1) * 16)
            P.wait("vector", d_sv)
            for i2 in range(2):
                P.op("vector", lambda e, _i=i2, _sl=sl: e.tensor_tensor(
                    out=o1b[_sl][:, _i * D:(_i + 1) * D], in0=tmpd[:, _i * D:(_i + 1) * D],
                    in1=x0b[_sl][:, _i * D:(_i + 1) * D], op=ALU.add), [(d_vv, 1)])
            P.wait("gpsimd", d_vv)
            P.op("gpsimd", lambda e, _t=2 * c, _sl=sl: e.dma_start(
                out=_ap(x1_sp, _t * 128 * D, [[D, 128], [128 * D, 2], [1, D]]), in_=o1b[_sl][:, :]), [(d_dx1, 16)])
            stats(lambda i, _sl=sl: o1b[_sl][:, i * D:(i + 1) * D], 32)
            for i2 in range(2):
                P.op("vector", lambda e, _i=i2: e.scalar_tensor_tensor(
                    out=std[:, 52 + _i:53 + _i], in0=std[:, 36 + _i:37 + _i], scalar=-1.0,
                    in1=std[:, 46 + _i:47 + _i], op0=ALU.mult, op1=ALU.mult), [(d_vs, 1)])
            P.wait("scalar", d_vv)
            P.wait("scalar", d_vs)
            P.wait("scalar", d_ts)
            for i2 in range(2):
                P.op("scalar", lambda e, _i=i2, _sl=sl: e.activation(
                    out=zb[_sl][:, _i * D:(_i + 1) * D], in_=o1b[_sl][:, _i * D:(_i + 1) * D],
                    func=AF.Identity, scale=std[:, 46 + _i:47 + _i],
                    bias=std[:, 52 + _i:53 + _i]), [(d_sv, 1)])
            zb_mark[c] = d_sv.v
            d_marks[c] = (d_vv.v, d_ts.v)

        def emit_P3_pe(c):
            P.wait("tensor", d_sv, zb_mark[c])
            P.wait("tensor", d_dzt)
            transpose_in(zb[c % 2][:, :], ztst)
            pclB2.wait_all(P, "gpsimd")
            P.op("gpsimd", lambda e, _c=c: e.dma_start(
                out=_ap(zt_sp, _c * 256, [[S, 128], [128 * S, 8], [1, 256]]), in_=ztst[:, :]), [(d_dzt, 16)])

        emit_P1_pe.banks = []
        emit_P2_pe.banks = []
        emit_loads(0)
        emit_loads(1)
        if D_SEQ[0]:
            for c in range(16):
                emit_P1_pe(c)
                emit_o1_v(c)
                emit_LN_v(c)
                emit_P2_pe(c)
                emit_a_v(c)
                emit_chains_v(c)
                if 2 <= c + 2 < 16:
                    emit_loads(c + 2)
                emit_P3_pe(c)
        else:
            for c in range(18):
                if c < 16:
                    emit_P1_pe(c)
                    emit_o1_v(c)
                if 1 <= c <= 16:
                    emit_P2_pe(c - 1)
                if c < 16:
                    emit_LN_v(c)
                if c >= 2:
                    emit_P3_pe(c - 2)
                if 1 <= c <= 16:
                    emit_a_v(c - 1)
                    emit_chains_v(c - 1)
                if 2 <= c + 1 < 16:
                    emit_loads(c + 1)
        P.wait("gpsimd", d_dx1)
        P.wait("gpsimd", d_dzt)
        P.run()
        A.release()
        if stop_after == "D":
            return nc

        A.mark()
        wgt_sb = A.alloc(8 * DF, BF16)
        wut_sb = A.alloc(8 * DF, BF16)
        wdt_sb = A.alloc(24 * D, BF16)
        ztb = A.alloc(8 * 512, BF16)
        htb = A.alloc(24 * 512, BF16)
        silb = A.alloc(512, F32)
        x1b = A.alloc(D, F32)
        fb = A.alloc(D, F32)
        lnfb = A.alloc(D, F32)
        outb = A.alloc(D, F32)
        ste = A.alloc(32, F32)
        tmpe = A.alloc(D, F32)

        P = Prog(nc)
        (e_dz, e_dx1, e_do, e_vv, e_sv, e_vs, e_ts, e_dwg, e_dwu, e_dwd, e_hc) = sems(
            "edz", "edx1", "edo", "evv", "esv", "evs", "ets", "edwg", "edwu", "edwd", "ehc")
        mmB3 = Banks(ctx, nc, pb[0:4], "emm")

        P.op("sync", lambda e: e.dma_start(
            out=ztb[:, :], in_=_ap(zt_sp, 0, [[S, 128], [128 * S, 8], [1, 512]])), [(e_dz, 16)])
        for sb, wbf, wsem in ((wgt_sb, wg_bf, e_dwg), (wut_sb, wu_bf, e_dwu)):
            for s in range(8):
                P.op("sync", lambda e, _w=wbf, _sb=sb, _s=s: e.dma_start_transpose(
                    _sb[:, _s * DF:(_s + 1) * DF], _w.ap()[:, _s * 128:(_s + 1) * 128]), [(wsem, 16)])
        for cs in range(24):
            P.op("sync", lambda e, _cs=cs: e.dma_start_transpose(
                wdt_sb[:, _cs * D:(_cs + 1) * D], wd_bf.ap()[:, _cs * 128:(_cs + 1) * 128]), [(e_dwd, 16)])

        def ln_math_e():
            P.op("vector", lambda e: e.tensor_scalar(out=ste[:, 4:5], in0=ste[:, 0:1],
                                                     scalar1=1.0 / D, scalar2=None, op0=ALU.mult), [(e_vv, 1)])
            P.op("vector", lambda e: e.tensor_scalar(out=ste[:, 6:7], in0=ste[:, 2:3],
                                                     scalar1=1.0 / D, scalar2=None, op0=ALU.mult), [(e_vv, 1)])
            P.wait("vector", e_vv)
            P.op("vector", lambda e: e.tensor_tensor(out=ste[:, 8:9], in0=ste[:, 4:5],
                                                     in1=ste[:, 4:5], op=ALU.mult), [(e_vv, 1)])
            P.wait("vector", e_vv)
            P.op("vector", lambda e: e.tensor_tensor(out=ste[:, 10:11], in0=ste[:, 6:7],
                                                     in1=ste[:, 8:9], op=ALU.subtract), [(e_vs, 1)])
            P.wait("scalar", e_vs)
            P.op("scalar", lambda e: e.activation(out=ste[:, 12:13], in_=ste[:, 10:11],
                                                  func=AF.Sqrt, bias=eps_t[:, :]), [(e_sv, 1)])
            P.wait("vector", e_sv)
            P.op("vector", lambda e: e.reciprocal(out=ste[:, 14:15], in_=ste[:, 12:13]), [(e_vv, 1)])
            P.wait("vector", e_vv)

        e_marks = {}
        for n in range(8):
            if n >= 1:
                P.wait("sync", e_ts, e_marks[n - 1])
                P.op("sync", lambda e, _n=n: e.dma_start(
                    out=ztb[:, :], in_=_ap(zt_sp, _n * 512, [[S, 128], [128 * S, 8], [1, 512]])), [(e_dz, 16)])
            P.wait("tensor", e_dz, 16 * (n + 1))
            if n == 0:
                P.wait("tensor", e_dwg, 8 * 16)
            for dt in range(24):
                bank_g, bsm_g = mmB3.acquire(P, "tensor")
                for s in range(8):
                    P.op("tensor", lambda e, _s=s, _dt=dt, _b=bank_g: e.matmul(
                        _b[:, :], wgt_sb[:, _s * DF + _dt * 128: _s * DF + (_dt + 1) * 128],
                        ztb[:, _s * 512:(_s + 1) * 512], start=(_s == 0), stop=(_s == 7)),
                        [(e_ts, 1)] if s == 7 else ())
                if n == 0 and dt == 0:
                    P.wait("tensor", e_dwu, 8 * 16)
                bank_u, bsm_u = mmB3.acquire(P, "tensor")
                for s in range(8):
                    P.op("tensor", lambda e, _s=s, _dt=dt, _b=bank_u: e.matmul(
                        _b[:, :], wut_sb[:, _s * DF + _dt * 128: _s * DF + (_dt + 1) * 128],
                        ztb[:, _s * 512:(_s + 1) * 512], start=(_s == 0), stop=(_s == 7)),
                        [(e_ts, 1)] if s == 7 else ())
                P.wait("scalar", e_ts)
                P.op("scalar", lambda e, _b=bank_g: e.activation(
                    out=silb[:, :], in_=_b[:, :], func=AF.Silu, scale=SC_FFN), [(bsm_g, 1)])
                gcnt = mmB3.release(bsm_g)
                P.wait("vector", bsm_g, gcnt)
                P.wait("vector", e_hc, n)
                P.op("vector", lambda e, _b=bank_u, _dt=dt: e.scalar_tensor_tensor(
                    out=htb[:, _dt * 512:(_dt + 1) * 512], in0=_b[:, :], scalar=SC_FFN,
                    in1=silb[:, :], op0=ALU.mult, op1=ALU.mult), [(bsm_u, 1)])
                mmB3.release(bsm_u)
            e_marks[n] = e_ts.v
            mmB3.wait_all(P, "tensor")
            if n == 0:
                P.wait("tensor", e_dwd, 24 * 16)
            for tc in range(4):
                t_abs = 4 * n + tc
                P.wait("sync", e_do, max(0, t_abs * 16))
                P.op("sync", lambda e, _t=t_abs: e.dma_start(
                    out=x1b[:, :], in_=_ap(x1_sp, _t * 128 * D, [[D, 128], [1, D]])), [(e_dx1, 16)])
                for nc2 in range(2):
                    bank, bsm = mmB3.acquire(P, "tensor")
                    last_of_chunk = (tc == 3 and nc2 == 1)
                    for s in range(24):
                        if s < 23:
                            incs = []
                        else:
                            incs = [(e_hc, 1)] if last_of_chunk else [(e_ts, 1)]
                        P.op("tensor", lambda e, _s=s, _tc=tc, _n2=nc2, _b=bank: e.matmul(
                            _b[:, :], htb[:, _s * 512 + _tc * 128: _s * 512 + (_tc + 1) * 128],
                            wdt_sb[:, _s * D + _n2 * 512: _s * D + (_n2 + 1) * 512],
                            start=(_s == 0), stop=(_s == 23)), incs)
                    if last_of_chunk:
                        P.wait("vector", e_hc, n + 1)
                    else:
                        P.wait("vector", e_ts)
                    P.op("vector", lambda e, _b=bank, _n2=nc2: e.tensor_scalar(
                        out=fb[:, _n2 * 512:(_n2 + 1) * 512], in0=_b[:, :],
                        scalar1=SC_FFN, scalar2=None, op0=ALU.mult), [(bsm, 1)])
                    mmB3.release(bsm)
                mmB3.wait_all(P, "scalar")
                P.op("scalar", lambda e: e.activation(
                    out=tmpe[:, :], in_=fb[:, :], func=AF.Square, accum_out=ste[:, 2:3]), [(e_sv, 1)])
                P.op("vector", lambda e: e.reduce_sum(out=ste[:, 0:1], in_=fb[:, :], axis=AX.X), [(e_vv, 1)])
                P.wait("vector", e_sv)
                P.wait("vector", e_vv)
                ln_math_e()
                P.wait("vector", e_dx1)
                P.op("vector", lambda e: e.tensor_scalar(
                    out=tmpe[:, :], in0=fb[:, :], scalar1=ste[:, 4:5], scalar2=ste[:, 14:15],
                    op0=ALU.subtract, op1=ALU.mult), [(e_vv, 1)])
                P.wait("vector", e_vv)
                P.op("vector", lambda e: e.tensor_tensor(
                    out=outb[:, :], in0=tmpe[:, :], in1=x1b[:, :], op=ALU.add), [(e_vv, 1)])
                P.wait("gpsimd", e_vv)
                P.op("gpsimd", lambda e, _t=t_abs: e.dma_start(
                    out=_ap(out, _t * 128 * D, [[D, 128], [1, D]]), in_=outb[:, :]), [(e_do, 16)])
        if ht_tap is not None:
            s_tp2, = sems("etap")
            P.wait("gpsimd", e_vv)
            P.op("gpsimd", lambda e: e.dma_start(out=ht_tap.ap()[:, :], in_=htb[:, :]), [(s_tp2, 16)])
            P.op("gpsimd", lambda e: e.dma_start(out=zb_tap.ap()[:, :], in_=ztb[:, :]), [(s_tp2, 16)])
            P.op("gpsimd", lambda e: e.dma_start(out=wdt_tap.ap()[:, :], in_=wdt_sb[:, :]), [(s_tp2, 16)])
            P.op("gpsimd", lambda e: e.dma_start(out=fb_tap.ap()[:, :], in_=fb[:, :]), [(s_tp2, 16)])
            P.wait("gpsimd", s_tp2)
        P.wait("gpsimd", e_do)
        P.run()
        A.release()

        assert A.peak <= A.nbytes

    return nc


_NC_CACHE = {}


def _get_program(taps=()):
    key = tuple(sorted(taps))
    if key not in _NC_CACHE:
        _NC_CACHE[key] = build_program(taps)
    return _NC_CACHE[key]


def kernel(L0_attn_x, L0_attn_Wq, L0_attn_bq, L0_attn_Wk, L0_attn_bk,
           L0_attn_Wv, L0_attn_bv, L0_attn_Wqv, L0_attn_bqv, L0_attn_Ws,
           L0_attn_bs, L0_attn_Wf, L0_attn_bf, L0_attn_Wr1, L0_attn_br1,
           L0_attn_Wr2, L0_attn_br2, L0_attn_g1, L0_attn_b1, L0_attn_g2,
           L0_attn_b2, L0_ffn_w_gate, L0_ffn_w_up, L0_ffn_w_down):
    x = np.ascontiguousarray(np.asarray(L0_attn_x, dtype=np.float32))
    weights = {
        "Wq": L0_attn_Wq, "bq": L0_attn_bq, "Wk": L0_attn_Wk, "bk": L0_attn_bk,
        "Wv": L0_attn_Wv, "bv": L0_attn_bv, "Wqv": L0_attn_Wqv, "bqv": L0_attn_bqv,
        "Ws": L0_attn_Ws, "bs": L0_attn_bs, "Wf": L0_attn_Wf, "bf": L0_attn_bf,
        "Wr1": L0_attn_Wr1, "br1": L0_attn_br1, "Wr2": L0_attn_Wr2, "br2": L0_attn_br2,
        "g1": L0_attn_g1, "b1": L0_attn_b1, "g2": L0_attn_g2, "b2": L0_attn_b2,
        "wg": L0_ffn_w_gate, "wu": L0_ffn_w_up, "wd": L0_ffn_w_down,
    }
    weights = {k: np.ascontiguousarray(np.asarray(v)) for k, v in weights.items()}
    nc = _get_program()
    in_maps = [dict(weights, x=np.ascontiguousarray(x[c])) for c in range(8)]
    res = run_bass_kernel_spmd(nc, in_maps, core_ids=list(range(8)))
    return np.stack([np.asarray(r["out"], dtype=np.float32) for r in res.results], axis=0)
